# version 14
# baseline (speedup 1.0000x reference)
import numpy as np
import ml_dtypes
from contextlib import ExitStack
import os as _os
import concourse.bass as bass
import concourse.mybir as mybir
from concourse.bass_utils import run_bass_kernel_spmd

F32 = mybir.dt.float32
BF16 = mybir.dt.bfloat16
AF = mybir.ActivationFunctionType
ALU = mybir.AluOpType
AX = mybir.AxisListType

B, S, D, H = 2, 16384, 1024, 8
NCORE = 8
SCORE = S // NCORE
C = 128
NCH = SCORE // C
NCC = 2 * NCH
I_FF = 2816
EPS = 1e-8
C2V = 1.0 + EPS
C1V = EPS * (1.0 + EPS)
SCL_B = 0.125
T2 = [0.0625, 0.5625, 1.5625, 3.0625, 6.25, 12.25, 25.0]
NEG = -1.0e30

DMA_SEM = {"sync": "sd", "gpsimd": "gd"}


class Item:
    __slots__ = ("eng", "fn", "deps", "is_dma", "idx", "sem_val", "needed",
                 "force", "dma_cum")

    def __init__(self, eng, fn, deps, is_dma, idx, force=False):
        self.eng, self.fn, self.is_dma, self.idx = eng, fn, is_dma, idx
        self.deps = [d for d in deps if d is not None]
        self.sem_val = None
        self.needed = force
        self.force = force


class Sched:
    def __init__(self):
        self.items = []
        self.final = {}

    def add(self, eng, fn, deps=(), is_dma=False, force=False):
        it = Item(eng, fn, deps, is_dma, len(self.items), force)
        self.items.append(it)
        return it

    def resolve(self):
        for it in self.items:
            for d in it.deps:
                assert d.idx < it.idx
                if d.eng != it.eng or d.is_dma:
                    d.needed = True
        cnt = {}
        dcum = {"sd": 0, "gd": 0}
        for it in self.items:
            it.dma_cum = dict(dcum)
            if it.is_dma:
                sem = DMA_SEM[it.eng]
                cnt[sem] = cnt.get(sem, 0) + 16
                it.sem_val = cnt[sem]
                dcum[sem] = cnt[sem]
            elif it.needed:
                cnt[it.eng] = cnt.get(it.eng, 0) + 1
                it.sem_val = cnt[it.eng]
        self.final = dict(cnt)

    def emit(self, eng_name, eng, sems):
        if not hasattr(self, "hw"):
            self.hw = {}
        for it in self.items:
            if it.eng != eng_name:
                continue
            waited = {}
            for d in it.deps:
                if d.eng == eng_name and not d.is_dma and not d.force:
                    continue
                if d.is_dma:
                    sem = DMA_SEM[d.eng]
                    val = it.dma_cum[sem]
                else:
                    sem = d.eng
                    val = d.sem_val
                if waited.get(sem, -1) < val:
                    waited[sem] = val
            for sem, val in sorted(waited.items()):
                if self.hw.get((eng_name, sem), -1) >= val:
                    continue
                self.hw[(eng_name, sem)] = val
                eng.wait_ge(sems[sem], val)
            ins = it.fn(eng)
            if it.is_dma:
                ins.then_inc(sems[DMA_SEM[it.eng]], 16)
            elif it.needed:
                ins.then_inc(sems[it.eng], 1)


def build(cutoff=None):
    PHASE = int(_os.environ.get("MK_PHASE", "9"))
    nc = bass.Bass("TRN2", target_bir_lowering=False, debug=False,
                   detect_race_conditions=False)
    dt = nc.dram_tensor
    x_sl = dt("x_sl", [2 * SCORE, D], F32, kind="ExternalInput").ap()
    Wq = dt("Wq", [512, D], F32, kind="ExternalInput").ap()
    Wk = dt("Wk", [512, D], F32, kind="ExternalInput").ap()
    Wa = dt("Wa", [512, D], F32, kind="ExternalInput").ap()
    Wab = dt("Wab", [1, 512], F32, kind="ExternalInput").ap()
    Wv = dt("Wv", [D, D], F32, kind="ExternalInput").ap()
    Wo = dt("Wo", [D, D], F32, kind="ExternalInput").ap()
    Wg = dt("Wg", [I_FF, D], F32, kind="ExternalInput").ap()
    Wu = dt("Wu", [I_FF, D], F32, kind="ExternalInput").ap()
    Wd = dt("Wd", [D, I_FF], F32, kind="ExternalInput").ap()
    idenf = dt("idenf", [128, 128], F32, kind="ExternalInput").ap()
    idenb = dt("idenb", [128, 128], BF16, kind="ExternalInput").ap()
    maskut = dt("maskut", [128, 128], F32, kind="ExternalInput").ap()
    rhsec = dt("rhsec", [9, 1024], F32, kind="ExternalInput").ap()
    onesc = dt("onesc", [1, 128], F32, kind="ExternalInput").ap()
    adjc = dt("adjc", [128, 2], F32, kind="ExternalInput").ap()
    y0fc = dt("y0fc", [128, 1], F32, kind="ExternalInput").ap()
    out_sl = dt("out_sl", [2 * SCORE, D], F32, kind="ExternalOutput").ap()
    _dbg = "ExternalOutput" if _os.environ.get("MK_DBG") else "Internal"
    s_x2 = dt("s_x2", [2 * SCORE, D], F32, kind=_dbg).ap()
    s_vv = dt("s_vv", [NCC, 128, D], BF16, kind=_dbg).ap()
    s_xqT = dt("s_xqT", [NCC, 128, D], BF16, kind=_dbg).ap()
    s_wgu = dt("s_wgu", [8, 128, 2 * I_FF], BF16, kind=_dbg).ap()
    s_wd = dt("s_wd", [22, 128, D], BF16, kind=_dbg).ap()
    if _os.environ.get("MK_DBG"):
        d_xnt = dt("d_xnt", [128, D], BF16, kind="ExternalOutput").ap()
        d_kev = dt("d_kev", [128, 512], F32, kind="ExternalOutput").ap()
        d_bat = dt("d_bat", [128, 16], F32, kind="ExternalOutput").ap()
        d_smu = dt("d_smu", [9, 128], F32, kind="ExternalOutput").ap()
        d_trn = dt("d_trn", [8, 768], F32, kind="ExternalOutput").ap()
        d_st = dt("d_st", [128, 8], F32, kind="ExternalOutput").ap()
        d_smub = dt("d_smub", [9, 2 * SCORE], F32, kind="ExternalOutput").ap()
        d_smbb = dt("d_smbb", [8, 2 * SCORE], F32, kind="ExternalOutput").ap()
        d_vvl = dt("d_vvl", [128, D], BF16, kind="ExternalOutput").ap()
        d_xt = dt("d_xt", [128, D], F32, kind="ExternalOutput").ap()
    cc_in = dt("cc_in", [16, 1032], F32).ap()
    cc_out = dt("cc_out", [128, 1032], F32, addr_space="Shared").ap()

    S_ = Sched()
    es = ExitStack()

    def sb(n, s, d=F32):
        return es.enter_context(nc.sbuf_tensor(n, s, d)).ap()

    with es:
        idf = sb("idf", [128, 128], F32)
        idb = sb("idb", [128, 128], BF16)
        mskt = sb("mskt", [128, 128], F32)
        onet = sb("onet", [1, 128], F32)
        wabt = sb("wabt", [1, 512], F32)
        adjt = sb("adjt", [128, 2], F32)
        y0t = sb("y0t", [128, 1], F32)
        cst = sb("cst", [128, 2], F32)
        xtP = [sb("xt0", [128, D], F32), sb("xt1", [128, D], F32)]
        x2t = sb("x2t", [128, D], F32)
        xt = xtP[0]
        q_u = sb("q_u", [128, 1024], F32)
        q_sq = sb("q_sq", [128, 2048], F32)
        q_z = sb("q_z", [128, 1024], F32)
        q_zc = sb("q_zc", [128, 1024], F32)
        q_sg = sb("q_sg", [128, 1536], F32)
        q_bm = sb("q_bm", [128, 88], F32)
        q_sc = sb("q_sc", [128, 88], F32)
        q_rs = sb("q_rs", [128, 88], F32)
        psA = es.enter_context(nc.psum_tensor("psA", [128, 1536], F32)).ap()
        psB = es.enter_context(nc.psum_tensor("psB", [128, 1536], F32)).ap()
        psT = es.enter_context(nc.psum_tensor("psT", [128, 2048], BF16)).ap()
        pstS = {"par": 0, "lastev": [None, None]}

        def twave(pieces, deps):
            b_ = pstS["par"]
            base = b_ * 1024
            lev = pstS["lastev"][b_]
            dms = []
            for i, (sap, _) in enumerate(pieces):
                dms.append(mm(psT[:, base + i * 128:base + (i + 1) * 128],
                              sap, idb[:, :], True, True,
                              deps + [dc_ib] + ([lev] if lev else []),
                              tr=True))
            evs = []
            for i, (_, efn) in enumerate(pieces):
                evs.append(so(lambda s, i_=i, e_=efn, bb=base: e_(
                    s, psT[:, bb + i_ * 128:bb + (i_ + 1) * 128]),
                    [dms[-1], dms[i]]))
            pstS["lastev"][b_] = evs[-1]
            pstS["par"] ^= 1
            return evs

        def mm(out, lhsT, rhs, start, stop, deps, tr=False):
            return S_.add("tensor", lambda t, o=out, l=lhsT, r=rhs, a=start,
                          b_=stop, c_=tr: t.matmul(o, l, r, start=a, stop=b_,
                                                   is_transpose=c_), deps)

        def vo(fn, deps):
            return S_.add("vector", fn, deps)

        def vof(fn, deps):
            return S_.add("vector", fn, deps, force=True)

        def so(fn, deps):
            return S_.add("scalar", fn, deps)

        def go(fn, deps=()):
            return S_.add("gpsimd", fn, deps)

        def dmas(fn, deps=()):
            return S_.add("sync", fn, deps, is_dma=True)

        def dmag(fn, deps=()):
            return S_.add("gpsimd", fn, deps, is_dma=True)

        def emit_quant(W, src, out, deps, srcoff=0, outoff=0,
                       pool=False):
            if pool:
                return emit_quant_pool(W, src, out, deps)
            nb = W // 16
            s_ap = src[:, srcoff:srcoff + W]
            s3 = s_ap.rearrange("p (b k) -> p b k", k=16)
            d0 = vof(lambda v: v.tensor_reduce(q_bm[:, 0:nb], s3, AX.X, ALU.max,
                                              apply_absolute_value=True), deps)
            d1 = vof(lambda v: v.tensor_scalar(q_sc[:, 0:nb], q_bm[:, 0:nb],
                                              1.0 / 6.0, 1e-12, ALU.mult,
                                              ALU.max), [d0])
            d2 = vof(lambda v: v.reciprocal(q_rs[:, 0:nb], q_sc[:, 0:nb]), [d1])
            rs3 = q_rs[:, 0:nb].rearrange("p (b o) -> p b o", o=1).broadcast_to(
                [128, nb, 16])
            u3 = q_u[:, 0:W].rearrange("p (b k) -> p b k", k=16)
            d3 = vo(lambda v: v.tensor_tensor(u3, s3, rs3, ALU.mult),
                    deps + [d2])
            d4 = so(lambda s: s.activation(out=q_sq[:, 0:W], in_=q_u[:, 0:W],
                                           func=AF.Square), [d3])
            d5 = so(lambda s: s.activation(out=q_sg[:, 0:W], in_=q_u[:, 0:W],
                                           func=AF.Sign), [d3])
            e = vo(lambda v: v.tensor_scalar(q_z[:, 0:W], q_sq[:, 0:W], T2[0],
                                             None, ALU.is_gt), [d4])
            for th in T2[1:4]:
                e = vo(lambda v, t_=th: v.scalar_tensor_tensor(
                    q_z[:, 0:W], q_sq[:, 0:W], t_, q_z[:, 0:W], ALU.is_gt,
                    ALU.add), [e])
            f = vo(lambda v: v.tensor_scalar(q_zc[:, 0:W], q_sq[:, 0:W], T2[4],
                                             None, ALU.is_gt), [d4])
            for th in (T2[5], T2[6], T2[6]):
                f = vo(lambda v, t_=th: v.scalar_tensor_tensor(
                    q_zc[:, 0:W], q_sq[:, 0:W], t_, q_zc[:, 0:W], ALU.is_gt,
                    ALU.add), [f])
            g = vo(lambda v: v.scalar_tensor_tensor(
                q_sq[:, 0:W], q_z[:, 0:W], 0.5, q_zc[:, 0:W], ALU.mult,
                ALU.add), [e, f])
            g2 = vo(lambda v: v.tensor_tensor(q_u[:, 0:W], q_sq[:, 0:W],
                                              q_sg[:, 0:W], ALU.mult),
                    [g, d5])
            sc3 = q_sc[:, 0:nb].rearrange("p (b o) -> p b o", o=1).broadcast_to(
                [128, nb, 16])
            o3 = out[:, outoff:outoff + W].rearrange("p (b k) -> p b k", k=16)
            return vo(lambda v: v.tensor_tensor(o3, u3, sc3, ALU.mult), [g2])

        def emit_quant_pool(W, src, out, deps):
            nb = W // 16
            s_ap = src[:, 0:W]
            s3 = s_ap.rearrange("p (b k) -> p b k", k=16)
            d0 = vof(lambda v: v.tensor_reduce(q_bm[:, 0:nb], s3, AX.X, ALU.max,
                                              apply_absolute_value=True), deps)
            d1 = vof(lambda v: v.tensor_scalar(q_sc[:, 0:nb], q_bm[:, 0:nb],
                                              1.0 / 6.0, 1e-12, ALU.mult,
                                              ALU.max), [d0])
            d2 = vof(lambda v: v.reciprocal(q_rs[:, 0:nb], q_sc[:, 0:nb]), [d1])
            rs3 = q_rs[:, 0:nb].rearrange("p (b o) -> p b o", o=1).broadcast_to(
                [128, nb, 16])
            u3 = q_u[:, 0:W].rearrange("p (b k) -> p b k", k=16)
            d3 = go(lambda g: g.tensor_tensor(u3, s3, rs3, ALU.mult),
                    deps + [d2])
            d4 = so(lambda s: s.activation(out=q_sq[:, 0:W], in_=q_u[:, 0:W],
                                           func=AF.Square), [d3])
            d5 = so(lambda s: s.activation(out=q_sg[:, 0:W], in_=q_u[:, 0:W],
                                           func=AF.Sign), [d3])
            tmp = src
            e = go(lambda g: g.tensor_scalar(q_z[:, 0:W], q_sq[:, 0:W], T2[0],
                                             None, ALU.is_gt), [d4, d3])
            for th in T2[1:4]:
                t1 = go(lambda g, t_=th: g.tensor_scalar(
                    tmp[:, 0:W], q_sq[:, 0:W], t_, None, ALU.is_gt),
                    [d4, d3, e])
                e = go(lambda g: g.tensor_tensor(q_z[:, 0:W], q_z[:, 0:W],
                                                 tmp[:, 0:W], ALU.add), [t1, e])
            f = go(lambda g: g.tensor_scalar(q_zc[:, 0:W], q_sq[:, 0:W], T2[4],
                                             None, ALU.is_gt), [d4])
            for th in (T2[5], T2[6], T2[6]):
                t1 = go(lambda g, t_=th: g.tensor_scalar(
                    tmp[:, 0:W], q_sq[:, 0:W], t_, None, ALU.is_gt), [f])
                f = go(lambda g: g.tensor_tensor(q_zc[:, 0:W], q_zc[:, 0:W],
                                                 tmp[:, 0:W], ALU.add),
                       [t1, f])
            g_ = go(lambda g: g.tensor_scalar(q_z[:, 0:W], q_z[:, 0:W], 0.5,
                                              None, ALU.mult), [e, f])
            g1 = go(lambda g: g.tensor_tensor(q_sq[:, 0:W], q_z[:, 0:W],
                                              q_zc[:, 0:W], ALU.add), [g_])
            g2 = go(lambda g: g.tensor_tensor(q_u[:, 0:W], q_sq[:, 0:W],
                                              q_sg[:, 0:W], ALU.mult),
                    [g1, d5])
            sc3 = q_sc[:, 0:nb].rearrange("p (b o) -> p b o", o=1).broadcast_to(
                [128, nb, 16])
            o3 = out[:, 0:W].rearrange("p (b k) -> p b k", k=16)
            return go(lambda g: g.tensor_tensor(o3, u3, sc3, ALU.mult), [g2])

        dc_id = dmas(lambda e: e.dma_start(out=idf[:, :], in_=idenf[:, :]))
        dc_ib = dmas(lambda e: e.dma_start(out=idb[:, :], in_=idenb[:, :]))
        dc_mk = dmas(lambda e: e.dma_start(out=mskt[:, :], in_=maskut[:, :]))
        dc_on = dmas(lambda e: e.dma_start(out=onet[:, :], in_=onesc[:, :]))
        dc_wb = dmas(lambda e: e.dma_start(out=wabt[:, :], in_=Wab[:, :]))
        dc_aj = dmas(lambda e: e.dma_start(out=adjt[:, :], in_=adjc[:, :]))
        dc_y0 = dmas(lambda e: e.dma_start(out=y0t[:, :], in_=y0fc[:, :]))

        esA = ExitStack()

        def sbA(n, s, d=F32):
            return esA.enter_context(nc.sbuf_tensor(n, s, d)).ap()

        WT = sbA("WT", [128, 8 * 2560], BF16)
        WoT = sbA("WoT", [128, 8 * D], BF16)
        sm_b = sbA("sm_b", [8, 2 * SCORE], F32)
        sm_u = sbA("sm_u", [9, 2 * SCORE], F32)
        rhseP = [sbA("rhse0", [9, 1024], F32), sbA("rhse1", [9, 1024], F32)]
        trnP = [sbA("trn0", [8, 768], F32), sbA("trn1", [8, 768], F32)]
        rhse = rhseP[0]
        trn = trnP[0]
        lnZt = sbA("lnZt", [8, 2], F32)
        Pst = sbA("Pst", [8, 2 * D], F32)
        Pbf = sbA("Pbf", [8, 2 * D], BF16)
        dAt = sbA("dAt", [8, 2], F32)
        kdbfP = [sbA("kdbf0", [8, 128], BF16), sbA("kdbf1", [8, 128], BF16)]
        kdbf = kdbfP[0]
        ccs = sbA("ccs", [128, 1032], F32)
        ccj = sbA("ccj", [8, 2 * 1032], F32)
        vvtP = [sbA("vvt0", [128, D], BF16), sbA("vvt1", [128, D], BF16)]
        vvt = vvtP[0]
        accRP = [sbA("accR0", [128, 8], F32), sbA("accR1", [128, 8], F32)]
        accR = accRP[0]
        DtokP = [sbA("Dtok0", [128, 8], F32), sbA("Dtok1", [128, 8], F32)]
        Dtok = DtokP[0]
        rDtkP = [sbA("rDtk0", [128, 8], F32), sbA("rDtk1", [128, 8], F32)]
        rDtk = rDtkP[0]
        eKTP = [sbA("eKT0", [128, 8], F32), sbA("eKT1", [128, 8], F32)]
        eKT = eKTP[0]
        wclTP = [sbA("wclT0", [128, 8], BF16), sbA("wclT1", [128, 8], BF16)]
        wclT = wclTP[0]
        pvTP = [sbA("pvT0", [128, 64], F32), sbA("pvT1", [128, 64], F32)]
        pvT = pvTP[0]
        RhP = [sbA("Rh0", [128, 1024], F32), sbA("Rh1", [128, 1024], F32)]
        Rh = RhP[0]
        RTsP = [sbA("RTs0", [128, 128], BF16), sbA("RTs1", [128, 128], BF16)]
        RTs = RTsP[0]
        yTs = sbA("yTs", [128, D], BF16)
        sxP = [sbA("sx0", [128, 1], F32), sbA("sx1", [128, 1], F32)]
        sx = sxP[0]
        sxxP = [sbA("sxx0", [128, 1], F32), sbA("sxx1", [128, 1], F32)]
        sxx = sxxP[0]
        mntP = [sbA("mnt0", [128, 1], F32), sbA("mnt1", [128, 1], F32)]
        mnt = mntP[0]
        vrtP = [sbA("vrt0", [128, 1], F32), sbA("vrt1", [128, 1], F32)]
        vrt = vrtP[0]
        rstdP = [sbA("rstd0", [128, 1], F32), sbA("rstd1", [128, 1], F32)]
        rstd = rstdP[0]
        mbnP = [sbA("mbn0", [128, 1], F32), sbA("mbn1", [128, 1], F32)]
        mbn = mbnP[0]

        wsl = sbA("wsl", [128, 1024], F32)
        wq_o = sbA("wq_o", [128, 1024], BF16)
        wq_p = sbA("wq_p", [128, 1024], BF16)
        xnt = sbA("xnt", [128, D], BF16)
        xnT = sbA("xnT", [128, D], BF16)
        kev = sbA("kev", [128, 512], F32)
        qkt = sbA("qkt", [128, 512], F32)
        spe = sbA("spe", [128, 512], F32)
        batP = [sbA("bat0", [128, 16], F32), sbA("bat1", [128, 16], F32)]
        xqT = sbA("xqT", [128, D], BF16)

        mzc = [go(lambda g: g.memset(cst[:, 0:1], 1e-5)),
               go(lambda g: g.memset(cst[:, 1:2], 1.0))]
        mz = [go(lambda g: g.memset(sm_u[:, :], 1.0)),
              go(lambda g: g.memset(lnZt[:, :], NEG)),
              go(lambda g: g.memset(Pst[:, :], 0.0)),
              go(lambda g: g.memset(dAt[:, :], 0.0))]
        dc_re = dmas(lambda e: e.dma_start(out=rhseP[0][:, :],
                                           in_=rhsec[:, :]))
        dc_re2 = dmas(lambda e: e.dma_start(out=rhseP[1][:, :],
                                            in_=rhsec[:, :]))

        prevw = None
        for (Wsrc, nr, kind) in (() if _os.environ.get("MK_SKIPW") else
                                 ((Wq, 512, 0), (Wk, 512, 1), (Wa, 512, 2),
                                  (Wv, D, 3), (Wo, D, 4))):
            for r in range(nr // 128):
                dld = dmas(lambda e, Ws=Wsrc, rr=r: e.dma_start(
                    out=wsl[:, 0:1024], in_=Ws[rr * 128:(rr + 1) * 128, :]),
                    [prevw])
                wcb = vo(lambda v: v.tensor_copy(wq_o[:, 0:1024],
                                                 wsl[:, 0:1024]), [dld])
                pieces = []
                for dc in range(8):
                    if kind < 4:
                        col = dc * 2560 + kind * 512 + r * 128
                        tgt = WT
                    else:
                        col = dc * D + r * 128
                        tgt = WoT
                    pieces.append((wq_o[:, dc * 128:(dc + 1) * 128],
                                   (lambda s, ps, cl=col, tg=tgt:
                                    s.activation(out=tg[:, cl:cl + 128],
                                                 in_=ps, func=AF.Copy))))
                evs = twave(pieces, [wcb])
                prevw = evs[-1]
        wA_last = prevw if prevw is not None else dc_id

        wqs = {"q": wA_last, "pd": None}
        wq_emits = []

        def _wq_slice(Wsrc, rr, off, w_, dst, colbase):
            dld = dmas(lambda e, Ws=Wsrc: e.dma_start(
                out=wsl[:, 0:w_],
                in_=Ws[rr * 128:(rr + 1) * 128, off:off + w_]),
                [wqs["q"], wqs["pd"]])
            dq = emit_quant(w_, wsl, wq_o,
                            [dld] + ([wqs["pd"]] if wqs["pd"] else []),
                            pool=True)
            wqs["q"] = dq
            npc = w_ // 128
            evs = []
            for w0 in range(0, npc, 8):
                pieces = []
                for dcl in range(w0, min(w0 + 8, npc)):
                    pieces.append((wq_o[:, dcl * 128:(dcl + 1) * 128],
                                   (lambda s, ps, d2=dcl:
                                    s.activation(
                        out=wq_p[:, d2 * 128:(d2 + 1) * 128],
                        in_=ps, func=AF.Copy))))
                evs += twave(pieces,
                             [dq] + ([wqs["pd"]] if wqs["pd"] else []))
            for dcl in range(npc):
                dc = off // 128 + dcl
                wqs["pd"] = dmag(lambda e, ds=dst, dcc=dc, d2=dcl,
                                 cb=colbase: e.dma_start(
                    out=ds[dcc, :, cb + rr * 128:cb + rr * 128 + 128],
                    in_=wq_p[:, d2 * 128:(d2 + 1) * 128]),
                    [evs[dcl]] + ([wqs["pd"]] if wqs["pd"] else []))

        for (Wsrc, nr, wc, dst, colbase) in () if (
                PHASE < 2 or _os.environ.get("MK_SKIPW")) else (
                (Wg, I_FF, D, s_wgu, 0), (Wu, I_FF, D, s_wgu, I_FF),
                (Wd, D, I_FF, s_wd, 0)):
            slices = ([(0, 1024), (1024, 1024), (2048, 768)]
                      if wc == 2816 else [(0, wc)])
            for r in range(nr // 128):
                for (off, w_) in slices:
                    wq_emits.append(
                        (lambda Ws=Wsrc, rr=r, o_=off, ww=w_, ds=dst,
                         cb=colbase: _wq_slice(Ws, rr, o_, ww, ds, cb)))

        wq_i = {"i": 0}

        def wq_pump(frac):
            tgt = min(len(wq_emits), int(frac * len(wq_emits) + 0.999))
            while wq_i["i"] < tgt:
                wq_emits[wq_i["i"]]()
                wq_i["i"] += 1

        psA_free = [wA_last]
        psB_free = []
        vv_st = {}
        chain = list(mz)
        prev_dvv = None
        prev_pmm = []
        prev_ar = []
        prev_xnd = []
        pend0 = [[], []]
        first_l2 = mzc[0]
        A0S = {}
        freeA = list(psA_free)
        freeB01 = []
        freeB2A = []

        def a0_tiles(cc):
            pc = cc % 2
            return (trnP[pc], vvtP[pc], sxP[pc], sxxP[pc], mntP[pc],
                    vrtP[pc], rstdP[pc], mbnP[pc], batP[pc],
                    xtP[pc], wclTP[pc], pvTP[pc])

        def a0_alpha(cc):
            nonlocal prev_dvv, prev_pmm, prev_ar, first_l2, prev_xnd
            bi, c = cc // NCH, cc % NCH
            pc = cc % 2
            (trn, vvt, sx, sxx, mnt, vrt, rstd, mbn, bat,
             xt, wclT, pvT) = a0_tiles(cc)
            pguard = pend0[pc]
            st = {}
            A0S[cc] = st
            row0 = bi * SCORE + c * C
            dx = dmas(lambda e, r0=row0: e.dma_start(
                out=xt[:, :], in_=x_sl[r0:r0 + 128, :]), pguard)
            l1 = vof(lambda v: v.tensor_reduce(sx[:, :], xt[:, :], AX.X,
                                              ALU.add), [dx])
            l2 = so(lambda s: s.activation(out=q_sq[:, 1024:2048],
                                           in_=xt[:, :],
                                           func=AF.Square,
                                           accum_out=sxx[:, :]),
                    [dx, first_l2] + chain[-1:])
            first_l2 = l2
            l3 = vof(lambda v: v.tensor_scalar(mnt[:, :], sx[:, :], 1.0 / D,
                                              None, ALU.mult), [l1])
            l4 = vof(lambda v: v.tensor_scalar(vrt[:, :], sxx[:, :], 1.0 / D,
                                              None, ALU.mult), [l2])
            l5 = vof(lambda v: v.tensor_tensor(sx[:, :], mnt[:, :], mnt[:, :],
                                              ALU.mult), [l3])
            l6 = vof(lambda v: v.tensor_tensor(vrt[:, :], vrt[:, :], sx[:, :],
                                              ALU.subtract), [l4, l5])
            l7 = so(lambda s: s.activation(out=vrt[:, :], in_=vrt[:, :],
                                           func=AF.Ln, bias=cst[:, 0:1]),
                    [l6] + mzc)
            l8 = so(lambda s: s.activation(out=rstd[:, :], in_=vrt[:, :],
                                           func=AF.Exp, scale=-0.5), [l7])
            l9 = vof(lambda v: v.tensor_tensor(mbn[:, :], mnt[:, :],
                                              rstd[:, :], ALU.mult), [l8])
            l10 = vof(lambda v: v.tensor_scalar(mbn[:, :], mbn[:, :], -1.0,
                                               None, ALU.mult), [l9])
            l11 = vo(lambda v: v.tensor_scalar(xnt[:, :], xt[:, :],
                                               rstd[:, :], mbn[:, :],
                                               ALU.mult, ALU.add),
                      [l10, l8] + prev_xnd)
            xnd = twave([(xnt[:, dc * 128:(dc + 1) * 128],
                          (lambda s, ps, d2=dc: s.activation(
                              out=xnT[:, d2 * 128:(d2 + 1) * 128],
                              in_=ps, func=AF.Copy)))
                         for dc in range(8)], [l11] + prev_pmm)
            pb = mm(psA[:, 1024:1536], onet[0:1, :], wabt[0:1, :], True, False,
                    [dc_on, dc_wb] + xnd + freeA)
            pmm = []
            for dc in range(8):
                lT = xnT[:, dc * 128:(dc + 1) * 128]
                bs = dc * 2560
                for si, (oap, wcl) in enumerate((
                        (psA[:, 0:512], bs), (psA[:, 512:1024], bs + 512),
                        (psA[:, 1024:1536], bs + 1024),
                        (psB[:, 0:512], bs + 1536),
                        (psB[:, 512:1024], bs + 2048))):
                    dd = [pb] + xnd + (freeB01 if si >= 3 else [])
                    pmm.append(mm(oap, lT, WT[:, wcl:wcl + 512],
                                  (dc == 0) and si != 2, dc == 7, dd))
            kd_ = so(lambda s: s.activation(out=kev[:, :],
                                            in_=psA[:, 512:1024],
                                            func=AF.Copy), pmm)
            qk = vo(lambda v: v.tensor_tensor(qkt[:, :], psA[:, 0:512],
                                              kev[:, :], ALU.mult),
                    [kd_] + pmm)
            qk3 = qkt[:, :].rearrange("p (h k) -> p h k", k=64)
            br = vof(lambda v: v.tensor_reduce(bat[:, 0:8], qk3, AX.X, ALU.add),
                    [qk])
            sp1 = so(lambda s: s.activation(out=spe[:, :],
                                            in_=psA[:, 1024:1536],
                                            func=AF.Exp), pmm + prev_ar)
            sp2 = so(lambda s: s.activation(out=spe[:, :], in_=spe[:, :],
                                            func=AF.Ln, bias=cst[:, 1:2]),
                     [sp1] + mzc)
            sp3 = spe[:, :].rearrange("p (h k) -> p h k", k=64)
            ar = vof(lambda v: v.tensor_reduce(bat[:, 8:16], sp3, AX.X, ALU.add,
                                              negate=True), [sp2])
            vvd = []
            for (pp, c0, o0) in ((psB, 0, 0), (psB, 512, 512)):
                sg1 = so(lambda s, p_=pp, a_=c0: s.activation(
                    out=q_sg[:, 1024:1536], in_=p_[:, a_:a_ + 512],
                    func=AF.Sign),
                    pmm + [qk, sp1, kd_] + vvd +
                    ([prev_dvv] if prev_dvv else []))
                sg2 = so(lambda s: s.mul(q_sg[:, 1024:1536],
                                         q_sg[:, 1024:1536], C1V), [sg1])
                vvd.append(vo(lambda v, p_=pp, a_=c0, b_=o0:
                              v.scalar_tensor_tensor(
                    vvt[:, b_:b_ + 512], p_[:, a_:a_ + 512], C2V,
                    q_sg[:, 1024:1536], ALU.mult, ALU.add), [sg2]))
            dvv = dmag(lambda e, ccx=cc: e.dma_start(
                out=s_vv[ccx, :, :], in_=vvt[:, :]), vvd)
            if cc == NCC - 1 and _os.environ.get("MK_DBG"):
                S_.add("sync", lambda e: e.dma_start(
                    out=d_vvl[:, :], in_=vvt[:, :]), vvd + [dvv], is_dma=True)
            vv_st[cc] = dvv
            prev_dvv = dvv
            prev_pmm = pmm[-1:]
            prev_ar = [ar]
            prev_xnd = xnd[-1:]
            st.update(l11=l11, pmm=pmm, kd_=kd_, qk=qk, br=br, sp1=sp1,
                      sp2=sp2, ar=ar, vvd=vvd, dvv=dvv)

        def a0_beta(cc):
            nonlocal chain, psA_free, psB_free, freeA, freeB01, freeB2A
            bi, c = cc // NCH, cc % NCH
            pc = cc % 2
            (trn, vvt, sx, sxx, mnt, vrt, rstd, mbn, bat,
             xt, wclT, pvT) = a0_tiles(cc)
            st = A0S[cc]
            l11, pmm, kd_, qk, br, sp1, sp2, ar, vvd, dvv = (
                st["l11"], st["pmm"], st["kd_"], st["qk"], st["br"],
                st["sp1"], st["sp2"], st["ar"], st["vvd"], st["dvv"])
            gc = bi * SCORE + c * C
            uc = slice(gc, gc + C)
            bt = mm(psB[0:8, 1024:1152], bat[:, 0:8], idf[:, :],
                    True, True, [br, dc_id] + freeB2A, tr=True)
            at = mm(psB[0:8, 1152:1280], bat[:, 8:16], idf[:, :],
                    True, True, [ar, dc_id] + freeB2A, tr=True)
            acp = vof(lambda v: v.tensor_copy(
                trn[0:8, 384:512], psB[0:8, 1152:1280]), [at])
            Acs = vof(lambda v: v.tensor_tensor_scan(
                trn[0:8, 0:128], trn[0:8, 384:512],
                trn[0:8, 384:512], 0.0, ALU.add, ALU.bypass), [acp])
            ucv = vof(lambda v: v.scalar_tensor_tensor(
                trn[0:8, 128:256], psB[0:8, 1024:1152], SCL_B,
                trn[0:8, 0:128], ALU.mult, ALU.subtract), [Acs, bt])
            bsc = vof(lambda v: v.tensor_scalar(
                trn[0:8, 512:640], psB[0:8, 1024:1152], SCL_B,
                None, ALU.mult), [bt])
            du = dmag(lambda e, u_=uc: e.dma_start(
                out=sm_u[0:8, u_], in_=trn[0:8, 128:256]),
                [ucv] + mz[0:2])
            db = dmag(lambda e, u_=uc: e.dma_start(
                out=sm_b[0:8, u_], in_=trn[0:8, 512:640]), [bsc])
            nl = vof(lambda v: v.tensor_scalar(
                trn[0:8, 640:641], trn[0:8, 255:256], -1.0, None,
                ALU.mult), [ucv])
            wx = so(lambda s: s.activation(
                out=trn[0:8, 256:384], in_=trn[0:8, 128:256],
                func=AF.Exp, bias=trn[0:8, 640:641]), [nl])
            rz = vof(lambda v: v.tensor_reduce(
                trn[0:8, 641:642], trn[0:8, 256:384], AX.X,
                ALU.add), [wx])
            ek = so(lambda s, b2=bi: s.activation(
                out=trn[0:8, 642:643], in_=lnZt[0:8, b2:b2 + 1],
                func=AF.Exp, bias=trn[0:8, 640:641]), [nl] + chain)
            dl = vof(lambda v: v.tensor_tensor(
                trn[0:8, 643:644], trn[0:8, 641:642],
                trn[0:8, 642:643], ALU.add), [rz, ek])
            rdl = vof(lambda v: v.reciprocal(
                trn[0:8, 644:645], trn[0:8, 643:644]), [dl])
            kd0 = vof(lambda v: v.tensor_tensor(
                trn[0:8, 645:646], trn[0:8, 642:643],
                trn[0:8, 644:645], ALU.mult), [rdl])
            wn = vof(lambda v: v.tensor_scalar(
                trn[0:8, 256:384], trn[0:8, 256:384],
                trn[0:8, 644:645], None, ALU.mult), [rdl, rz])
            wnt = mm(psB[:, 1280:1288], trn[0:8, 256:384], idf[0:8, 0:8],
                     True, True, [wn, dc_id, bsc, db] + freeB2A, tr=True)
            wne = vo(lambda v: v.tensor_copy(wclT[:, :], psB[:, 1280:1288]),
                     [wnt])
            pvm = []
            for dc in range(8):
                pvm.append(mm(psB[:, 1288 + dc * 8:1296 + dc * 8],
                              vvt[:, dc * 128:(dc + 1) * 128], wclT[:, :],
                              True, True, [wne] + vvd + freeB2A))
            pve = vo(lambda v: v.tensor_copy(pvT[:, :], psB[:, 1288:1352]),
                     pvm)
            ptm = []
            for dc in range(8):
                ptm.append(mm(psA[0:8, dc * 128:(dc + 1) * 128],
                              pvT[:, dc * 8:(dc + 1) * 8], idf[:, :],
                              True, True,
                              [pve, dc_id, kd_, qk, sp1] + vvd, tr=True))
            pup = vo(lambda v, b2=bi: v.scalar_tensor_tensor(
                Pst[0:8, b2 * D:(b2 + 1) * D], Pst[0:8, b2 * D:(b2 + 1) * D],
                trn[0:8, 645:646], psA[0:8, 0:1024], ALU.mult,
                ALU.add), ptm + [kd0] + chain)
            lnd = so(lambda s: s.activation(
                out=trn[0:8, 646:647], in_=trn[0:8, 643:644],
                func=AF.Ln), [dl])
            lnu = vo(lambda v, b2=bi: v.tensor_tensor(
                lnZt[0:8, b2:b2 + 1], trn[0:8, 639:640],
                trn[0:8, 646:647], ALU.add), [lnd, bsc, ek] + chain)
            dau = vo(lambda v, b2=bi: v.tensor_tensor(
                dAt[0:8, b2:b2 + 1], dAt[0:8, b2:b2 + 1],
                trn[0:8, 127:128], ALU.add), [Acs, ucv] + chain)
            chain = [pup, lnu, dau]
            freeA = [pup, kd_, qk, sp1]
            freeB01 = list(vvd)
            freeB2A = [Acs, ucv, bsc, wne, pve, acp, pup]
            psA_free = [pup, kd_, qk, sp1]
            psB_free = [Acs, ucv, bsc, wne, pve, acp] + vvd
            pend0[pc] = [pup, lnu, dau, dvv, du, db, pve, wne, l11, qk,
                         kd_, sp2, ar]

        NA0 = 0 if _os.environ.get('MK_NOA0') else (NCC if PHASE >= 3 else 0)
        for cc in range(NA0):
            if cc > 0:
                a0_beta(cc - 1)
            wq_pump((cc + 1.0) / max(NA0, 1))
            a0_alpha(cc)
        if NA0:
            a0_beta(NA0 - 1)
        wq_pump(1.0)
        prevq = wqs["q"]
        wq_last = wqs["pd"] if wqs["pd"] is not None else wA_last

        mzp = go(lambda g: g.memset(trn[0:8, 700:706], 0.0))
        dis = [] if PHASE < 4 else [
            dmag(lambda e: e.dma_start(out=cc_in[0:8, 1026:1032],
                                       in_=trn[0:8, 700:706]), [mzp] + chain),
            dmag(lambda e: e.dma_start(out=cc_in[8:16, 1026:1032],
                                       in_=trn[0:8, 700:706]), [mzp] + chain),
            dmag(lambda e: e.dma_start(out=cc_in[0:8, 0:1],
                                       in_=lnZt[0:8, 0:1],
                                       allow_slow_non_contiguous=True), chain),
            dmag(lambda e: e.dma_start(out=cc_in[8:16, 0:1],
                                       in_=lnZt[0:8, 1:2],
                                       allow_slow_non_contiguous=True), chain),
            dmag(lambda e: e.dma_start(out=cc_in[0:8, 1:2],
                                       in_=dAt[0:8, 0:1],
                                       allow_slow_non_contiguous=True), chain),
            dmag(lambda e: e.dma_start(out=cc_in[8:16, 1:2],
                                       in_=dAt[0:8, 1:2],
                                       allow_slow_non_contiguous=True), chain),
            dmag(lambda e: e.dma_start(out=cc_in[0:8, 2:1026],
                                       in_=Pst[0:8, 0:D]), chain),
            dmag(lambda e: e.dma_start(out=cc_in[8:16, 2:1026],
                                       in_=Pst[0:8, D:2 * D]), chain),
        ]
        if _os.environ.get("MK_NOCOL"):
            col = None if PHASE < 4 else go(
                lambda g: g.memset(ccs[0:1, 0:1], 0.0), dis)
        else:
            col = None if PHASE < 4 else S_.add(
                "gpsimd", lambda g: g.collective_compute(
                    "AllGather", ALU.bypass,
                    replica_groups=[list(range(NCORE))],
                    ins=[cc_in[:, :]], outs=[cc_out[:, :]]), dis, force=True)
        dco = None if PHASE < 4 else dmag(
            lambda e: e.dma_start(out=ccs[:, :], in_=cc_out[:, :]), [col])
        aj1 = None if PHASE < 4 else vof(lambda v: v.tensor_tensor(ccs[:, 0:1], ccs[:, 0:1],
                                           adjt[:, 0:1], ALU.add),
                 [dco, dc_aj])
        aj2 = None if PHASE < 4 else vof(lambda v: v.tensor_tensor(ccs[:, 1:2], ccs[:, 1:2],
                                           adjt[:, 1:2], ALU.mult),
                 [dco, dc_aj])
        rs1 = None if PHASE < 4 else go(lambda g: g.memset(lnZt[:, :], NEG),
                                        chain + dis)
        rs2 = None if PHASE < 4 else go(lambda g: g.memset(Pst[:, :], 0.0),
                                        chain + dis)
        fold = [rs1, rs2, aj1, aj2] if PHASE >= 4 else list(chain)
        for j in range(0 if _os.environ.get('MK_NOF') else (NCORE - 1 if PHASE >= 4 else 0)):
            dj1 = dmag(lambda e, jj=j: e.dma_start(
                out=ccj[0:8, 0:1032], in_=ccs[16 * jj:16 * jj + 8, :]), fold)
            dj2 = dmag(lambda e, jj=j: e.dma_start(
                out=ccj[0:8, 1032:2064],
                in_=ccs[16 * jj + 8:16 * jj + 16, :]), fold)
            nfold = []
            for bi in range(2):
                co = bi * 1032
                f1 = vof(lambda v, c_=co, b2=bi: v.tensor_tensor(
                    trn[0:8, 0:1], lnZt[0:8, b2:b2 + 1],
                    ccj[0:8, c_ + 1:c_ + 2], ALU.add), [dj1, dj2] + fold)
                f2 = vof(lambda v, c_=co: v.tensor_tensor(
                    trn[0:8, 1:2], trn[0:8, 0:1], ccj[0:8, c_:c_ + 1],
                    ALU.max), [f1])
                f3a = vof(lambda v: v.tensor_tensor(
                    trn[0:8, 2:3], trn[0:8, 0:1], trn[0:8, 1:2],
                    ALU.subtract), [f2])
                f3b = vof(lambda v, c_=co: v.tensor_tensor(
                    trn[0:8, 3:4], ccj[0:8, c_:c_ + 1], trn[0:8, 1:2],
                    ALU.subtract), [f2])
                f4 = so(lambda s: s.activation(out=trn[0:8, 4:5],
                                               in_=trn[0:8, 2:3],
                                               func=AF.Exp), [f3a])
                f5 = so(lambda s: s.activation(out=trn[0:8, 5:6],
                                               in_=trn[0:8, 3:4],
                                               func=AF.Exp), [f3b])
                f6 = vof(lambda v: v.tensor_tensor(trn[0:8, 6:7], trn[0:8, 4:5],
                                                  trn[0:8, 5:6], ALU.add),
                        [f4, f5])
                f7 = so(lambda s: s.activation(out=trn[0:8, 7:8],
                                               in_=trn[0:8, 6:7],
                                               func=AF.Ln), [f6])
                f8 = vof(lambda v, b2=bi: v.tensor_tensor(
                    lnZt[0:8, b2:b2 + 1], trn[0:8, 1:2], trn[0:8, 7:8],
                    ALU.add), [f7])
                f9a = vof(lambda v, b2=bi: v.tensor_tensor(
                    trn[0:8, 8:9], trn[0:8, 0:1], lnZt[0:8, b2:b2 + 1],
                    ALU.subtract), [f8])
                f9b = vof(lambda v, c_=co, b2=bi: v.tensor_tensor(
                    trn[0:8, 9:10], ccj[0:8, c_:c_ + 1],
                    lnZt[0:8, b2:b2 + 1], ALU.subtract), [f8])
                fa = so(lambda s: s.activation(out=trn[0:8, 10:11],
                                               in_=trn[0:8, 8:9],
                                               func=AF.Exp), [f9a])
                fb = so(lambda s: s.activation(out=trn[0:8, 11:12],
                                               in_=trn[0:8, 9:10],
                                               func=AF.Exp), [f9b])
                fc = vof(lambda v, c_=co: v.tensor_scalar(
                    ccj[0:8, c_ + 2:c_ + 1026], ccj[0:8, c_ + 2:c_ + 1026],
                    trn[0:8, 11:12], None, ALU.mult), [fb])
                fd = vo(lambda v, c_=co, b2=bi: v.scalar_tensor_tensor(
                    Pst[0:8, b2 * D:(b2 + 1) * D],
                    Pst[0:8, b2 * D:(b2 + 1) * D], trn[0:8, 10:11],
                    ccj[0:8, c_ + 2:c_ + 1026], ALU.mult, ALU.add), [fa, fc])
                nfold += [fd, f8]
            fold = nfold

        a0_tail = (list(chain) + psA_free + psB_free + pend0[0] + pend0[1]
                   + [prevq, wq_last])
        chain = list(fold)
        prev_x2L = list(psB_free)
        prev_x2 = None
        prev_sx2 = None
        prev_xq = None
        pend1 = [list(a0_tail), list(a0_tail)]
        prev_dxq = []
        prev_m2 = []

        freeA01 = list(psA_free)
        freeA2 = list(psA_free)
        freeB2 = list(psB_free)
        prev_pups = list(psB_free)
        prev_yg = []
        A1S = {}

        def a1_tiles(cc):
            pc = cc % 2
            return (trnP[pc], vvtP[pc], accRP[pc], DtokP[pc], rDtkP[pc],
                    eKTP[pc], wclTP[pc], pvTP[pc], RhP[pc], RTsP[pc],
                    sxP[pc], sxxP[pc], mntP[pc], vrtP[pc],
                    rstdP[pc], mbnP[pc], batP[pc], xtP[pc],
                    rhseP[pc], kdbfP[pc])

        def a1_alpha(cc):
            nonlocal chain
            bi, c = cc // NCH, cc % NCH
            pc = cc % 2
            (trn, vvt, accR, Dtok, rDtk, eKT, wclT, pvT, Rh, RTs, sx,
             sxx, mnt, vrt, rstd, mbn, bat,
             xt, rhse, kdbf) = a1_tiles(cc)
            pguard = pend1[pc]
            st = {}
            A1S[cc] = st
            gc = bi * SCORE + c * C
            uc = slice(gc, gc + C)
            dvv = dmag(lambda e, ccx=cc: e.dma_start(
                out=vvt[:, :], in_=s_vv[ccx, :, :]),
                [vv_st[cc]] + pguard)
            duf = dmag(lambda e, u_=uc: e.dma_start(
                out=rhse[8:9, 0:1024], in_=sm_u[0:8, u_]),
                [dc_re, dc_re2] + pguard)
            pbf = vo(lambda v: v.tensor_copy(Pbf[0:8, :], Pst[0:8, :]),
                     chain + prev_yg)
            nl1 = vof(lambda v, u_=uc: v.tensor_scalar(
                trn[0:8, 0:128], sm_u[0:8, u_], -1.0, None,
                ALU.mult), chain)
            emm = []
            for h in range(8):
                emm.append(mm(psA[:, h * 128:(h + 1) * 128],
                              sm_u[0:9, uc],
                              rhse[0:9, h * 128:(h + 1) * 128],
                              h % 4 == 0, False, [duf] + freeA01))
            for h in range(8):
                emm.append(mm(psA[:, h * 128:(h + 1) * 128], idf[:, :],
                              mskt[:, :], False, h % 4 == 3,
                              [dc_id, dc_mk, emm[h]]))
            exd = []
            for h in range(8):
                stopper = emm[8 + (3 if h < 4 else 7)]
                exd.append(so(lambda s, hh=h: s.activation(
                    out=Rh[:, hh * 128:(hh + 1) * 128],
                    in_=psA[:, hh * 128:(hh + 1) * 128], func=AF.Exp,
                    accum_out=accR[:, hh:hh + 1]), [emm[8 + h], stopper]))
            ekh = so(lambda s, b2=bi: s.activation(
                out=trn[0:8, 128:256], in_=trn[0:8, 0:128],
                func=AF.Exp, bias=lnZt[0:8, b2:b2 + 1]), [nl1] + chain)
            ekt = mm(psB[:, 1024:1032], trn[0:8, 128:256], idf[0:8, 0:8],
                     True, True, [ekh, dc_id] + freeB2, tr=True)
            eke = vof(lambda v: v.tensor_copy(eKT[:, :], psB[:, 1024:1032]),
                     [ekt])
            dtk = vof(lambda v: v.tensor_tensor(Dtok[:, :], accR[:, :],
                                               eKT[:, :], ALU.add),
                     exd + [eke])
            rdt = vof(lambda v: v.reciprocal(rDtk[:, :], Dtok[:, :]), [dtk])
            nrm = []
            for h in range(8):
                nrm.append(vo(lambda v, hh=h: v.tensor_scalar(
                    Rh[:, hh * 128:(hh + 1) * 128],
                    Rh[:, hh * 128:(hh + 1) * 128], rDtk[:, hh:hh + 1], None,
                    ALU.mult), [rdt, exd[h]]))
            dtt = mm(psB[0:8, 1032:1160], Dtok[:, :], idf[:, :],
                     True, True, [dtk, dc_id, rdt] + freeB2, tr=True)
            kdt = vof(lambda v: v.tensor_tensor(eKT[:, :], eKT[:, :],
                                               rDtk[:, :], ALU.mult),
                     [rdt, dtk])
            rdl = vof(lambda v: v.reciprocal(
                trn[0:8, 648:649], psB[0:8, 1159:1160]), [dtt])
            lnd = so(lambda s: s.activation(
                out=trn[0:8, 650:651], in_=psB[0:8, 1159:1160],
                func=AF.Ln), [dtt])
            kdm = mm(psB[0:8, 1160:1288], eKT[:, :], idf[:, :],
                     True, True, [kdt, dc_id, rdl, lnd] + freeB2, tr=True)
            kde = vo(lambda v: v.tensor_copy(
                kdbf[0:8, :], psB[0:8, 1160:1288]), [kdm])
            kdl = vof(lambda v: v.tensor_copy(
                trn[0:8, 647:648], psB[0:8, 1287:1288]), [kdm])
            rtm = []
            for h in range(8):
                rtm.append(mm(psA[:, 1024:1152],
                              Rh[:, h * 128:(h + 1) * 128], idf[:, :],
                              h == 0, h == 7,
                              [nrm[h], dc_id] + freeA2, tr=True))
            rte = so(lambda s: s.activation(out=RTs[:, :],
                                            in_=psA[:, 1024:1152],
                                            func=AF.Copy), rtm)
            lnu = vo(lambda v, u_=uc, b2=bi: v.tensor_tensor(
                lnZt[0:8, b2:b2 + 1],
                sm_b[0:8, u_.start + 127:u_.start + 128],
                trn[0:8, 650:651], ALU.add), [lnd, ekh] + chain)
            st.update(dvv=dvv, exd=exd, nrm=nrm, rte=rte, kde=kde, kdl=kdl,
                      rdl=rdl, lnd=lnd, eke=eke, dtk=dtk, kdt=kdt, lnu=lnu,
                      pbf=pbf, ekh=ekh)

        def a1_beta(cc):
            nonlocal chain, prev_sx2, prev_xq, prev_dxq, freeA01, freeA2, \
                freeB2, prev_pups, prev_yg, prev_m2
            bi, c = cc // NCH, cc % NCH
            pc = cc % 2
            (trn, vvt, accR, Dtok, rDtk, eKT, wclT, pvT, Rh, RTs, sx,
             sxx, mnt, vrt, rstd, mbn, bat,
             xt, rhse, kdbf) = a1_tiles(cc)
            pguard = pend1[pc]
            st = A1S[cc]
            dvv, exd, rte, kde, kdl, rdl, lnd, lnu, pbf = (
                st["dvv"], st["exd"], st["rte"], st["kde"], st["kdl"],
                st["rdl"], st["lnd"], st["lnu"], st["pbf"])
            row0 = bi * SCORE + c * C
            gc = bi * SCORE + c * C
            uc = slice(gc, gc + C)
            ymm = []
            yg = []
            for dc in range(8):
                m1 = mm(psB[:, dc * 128:(dc + 1) * 128],
                        Pbf[0:8, bi * D + dc * 128:bi * D + (dc + 1) * 128],
                        kdbf[0:8, :], dc % 4 == 0, False,
                        [pbf, kde, dvv] + prev_pups)
                m2_ = mm(psB[:, dc * 128:(dc + 1) * 128],
                         vvt[:, dc * 128:(dc + 1) * 128], RTs[:, :],
                         False, dc % 4 == 3, [m1, rte, dvv])
                ymm += [m1, m2_]
                yg.append(m2_)
            yev = []
            for dc in range(8):
                stopper = yg[3 if dc < 4 else 7]
                yev.append(so(lambda s, dcc=dc: s.activation(
                    out=yTs[:, dcc * 128:(dcc + 1) * 128],
                    in_=psB[:, dcc * 128:(dcc + 1) * 128], func=AF.Copy),
                    [yg[dc], stopper]))
            if c == 0:
                for dc in range(8):
                    yev.append(vo(lambda v, dcc=dc: v.tensor_scalar(
                        yTs[:, dcc * 128:dcc * 128 + 1],
                        yTs[:, dcc * 128:dcc * 128 + 1], y0t[:, 0:1], None,
                        ALU.mult), [yev[dc], dc_y0]))
            nl2 = vof(lambda v, u_=uc: v.tensor_scalar(
                trn[0:8, 649:650],
                sm_u[0:8, u_.start + 127:u_.start + 128],
                -1.0, None, ALU.mult), chain)
            wx1 = so(lambda s, u_=uc: s.activation(
                out=trn[0:8, 256:384], in_=sm_u[0:8, u_],
                func=AF.Exp, bias=trn[0:8, 649:650]), [nl2])
            wn1 = vof(lambda v: v.tensor_scalar(
                trn[0:8, 256:384], trn[0:8, 256:384],
                trn[0:8, 648:649], None, ALU.mult), [wx1, rdl])
            wt1 = mm(psB[:, 1280:1288], trn[0:8, 256:384], idf[0:8, 0:8],
                     True, True, [wn1, dc_id, lnd, kde, kdl] + freeB2,
                     tr=True)
            we1 = vo(lambda v: v.tensor_copy(wclT[:, :], psB[:, 1280:1288]),
                     [wt1])
            pvm = []
            for dc in range(8):
                pvm.append(mm(psB[:, 1288 + dc * 8:1296 + dc * 8],
                              vvt[:, dc * 128:(dc + 1) * 128], wclT[:, :],
                              True, True, [we1, dvv]))
            pve = vo(lambda v: v.tensor_copy(pvT[:, :], psB[:, 1288:1352]),
                     pvm)
            ptm2 = []
            for dc in range(8):
                ptm2.append(mm(psB[0:8, dc * 128:(dc + 1) * 128],
                               pvT[:, dc * 8:(dc + 1) * 8], idf[:, :],
                               True, True,
                               [pve, dc_id] + yev[0:8], tr=True))
            pups = [vo(lambda v, b2=bi: v.scalar_tensor_tensor(
                Pst[0:8, b2 * D:(b2 + 1) * D], Pst[0:8, b2 * D:(b2 + 1) * D],
                trn[0:8, 647:648], psB[0:8, 0:1024], ALU.mult, ALU.add),
                ptm2 + [pbf, kdl] + chain)]
            wmm = []
            for dc in range(8):
                lT = yTs[:, dc * 128:(dc + 1) * 128]
                for si in range(2):
                    wmm.append(mm(
                        psA[:, si * 512:(si + 1) * 512], lT,
                        WoT[:, dc * D + si * 512:dc * D + (si + 1) * 512],
                        dc == 0, dc == 7, yev + exd))
            dx2 = dmas(lambda e, r0=row0: e.dma_start(
                out=xt[:, :], in_=x_sl[r0:r0 + 128, :]), pguard)
            x2v = vo(lambda v: v.tensor_tensor(x2t[:, :], psA[:, 0:1024],
                                               xt[:, :], ALU.add),
                     wmm + [dx2] + prev_m2
                     + ([prev_sx2] if prev_sx2 else []))
            sx2 = dmag(lambda e, r0=row0: e.dma_start(
                out=s_x2[r0:r0 + 128, :], in_=x2t[:, :]), [x2v])
            prev_sx2 = sx2
            m1 = vof(lambda v: v.tensor_reduce(sx[:, :], x2t[:, :], AX.X,
                                              ALU.add), [x2v])
            m2 = so(lambda s: s.activation(out=q_sq[:, 1024:2048],
                                           in_=x2t[:, :],
                                           func=AF.Square,
                                           accum_out=sxx[:, :]),
                    [x2v] + ([prev_xq] if prev_xq else []))
            m3 = vof(lambda v: v.tensor_scalar(mnt[:, :], sx[:, :], 1.0 / D,
                                              None, ALU.mult), [m1])
            m4 = vof(lambda v: v.tensor_scalar(vrt[:, :], sxx[:, :], 1.0 / D,
                                              None, ALU.mult), [m2])
            m5 = vof(lambda v: v.tensor_tensor(sx[:, :], mnt[:, :], mnt[:, :],
                                              ALU.mult), [m3])
            m6 = vof(lambda v: v.tensor_tensor(vrt[:, :], vrt[:, :], sx[:, :],
                                              ALU.subtract), [m4, m5])
            m7 = so(lambda s: s.activation(out=vrt[:, :], in_=vrt[:, :],
                                           func=AF.Ln, bias=cst[:, 0:1]),
                    [m6] + mzc)
            m8 = so(lambda s: s.activation(out=rstd[:, :], in_=vrt[:, :],
                                           func=AF.Exp, scale=-0.5), [m7])
            m9 = vof(lambda v: v.tensor_tensor(mbn[:, :], mnt[:, :], rstd[:, :],
                                              ALU.mult), [m8])
            m10 = vof(lambda v: v.tensor_scalar(mbn[:, :], mbn[:, :], -1.0,
                                               None, ALU.mult), [m9])
            m11 = vo(lambda v: v.tensor_scalar(xt[:, :], x2t[:, :],
                                               rstd[:, :], mbn[:, :],
                                               ALU.mult, ALU.add),
                     [m10, m8, sx2])
            xq = emit_quant(1024, xt, xnt,
                            ([m11, m2] + a0_tail) if cc == 0
                            else ([m11, m2] + prev_dxq))
            prev_xq = xq
            xqd = twave([(xnt[:, dc * 128:(dc + 1) * 128],
                          (lambda s, ps, d2=dc: s.activation(
                              out=xqT[:, d2 * 128:(d2 + 1) * 128],
                              in_=ps, func=AF.Copy)))
                         for dc in range(8)],
                        [xq] + pguard + prev_dxq)
            dxq = dmag(lambda e, ccx=cc: e.dma_start(
                out=s_xqT[ccx, :, :], in_=xqT[:, :]), xqd)
            prev_dxq = [dxq]
            prev_m2 = [m2]
            chain = [lnu] + pups[-1:]
            freeA01 = [x2v]
            freeA2 = [rte]
            freeB2 = [st["eke"], kde, kdl, rdl, lnd, we1, pve]
            prev_pups = list(pups)
            prev_yg = yg[-1:]
            pend1[pc] = ([x2v, sx2, dxq, xq, pve, lnu, kde, kdl, rdl, lnd,
                          st["eke"], we1, rte] + pups + yev[0:8] + xqd
                         + pvm[-1:] + [st["dtk"], st["kdt"], st["nrm"][-1]])

        NA1 = NCC if PHASE >= 5 else 0
        for cc in range(NA1):
            if cc > 0:
                a1_beta(cc - 1)
            a1_alpha(cc)
        if NA1:
            a1_beta(NA1 - 1)

        if _os.environ.get("MK_DBG"):
            S_.add("sync", lambda e: e.dma_start(
                out=d_smub[:, :], in_=sm_u[:, :]), chain, is_dma=True)
            S_.add("sync", lambda e: e.dma_start(
                out=d_smbb[:, :], in_=sm_b[:, :]), chain, is_dma=True)
        prev_odma = None
        aend = [x for x in (pend1[0] + pend1[1]
                            + [prev_sx2, prev_dvv] + chain
                            + psA_free + psB_free + [prev_xq])
                if x is not None]
        esA.close()

        esB = ExitStack()

        def sbB(n, s, d=F32):
            return esB.enter_context(nc.sbuf_tensor(n, s, d)).ap()

        with esB:
            WguT = sbB("WguT", [128, 8 * 2 * I_FF], BF16)
            WdT = sbB("WdT", [128, 22 * D], BF16)
            hq = sbB("hq", [128, I_FF], BF16)
            hqT = sbB("hqT", [128, I_FF], BF16)
            silg = sbB("silg", [128, 704], F32)
            hhf = sbB("hhf", [128, 704], F32)
            outt = sbB("outt", [128, D], F32)
            xqL = sbB("xqL", [128, D], BF16)

            wld = []
            for dc in range(8 if PHASE >= 6 else 0):
                wld.append(dmas(lambda e, dcc=dc: e.dma_start(
                    out=WguT[:, dcc * 2 * I_FF:(dcc + 1) * 2 * I_FF],
                    in_=s_wgu[dcc, :, :]), [wq_last] + aend))
            for ic in range(22 if PHASE >= 6 else 0):
                wld.append(dmas(lambda e, icc=ic: e.dma_start(
                    out=WdT[:, icc * D:(icc + 1) * D], in_=s_wd[icc, :, :]),
                    [wq_last] + aend))

            prev_out = None
            prev_odma2 = None
            prev_hte = []
            psAf = aend
            psBf = aend
            for cc in range(NCC if PHASE >= 6 else 0):
                bi, c = cc // NCH, cc % NCH
                row0 = bi * SCORE + c * C
                dxq = dmas(lambda e, ccx=cc: e.dma_start(
                    out=xqL[:, :], in_=s_xqT[ccx, :, :]),
                    ([prev_out] if prev_out else aend))
                dx2 = dmas(lambda e, r0=row0: e.dma_start(
                    out=x2t[:, :], in_=s_x2[r0:r0 + 128, :]),
                    ([prev_out] if prev_out else aend))
                hqd = []
                hte = []
                hte_prev = list(prev_hte)
                isl = [(0, 512), (512, 512), (1024, 512), (1536, 512),
                       (2048, 512), (2560, 256)]
                for (io, wi) in isl:
                    gmm = []
                    for dc in range(8):
                        lT = xqL[:, dc * 128:(dc + 1) * 128]
                        wb = dc * 2 * I_FF
                        gmm.append(mm(psA[:, 0:wi], lT,
                                      WguT[:, wb + io:wb + io + wi],
                                      dc == 0, dc == 7,
                                      [dxq, wld[-1]] + psAf))
                        gmm.append(mm(psA[:, 512:512 + wi], lT,
                                      WguT[:, wb + I_FF + io:wb + I_FF + io + wi],
                                      dc == 0, dc == 7,
                                      [dxq, wld[-1]] + psAf))
                    sil = so(lambda s, wi_=wi: s.activation(
                        out=silg[:, 0:wi_], in_=psA[:, 0:wi_],
                        func=AF.Silu), gmm)
                    hmu = vo(lambda v, wi_=wi: v.tensor_tensor(
                        hhf[:, 0:wi_], psA[:, 512:512 + wi_],
                        silg[:, 0:wi_], ALU.mult), [sil] + gmm)
                    hqd.append(emit_quant(wi, hhf, hq, [hmu] + hte_prev,
                                          outoff=io))
                    hte_prev = []
                    psAf = [hmu, sil]
                for w0 in range(0, 22, 8):
                    pieces = []
                    for ic in range(w0, min(w0 + 8, 22)):
                        pieces.append((hq[:, ic * 128:(ic + 1) * 128],
                                       (lambda s, ps, i2=ic: s.activation(
                                           out=hqT[:, i2 * 128:(i2 + 1) * 128],
                                           in_=ps, func=AF.Copy))))
                    hte += twave(pieces, hqd)
                dmm = []
                for ic in range(22):
                    lT = hqT[:, ic * 128:(ic + 1) * 128]
                    for si in range(2):
                        dmm.append(mm(
                            psB[:, si * 512:(si + 1) * 512], lT,
                            WdT[:, ic * D + si * 512:ic * D + (si + 1) * 512],
                            ic == 0, ic == 21, hte + psBf))
                ov = vo(lambda v: v.tensor_tensor(outt[:, :], psB[:, 0:1024],
                                                  x2t[:, :], ALU.add),
                        dmm + [dx2]
                        + ([prev_odma2] if prev_odma2 else []))
                odma = dmas(lambda e, r0=row0: e.dma_start(
                    out=out_sl[r0:r0 + 128, :], in_=outt[:, :]), [ov])
                prev_out = ov
                prev_odma2 = odma
                prev_hte = [hte[-1]]
                psBf = [ov]

            if cutoff is not None:
                S_.items = S_.items[:cutoff]
            if _os.environ.get("MK_NITEMS"):
                print("SCHED items:", len(S_.items))
            S_.resolve()
            with (nc.semaphore("sd") as sem_sd, nc.semaphore("gd") as sem_gd,
                  nc.semaphore("v") as sem_v, nc.semaphore("s") as sem_s,
                  nc.semaphore("t") as sem_t, nc.semaphore("g") as sem_g,
                  nc.Block() as block):
                sems = {"sd": sem_sd, "gd": sem_gd, "vector": sem_v,
                        "scalar": sem_s, "tensor": sem_t, "gpsimd": sem_g}

                @block.sync
                def _(sync):
                    S_.emit("sync", sync, sems)
                    sync.wait_ge(sem_sd, S_.final.get("sd", 0))
                    if S_.final.get("gd"):
                        sync.wait_ge(sem_gd, S_.final["gd"])

                @block.gpsimd
                def _(g):
                    S_.emit("gpsimd", g, sems)

                @block.vector
                def _(v):
                    S_.emit("vector", v, sems)

                @block.scalar
                def _(s):
                    S_.emit("scalar", s, sems)

                @block.tensor
                def _(t):
                    S_.emit("tensor", t, sems)
    return nc


_NC_CACHE = [None]


def kernel(L0_attn_x, L0_attn_W_q, L0_attn_W_k, L0_attn_W_v, L0_attn_W_alpha_w,
           L0_attn_W_alpha_b, L0_attn_W_o, L0_ffn_w_gate, L0_ffn_w_up,
           L0_ffn_w_down):
    x = np.ascontiguousarray(np.asarray(L0_attn_x, np.float32))
    if _NC_CACHE[0] is None:
        _NC_CACHE[0] = build()
    nc = _NC_CACHE[0]
    iden = np.eye(128, dtype=np.float32)
    idenb = np.eye(128, dtype=ml_dtypes.bfloat16)
    maskut = np.where(np.arange(128)[:, None] < np.arange(128)[None, :],
                      np.float32(NEG), np.float32(0.0)).astype(np.float32)
    rhsec = np.zeros((9, 1024), np.float32)
    for h in range(8):
        rhsec[h, h * 128:(h + 1) * 128] = -1.0
    onesc = np.ones((1, 128), np.float32)
    wab2 = np.ascontiguousarray(
        np.asarray(L0_attn_W_alpha_b, np.float32).reshape(1, 512))
    in_maps = []
    for r in range(NCORE):
        adj = np.zeros((128, 2), np.float32)
        for j in range(NCORE):
            if j >= r:
                adj[16 * j:16 * (j + 1), 0] = NEG
            else:
                adj[16 * j:16 * (j + 1), 1] = 1.0
        y0 = np.full((128, 1), 1.0, np.float32)
        if r == 0:
            y0[:] = np.float32(1.0) / np.float32(1.0 + EPS)
        xs = np.ascontiguousarray(
            x[:, r * SCORE:(r + 1) * SCORE, :].reshape(2 * SCORE, D))
        in_maps.append({
            "x_sl": xs,
            "Wq": np.ascontiguousarray(np.asarray(L0_attn_W_q, np.float32)),
            "Wk": np.ascontiguousarray(np.asarray(L0_attn_W_k, np.float32)),
            "Wa": np.ascontiguousarray(
                np.asarray(L0_attn_W_alpha_w, np.float32)),
            "Wab": wab2,
            "Wv": np.ascontiguousarray(np.asarray(L0_attn_W_v, np.float32)),
            "Wo": np.ascontiguousarray(np.asarray(L0_attn_W_o, np.float32)),
            "Wg": np.ascontiguousarray(np.asarray(L0_ffn_w_gate, np.float32)),
            "Wu": np.ascontiguousarray(np.asarray(L0_ffn_w_up, np.float32)),
            "Wd": np.ascontiguousarray(np.asarray(L0_ffn_w_down, np.float32)),
            "idenf": iden, "idenb": idenb, "maskut": maskut,
            "rhsec": rhsec, "onesc": onesc, "adjc": adj, "y0fc": y0,
        })
    res = run_bass_kernel_spmd(nc, in_maps, core_ids=list(range(NCORE)))
    out = np.empty((B, S, D), np.float32)
    for r in range(NCORE):
        o = np.asarray(res.results[r]["out_sl"]).reshape(2, SCORE, D)
        out[:, r * SCORE:(r + 1) * SCORE, :] = o
    return out


# version 15
# speedup vs baseline: 2.2108x; 2.2108x over previous
import numpy as np
import ml_dtypes
from contextlib import ExitStack
import os as _os
import concourse.bass as bass
import concourse.mybir as mybir
from concourse.bass_utils import run_bass_kernel_spmd

F32 = mybir.dt.float32
BF16 = mybir.dt.bfloat16
AF = mybir.ActivationFunctionType
ALU = mybir.AluOpType
AX = mybir.AxisListType

B, S, D, H = 2, 16384, 1024, 8
NCORE = 8
SCORE = S // NCORE
C = 128
NCH = SCORE // C
NCC = 2 * NCH
I_FF = 2816
EPS = 1e-8
C2V = 1.0 + EPS
C1V = EPS * (1.0 + EPS)
SCL_B = 0.125
T2 = [0.0625, 0.5625, 1.5625, 3.0625, 6.25, 12.25, 25.0]
NEG = -1.0e30

DMA_SEM = {"sync": "sd", "gpsimd": "gd"}


class Item:
    __slots__ = ("eng", "fn", "deps", "is_dma", "idx", "sem_val", "needed",
                 "force", "dma_cum")

    def __init__(self, eng, fn, deps, is_dma, idx, force=False):
        self.eng, self.fn, self.is_dma, self.idx = eng, fn, is_dma, idx
        self.deps = [d for d in deps if d is not None]
        self.sem_val = None
        self.needed = force
        self.force = force


class Sched:
    def __init__(self):
        self.items = []
        self.final = {}

    def add(self, eng, fn, deps=(), is_dma=False, force=False):
        it = Item(eng, fn, deps, is_dma, len(self.items), force)
        self.items.append(it)
        return it

    def resolve(self):
        for it in self.items:
            for d in it.deps:
                assert d.idx < it.idx
                if d.eng != it.eng or d.is_dma:
                    d.needed = True
        cnt = {}
        dcum = {"sd": 0, "gd": 0}
        for it in self.items:
            it.dma_cum = dict(dcum)
            if it.is_dma:
                sem = DMA_SEM[it.eng]
                cnt[sem] = cnt.get(sem, 0) + 16
                it.sem_val = cnt[sem]
                dcum[sem] = cnt[sem]
            elif it.needed:
                cnt[it.eng] = cnt.get(it.eng, 0) + 1
                it.sem_val = cnt[it.eng]
        self.final = dict(cnt)

    def emit(self, eng_name, eng, sems):
        if not hasattr(self, "hw"):
            self.hw = {}
        for it in self.items:
            if it.eng != eng_name:
                continue
            waited = {}
            for d in it.deps:
                if d.eng == eng_name and not d.is_dma and not d.force:
                    continue
                if d.is_dma:
                    sem = DMA_SEM[d.eng]
                    val = it.dma_cum[sem]
                else:
                    sem = d.eng
                    val = d.sem_val
                if waited.get(sem, -1) < val:
                    waited[sem] = val
            for sem, val in sorted(waited.items()):
                if self.hw.get((eng_name, sem), -1) >= val:
                    continue
                self.hw[(eng_name, sem)] = val
                eng.wait_ge(sems[sem], val)
            ins = it.fn(eng)
            if it.is_dma:
                ins.then_inc(sems[DMA_SEM[it.eng]], 16)
            elif it.needed:
                ins.then_inc(sems[it.eng], 1)


def build(cutoff=None):
    PHASE = int(_os.environ.get("MK_PHASE", "9"))
    nc = bass.Bass("TRN2", target_bir_lowering=False, debug=False,
                   detect_race_conditions=False)
    dt = nc.dram_tensor
    x_sl = dt("x_sl", [2 * SCORE, D], F32, kind="ExternalInput").ap()
    Wq = dt("Wq", [512, D], F32, kind="ExternalInput").ap()
    Wk = dt("Wk", [512, D], F32, kind="ExternalInput").ap()
    Wa = dt("Wa", [512, D], F32, kind="ExternalInput").ap()
    Wab = dt("Wab", [1, 512], F32, kind="ExternalInput").ap()
    Wv = dt("Wv", [D, D], F32, kind="ExternalInput").ap()
    Wo = dt("Wo", [D, D], F32, kind="ExternalInput").ap()
    Wg = dt("Wg", [I_FF, D], F32, kind="ExternalInput").ap()
    Wu = dt("Wu", [I_FF, D], F32, kind="ExternalInput").ap()
    Wd = dt("Wd", [D, I_FF], F32, kind="ExternalInput").ap()
    idenf = dt("idenf", [128, 128], F32, kind="ExternalInput").ap()
    idenb = dt("idenb", [128, 128], BF16, kind="ExternalInput").ap()
    maskut = dt("maskut", [128, 128], F32, kind="ExternalInput").ap()
    rhsec = dt("rhsec", [9, 1024], F32, kind="ExternalInput").ap()
    onesc = dt("onesc", [1, 128], F32, kind="ExternalInput").ap()
    adjc = dt("adjc", [128, 2], F32, kind="ExternalInput").ap()
    y0fc = dt("y0fc", [128, 1], F32, kind="ExternalInput").ap()
    out_sl = dt("out_sl", [2 * SCORE, D], F32, kind="ExternalOutput").ap()
    _dbg = "ExternalOutput" if _os.environ.get("MK_DBG") else "Internal"
    s_x2 = dt("s_x2", [2 * SCORE, D], F32, kind=_dbg).ap()
    s_vv = dt("s_vv", [NCC, 128, D], BF16, kind=_dbg).ap()
    s_xqT = dt("s_xqT", [NCC, 128, D], BF16, kind=_dbg).ap()
    s_wgu = dt("s_wgu", [8, 128, 2 * I_FF], BF16, kind=_dbg).ap()
    s_wd = dt("s_wd", [22, 128, D], BF16, kind=_dbg).ap()
    if _os.environ.get("MK_DBG"):
        d_xnt = dt("d_xnt", [128, D], BF16, kind="ExternalOutput").ap()
        d_kev = dt("d_kev", [128, 512], F32, kind="ExternalOutput").ap()
        d_bat = dt("d_bat", [128, 16], F32, kind="ExternalOutput").ap()
        d_smu = dt("d_smu", [9, 128], F32, kind="ExternalOutput").ap()
        d_trn = dt("d_trn", [8, 768], F32, kind="ExternalOutput").ap()
        d_st = dt("d_st", [128, 8], F32, kind="ExternalOutput").ap()
        d_smub = dt("d_smub", [9, 2 * SCORE], F32, kind="ExternalOutput").ap()
        d_smbb = dt("d_smbb", [8, 2 * SCORE], F32, kind="ExternalOutput").ap()
        d_vvl = dt("d_vvl", [128, D], BF16, kind="ExternalOutput").ap()
        d_xt = dt("d_xt", [128, D], F32, kind="ExternalOutput").ap()
    cc_in = dt("cc_in", [16, 1032], F32).ap()
    cc_out = dt("cc_out", [128, 1032], F32, addr_space="Shared").ap()

    S_ = Sched()
    es = ExitStack()

    def sb(n, s, d=F32):
        return es.enter_context(nc.sbuf_tensor(n, s, d)).ap()

    with es:
        idf = sb("idf", [128, 128], F32)
        idb = sb("idb", [128, 128], BF16)
        mskt = sb("mskt", [128, 128], F32)
        onet = sb("onet", [1, 128], F32)
        wabt = sb("wabt", [1, 512], F32)
        adjt = sb("adjt", [128, 2], F32)
        y0t = sb("y0t", [128, 1], F32)
        cst = sb("cst", [128, 2], F32)
        xtP = [sb("xt0", [128, D], F32), sb("xt1", [128, D], F32)]
        x2t = sb("x2t", [128, D], F32)
        xt = xtP[0]
        q_u = sb("q_u", [128, 1024], F32)
        q_sq = sb("q_sq", [128, 2048], F32)
        q_z = sb("q_z", [128, 1024], F32)
        q_zc = sb("q_zc", [128, 1024], F32)
        q_sg = sb("q_sg", [128, 1536], F32)
        q_bm = sb("q_bm", [128, 88], F32)
        q_sc = sb("q_sc", [128, 88], F32)
        q_rs = sb("q_rs", [128, 88], F32)
        psA = es.enter_context(nc.psum_tensor("psA", [128, 1536], F32)).ap()
        psB = es.enter_context(nc.psum_tensor("psB", [128, 1536], F32)).ap()
        psT = es.enter_context(nc.psum_tensor("psT", [128, 2048], BF16)).ap()
        pstS = {"par": 0, "lastev": [None, None]}

        def twave(pieces, deps):
            b_ = pstS["par"]
            base = b_ * 1024
            lev = pstS["lastev"][b_]
            dms = []
            for i, (sap, _) in enumerate(pieces):
                dms.append(mm(psT[:, base + i * 128:base + (i + 1) * 128],
                              sap, idb[:, :], True, True,
                              deps + [dc_ib] + ([lev] if lev else []),
                              tr=True))
            evs = []
            for i, (_, efn) in enumerate(pieces):
                evs.append(so(lambda s, i_=i, e_=efn, bb=base: e_(
                    s, psT[:, bb + i_ * 128:bb + (i_ + 1) * 128]),
                    [dms[-1], dms[i]]))
            pstS["lastev"][b_] = evs[-1]
            pstS["par"] ^= 1
            return evs

        def mm(out, lhsT, rhs, start, stop, deps, tr=False):
            return S_.add("tensor", lambda t, o=out, l=lhsT, r=rhs, a=start,
                          b_=stop, c_=tr: t.matmul(o, l, r, start=a, stop=b_,
                                                   is_transpose=c_), deps)

        def vo(fn, deps):
            return S_.add("vector", fn, deps)

        def vof(fn, deps):
            return S_.add("vector", fn, deps, force=True)

        def so(fn, deps):
            return S_.add("scalar", fn, deps)

        def go(fn, deps=()):
            return S_.add("gpsimd", fn, deps)

        def dmas(fn, deps=()):
            return S_.add("sync", fn, deps, is_dma=True)

        def dmag(fn, deps=()):
            return S_.add("gpsimd", fn, deps, is_dma=True)

        def emit_quant(W, src, out, deps, srcoff=0, outoff=0,
                       outguard=(), track=None):
            nb = W // 16
            s_ap = src[:, srcoff:srcoff + W]
            s3 = s_ap.rearrange("p (b k) -> p b k", k=16)
            d0 = vof(lambda v: v.tensor_reduce(q_bm[:, 0:nb], s3, AX.X, ALU.max,
                                              apply_absolute_value=True), deps)
            d1 = vof(lambda v: v.tensor_scalar(q_sc[:, 0:nb], q_bm[:, 0:nb],
                                              1.0 / 6.0, 1e-12, ALU.mult,
                                              ALU.max), [d0])
            d2 = vof(lambda v: v.reciprocal(q_rs[:, 0:nb], q_sc[:, 0:nb]), [d1])
            rs3 = q_rs[:, 0:nb].rearrange("p (b o) -> p b o", o=1).broadcast_to(
                [128, nb, 16])
            u3 = q_u[:, 0:W].rearrange("p (b k) -> p b k", k=16)
            d3 = vo(lambda v: v.tensor_tensor(u3, s3, rs3, ALU.mult),
                    deps + [d2])
            if track is not None:
                track["d3"] = d3
            d4 = so(lambda s: s.activation(out=q_sq[:, 0:W], in_=q_u[:, 0:W],
                                           func=AF.Square), [d3])
            d5 = so(lambda s: s.activation(out=q_sg[:, 0:W], in_=q_u[:, 0:W],
                                           func=AF.Sign), [d3])
            e = vo(lambda v: v.tensor_scalar(q_z[:, 0:W], q_sq[:, 0:W], T2[0],
                                             None, ALU.is_gt), [d4])
            for th in T2[1:4]:
                e = vo(lambda v, t_=th: v.scalar_tensor_tensor(
                    q_z[:, 0:W], q_sq[:, 0:W], t_, q_z[:, 0:W], ALU.is_gt,
                    ALU.add), [e])
            f = vo(lambda v: v.tensor_scalar(q_zc[:, 0:W], q_sq[:, 0:W], T2[4],
                                             None, ALU.is_gt), [d4])
            for th in (T2[5], T2[6], T2[6]):
                f = vo(lambda v, t_=th: v.scalar_tensor_tensor(
                    q_zc[:, 0:W], q_sq[:, 0:W], t_, q_zc[:, 0:W], ALU.is_gt,
                    ALU.add), [f])
            g = vo(lambda v: v.scalar_tensor_tensor(
                q_sq[:, 0:W], q_z[:, 0:W], 0.5, q_zc[:, 0:W], ALU.mult,
                ALU.add), [e, f])
            g2 = vo(lambda v: v.tensor_tensor(q_u[:, 0:W], q_sq[:, 0:W],
                                              q_sg[:, 0:W], ALU.mult),
                    [g, d5])
            sc3 = q_sc[:, 0:nb].rearrange("p (b o) -> p b o", o=1).broadcast_to(
                [128, nb, 16])
            o3 = out[:, outoff:outoff + W].rearrange("p (b k) -> p b k", k=16)
            return vo(lambda v: v.tensor_tensor(o3, u3, sc3, ALU.mult),
                      [g2] + list(outguard))

        def emit_quant_pool(W, src, out, deps):
            nb = W // 16
            s_ap = src[:, 0:W]
            s3 = s_ap.rearrange("p (b k) -> p b k", k=16)
            d0 = vof(lambda v: v.tensor_reduce(q_bm[:, 0:nb], s3, AX.X, ALU.max,
                                              apply_absolute_value=True), deps)
            d1 = vof(lambda v: v.tensor_scalar(q_sc[:, 0:nb], q_bm[:, 0:nb],
                                              1.0 / 6.0, 1e-12, ALU.mult,
                                              ALU.max), [d0])
            d2 = vof(lambda v: v.reciprocal(q_rs[:, 0:nb], q_sc[:, 0:nb]), [d1])
            rs3 = q_rs[:, 0:nb].rearrange("p (b o) -> p b o", o=1).broadcast_to(
                [128, nb, 16])
            u3 = q_u[:, 0:W].rearrange("p (b k) -> p b k", k=16)
            d3 = go(lambda g: g.tensor_tensor(u3, s3, rs3, ALU.mult),
                    deps + [d2])
            d4 = so(lambda s: s.activation(out=q_sq[:, 0:W], in_=q_u[:, 0:W],
                                           func=AF.Square), [d3])
            d5 = so(lambda s: s.activation(out=q_sg[:, 0:W], in_=q_u[:, 0:W],
                                           func=AF.Sign), [d3])
            tmp = src
            e = go(lambda g: g.tensor_scalar(q_z[:, 0:W], q_sq[:, 0:W], T2[0],
                                             None, ALU.is_gt), [d4, d3])
            for th in T2[1:4]:
                t1 = go(lambda g, t_=th: g.tensor_scalar(
                    tmp[:, 0:W], q_sq[:, 0:W], t_, None, ALU.is_gt),
                    [d4, d3, e])
                e = go(lambda g: g.tensor_tensor(q_z[:, 0:W], q_z[:, 0:W],
                                                 tmp[:, 0:W], ALU.add), [t1, e])
            f = go(lambda g: g.tensor_scalar(q_zc[:, 0:W], q_sq[:, 0:W], T2[4],
                                             None, ALU.is_gt), [d4])
            for th in (T2[5], T2[6], T2[6]):
                t1 = go(lambda g, t_=th: g.tensor_scalar(
                    tmp[:, 0:W], q_sq[:, 0:W], t_, None, ALU.is_gt), [f])
                f = go(lambda g: g.tensor_tensor(q_zc[:, 0:W], q_zc[:, 0:W],
                                                 tmp[:, 0:W], ALU.add),
                       [t1, f])
            g_ = go(lambda g: g.tensor_scalar(q_z[:, 0:W], q_z[:, 0:W], 0.5,
                                              None, ALU.mult), [e, f])
            g1 = go(lambda g: g.tensor_tensor(q_sq[:, 0:W], q_z[:, 0:W],
                                              q_zc[:, 0:W], ALU.add), [g_])
            g2 = go(lambda g: g.tensor_tensor(q_u[:, 0:W], q_sq[:, 0:W],
                                              q_sg[:, 0:W], ALU.mult),
                    [g1, d5])
            sc3 = q_sc[:, 0:nb].rearrange("p (b o) -> p b o", o=1).broadcast_to(
                [128, nb, 16])
            o3 = out[:, 0:W].rearrange("p (b k) -> p b k", k=16)
            return go(lambda g: g.tensor_tensor(o3, u3, sc3, ALU.mult), [g2])

        dc_id = dmas(lambda e: e.dma_start(out=idf[:, :], in_=idenf[:, :]))
        dc_ib = dmas(lambda e: e.dma_start(out=idb[:, :], in_=idenb[:, :]))
        dc_mk = dmas(lambda e: e.dma_start(out=mskt[:, :], in_=maskut[:, :]))
        dc_on = dmas(lambda e: e.dma_start(out=onet[:, :], in_=onesc[:, :]))
        dc_wb = dmas(lambda e: e.dma_start(out=wabt[:, :], in_=Wab[:, :]))
        dc_aj = dmas(lambda e: e.dma_start(out=adjt[:, :], in_=adjc[:, :]))
        dc_y0 = dmas(lambda e: e.dma_start(out=y0t[:, :], in_=y0fc[:, :]))

        esA = ExitStack()

        def sbA(n, s, d=F32):
            return esA.enter_context(nc.sbuf_tensor(n, s, d)).ap()

        WT = sbA("WT", [128, 8 * 2560], BF16)
        WoT = sbA("WoT", [128, 8 * D], BF16)
        sm_b = sbA("sm_b", [8, 2 * SCORE], F32)
        sm_u = sbA("sm_u", [9, 2 * SCORE], F32)
        rhseP = [sbA("rhse0", [9, 1024], F32), sbA("rhse1", [9, 1024], F32)]
        trnP = [sbA("trn0", [8, 768], F32), sbA("trn1", [8, 768], F32)]
        rhse = rhseP[0]
        trn = trnP[0]
        lnZt = sbA("lnZt", [8, 2], F32)
        Pst = sbA("Pst", [8, 2 * D], F32)
        Pbf = sbA("Pbf", [8, 2 * D], BF16)
        dAt = sbA("dAt", [8, 2], F32)
        kdbfP = [sbA("kdbf0", [8, 128], BF16), sbA("kdbf1", [8, 128], BF16)]
        kdbf = kdbfP[0]
        ccs = sbA("ccs", [128, 1032], F32)
        ccj = sbA("ccj", [8, 2 * 1032], F32)
        vvtP = [sbA("vvt0", [128, D], BF16), sbA("vvt1", [128, D], BF16)]
        vvt = vvtP[0]
        accRP = [sbA("accR0", [128, 8], F32), sbA("accR1", [128, 8], F32)]
        accR = accRP[0]
        DtokP = [sbA("Dtok0", [128, 8], F32), sbA("Dtok1", [128, 8], F32)]
        Dtok = DtokP[0]
        rDtkP = [sbA("rDtk0", [128, 8], F32), sbA("rDtk1", [128, 8], F32)]
        rDtk = rDtkP[0]
        eKTP = [sbA("eKT0", [128, 8], F32), sbA("eKT1", [128, 8], F32)]
        eKT = eKTP[0]
        wclTP = [sbA("wclT0", [128, 8], BF16), sbA("wclT1", [128, 8], BF16)]
        wclT = wclTP[0]
        pvTP = [sbA("pvT0", [128, 64], F32), sbA("pvT1", [128, 64], F32)]
        pvT = pvTP[0]
        RhP = [sbA("Rh0", [128, 1024], F32), sbA("Rh1", [128, 1024], F32)]
        Rh = RhP[0]
        RTsP = [sbA("RTs0", [128, 128], BF16), sbA("RTs1", [128, 128], BF16)]
        RTs = RTsP[0]
        yTs = sbA("yTs", [128, D], BF16)
        sxP = [sbA("sx0", [128, 1], F32), sbA("sx1", [128, 1], F32)]
        sx = sxP[0]
        sxxP = [sbA("sxx0", [128, 1], F32), sbA("sxx1", [128, 1], F32)]
        sxx = sxxP[0]
        mntP = [sbA("mnt0", [128, 1], F32), sbA("mnt1", [128, 1], F32)]
        mnt = mntP[0]
        vrtP = [sbA("vrt0", [128, 1], F32), sbA("vrt1", [128, 1], F32)]
        vrt = vrtP[0]
        rstdP = [sbA("rstd0", [128, 1], F32), sbA("rstd1", [128, 1], F32)]
        rstd = rstdP[0]
        mbnP = [sbA("mbn0", [128, 1], F32), sbA("mbn1", [128, 1], F32)]
        mbn = mbnP[0]

        wsl = sbA("wsl", [128, 1024], F32)
        wq_o = sbA("wq_o", [128, 1024], BF16)
        wq_p = sbA("wq_p", [128, 1024], BF16)
        xnt = sbA("xnt", [128, D], BF16)
        xnT = sbA("xnT", [128, D], BF16)
        kev = sbA("kev", [128, 512], F32)
        qkt = sbA("qkt", [128, 512], F32)
        spe = sbA("spe", [128, 512], F32)
        batP = [sbA("bat0", [128, 16], F32), sbA("bat1", [128, 16], F32)]
        xqT = sbA("xqT", [128, D], BF16)

        mzc = [go(lambda g: g.memset(cst[:, 0:1], 1e-5)),
               go(lambda g: g.memset(cst[:, 1:2], 1.0))]
        mz = [go(lambda g: g.memset(sm_u[:, :], 1.0)),
              go(lambda g: g.memset(lnZt[:, :], NEG)),
              go(lambda g: g.memset(Pst[:, :], 0.0)),
              go(lambda g: g.memset(dAt[:, :], 0.0))]
        dc_re = dmas(lambda e: e.dma_start(out=rhseP[0][:, :],
                                           in_=rhsec[:, :]))
        dc_re2 = dmas(lambda e: e.dma_start(out=rhseP[1][:, :],
                                            in_=rhsec[:, :]))

        prevw = None
        for (Wsrc, nr, kind) in (() if _os.environ.get("MK_SKIPW") else
                                 ((Wq, 512, 0), (Wk, 512, 1), (Wa, 512, 2),
                                  (Wv, D, 3), (Wo, D, 4))):
            for r in range(nr // 128):
                dld = dmas(lambda e, Ws=Wsrc, rr=r: e.dma_start(
                    out=wsl[:, 0:1024], in_=Ws[rr * 128:(rr + 1) * 128, :]),
                    [prevw])
                wcb = vo(lambda v: v.tensor_copy(wq_o[:, 0:1024],
                                                 wsl[:, 0:1024]), [dld])
                pieces = []
                for dc in range(8):
                    if kind < 4:
                        col = dc * 2560 + kind * 512 + r * 128
                        tgt = WT
                    else:
                        col = dc * D + r * 128
                        tgt = WoT
                    pieces.append((wq_o[:, dc * 128:(dc + 1) * 128],
                                   (lambda s, ps, cl=col, tg=tgt:
                                    s.activation(out=tg[:, cl:cl + 128],
                                                 in_=ps, func=AF.Copy))))
                evs = twave(pieces, [wcb])
                prevw = evs[-1]
        wA_last = prevw if prevw is not None else dc_id

        wqs = {"q": wA_last, "pd": None, "d3": None, "ev": None}
        wq_emits = []

        def _wq_slice(Wsrc, rr, off, w_, dst, colbase):
            dld = dmas(lambda e, Ws=Wsrc: e.dma_start(
                out=wsl[:, 0:w_],
                in_=Ws[rr * 128:(rr + 1) * 128, off:off + w_]),
                [wqs["d3"] if wqs["d3"] is not None else wqs["q"]])
            dq = emit_quant(w_, wsl, wq_o, [dld],
                            outguard=([wqs["ev"]] if wqs["ev"] else []),
                            track=wqs)
            wqs["q"] = dq
            npc = w_ // 128
            evs = []
            for w0 in range(0, npc, 8):
                pieces = []
                for dcl in range(w0, min(w0 + 8, npc)):
                    pieces.append((wq_o[:, dcl * 128:(dcl + 1) * 128],
                                   (lambda s, ps, d2=dcl:
                                    s.activation(
                        out=wq_p[:, d2 * 128:(d2 + 1) * 128],
                        in_=ps, func=AF.Copy))))
                evs += twave(pieces,
                             [dq] + ([wqs["pd"]] if wqs["pd"] else []))
            wqs["ev"] = evs[-1]
            for dcl in range(npc):
                dc = off // 128 + dcl
                wqs["pd"] = dmag(lambda e, ds=dst, dcc=dc, d2=dcl,
                                 cb=colbase: e.dma_start(
                    out=ds[dcc, :, cb + rr * 128:cb + rr * 128 + 128],
                    in_=wq_p[:, d2 * 128:(d2 + 1) * 128]),
                    [evs[dcl]] + ([wqs["pd"]] if wqs["pd"] else []))

        for (Wsrc, nr, wc, dst, colbase) in () if (
                PHASE < 2 or _os.environ.get("MK_SKIPW")) else (
                (Wg, I_FF, D, s_wgu, 0), (Wu, I_FF, D, s_wgu, I_FF),
                (Wd, D, I_FF, s_wd, 0)):
            slices = ([(0, 1024), (1024, 1024), (2048, 768)]
                      if wc == 2816 else [(0, wc)])
            for r in range(nr // 128):
                for (off, w_) in slices:
                    wq_emits.append(
                        (lambda Ws=Wsrc, rr=r, o_=off, ww=w_, ds=dst,
                         cb=colbase: _wq_slice(Ws, rr, o_, ww, ds, cb)))

        wq_i = {"i": 0}

        def wq_pump(frac):
            tgt = min(len(wq_emits), int(frac * len(wq_emits) + 0.999))
            while wq_i["i"] < tgt:
                wq_emits[wq_i["i"]]()
                wq_i["i"] += 1

        psA_free = [wA_last]
        psB_free = []
        vv_st = {}
        chain = list(mz)
        prev_dvv = None
        prev_pmm = []
        prev_ar = []
        prev_xnd = []
        pend0 = [[], []]
        first_l2 = mzc[0]
        A0S = {}
        freeA = list(psA_free)
        freeB01 = []
        freeB2A = []

        def a0_tiles(cc):
            pc = cc % 2
            return (trnP[pc], vvtP[pc], sxP[pc], sxxP[pc], mntP[pc],
                    vrtP[pc], rstdP[pc], mbnP[pc], batP[pc],
                    xtP[pc], wclTP[pc], pvTP[pc])

        def a0_alpha(cc):
            nonlocal prev_dvv, prev_pmm, prev_ar, first_l2, prev_xnd
            bi, c = cc // NCH, cc % NCH
            pc = cc % 2
            (trn, vvt, sx, sxx, mnt, vrt, rstd, mbn, bat,
             xt, wclT, pvT) = a0_tiles(cc)
            pguard = pend0[pc]
            st = {}
            A0S[cc] = st
            row0 = bi * SCORE + c * C
            dx = dmas(lambda e, r0=row0: e.dma_start(
                out=xt[:, :], in_=x_sl[r0:r0 + 128, :]), pguard)
            l1 = vof(lambda v: v.tensor_reduce(sx[:, :], xt[:, :], AX.X,
                                              ALU.add), [dx])
            l2 = so(lambda s: s.activation(out=q_sq[:, 1024:2048],
                                           in_=xt[:, :],
                                           func=AF.Square,
                                           accum_out=sxx[:, :]),
                    [dx, first_l2] + chain[-1:])
            first_l2 = l2
            l3 = vof(lambda v: v.tensor_scalar(mnt[:, :], sx[:, :], 1.0 / D,
                                              None, ALU.mult), [l1])
            l4 = vof(lambda v: v.tensor_scalar(vrt[:, :], sxx[:, :], 1.0 / D,
                                              None, ALU.mult), [l2])
            l5 = vof(lambda v: v.tensor_tensor(sx[:, :], mnt[:, :], mnt[:, :],
                                              ALU.mult), [l3])
            l6 = vof(lambda v: v.tensor_tensor(vrt[:, :], vrt[:, :], sx[:, :],
                                              ALU.subtract), [l4, l5])
            l7 = so(lambda s: s.activation(out=vrt[:, :], in_=vrt[:, :],
                                           func=AF.Ln, bias=cst[:, 0:1]),
                    [l6] + mzc)
            l8 = so(lambda s: s.activation(out=rstd[:, :], in_=vrt[:, :],
                                           func=AF.Exp, scale=-0.5), [l7])
            l9 = vof(lambda v: v.tensor_tensor(mbn[:, :], mnt[:, :],
                                              rstd[:, :], ALU.mult), [l8])
            l10 = vof(lambda v: v.tensor_scalar(mbn[:, :], mbn[:, :], -1.0,
                                               None, ALU.mult), [l9])
            l11 = vo(lambda v: v.tensor_scalar(xnt[:, :], xt[:, :],
                                               rstd[:, :], mbn[:, :],
                                               ALU.mult, ALU.add),
                      [l10, l8] + prev_xnd)
            xnd = twave([(xnt[:, dc * 128:(dc + 1) * 128],
                          (lambda s, ps, d2=dc: s.activation(
                              out=xnT[:, d2 * 128:(d2 + 1) * 128],
                              in_=ps, func=AF.Copy)))
                         for dc in range(8)], [l11] + prev_pmm)
            pb = mm(psA[:, 1024:1536], onet[0:1, :], wabt[0:1, :], True, False,
                    [dc_on, dc_wb] + xnd + freeA)
            pmm = []
            for dc in range(8):
                lT = xnT[:, dc * 128:(dc + 1) * 128]
                bs = dc * 2560
                for si, (oap, wcl) in enumerate((
                        (psA[:, 0:512], bs), (psA[:, 512:1024], bs + 512),
                        (psA[:, 1024:1536], bs + 1024),
                        (psB[:, 0:512], bs + 1536),
                        (psB[:, 512:1024], bs + 2048))):
                    dd = [pb] + xnd + (freeB01 if si >= 3 else [])
                    pmm.append(mm(oap, lT, WT[:, wcl:wcl + 512],
                                  (dc == 0) and si != 2, dc == 7, dd))
            kd_ = so(lambda s: s.activation(out=kev[:, :],
                                            in_=psA[:, 512:1024],
                                            func=AF.Copy), pmm)
            qk = vo(lambda v: v.tensor_tensor(qkt[:, :], psA[:, 0:512],
                                              kev[:, :], ALU.mult),
                    [kd_] + pmm)
            qk3 = qkt[:, :].rearrange("p (h k) -> p h k", k=64)
            br = vof(lambda v: v.tensor_reduce(bat[:, 0:8], qk3, AX.X, ALU.add),
                    [qk])
            sp1 = so(lambda s: s.activation(out=spe[:, :],
                                            in_=psA[:, 1024:1536],
                                            func=AF.Exp), pmm + prev_ar)
            sp2 = so(lambda s: s.activation(out=spe[:, :], in_=spe[:, :],
                                            func=AF.Ln, bias=cst[:, 1:2]),
                     [sp1] + mzc)
            sp3 = spe[:, :].rearrange("p (h k) -> p h k", k=64)
            ar = vof(lambda v: v.tensor_reduce(bat[:, 8:16], sp3, AX.X, ALU.add,
                                              negate=True), [sp2])
            vvd = []
            for (pp, c0, o0) in ((psB, 0, 0), (psB, 512, 512)):
                sg1 = so(lambda s, p_=pp, a_=c0: s.activation(
                    out=q_sg[:, 1024:1536], in_=p_[:, a_:a_ + 512],
                    func=AF.Sign),
                    pmm + [qk, sp1, kd_] + vvd +
                    ([prev_dvv] if prev_dvv else []))
                sg2 = so(lambda s: s.mul(q_sg[:, 1024:1536],
                                         q_sg[:, 1024:1536], C1V), [sg1])
                vvd.append(vo(lambda v, p_=pp, a_=c0, b_=o0:
                              v.scalar_tensor_tensor(
                    vvt[:, b_:b_ + 512], p_[:, a_:a_ + 512], C2V,
                    q_sg[:, 1024:1536], ALU.mult, ALU.add), [sg2]))
            dvv = dmag(lambda e, ccx=cc: e.dma_start(
                out=s_vv[ccx, :, :], in_=vvt[:, :]), vvd)
            if cc == NCC - 1 and _os.environ.get("MK_DBG"):
                S_.add("sync", lambda e: e.dma_start(
                    out=d_vvl[:, :], in_=vvt[:, :]), vvd + [dvv], is_dma=True)
            vv_st[cc] = dvv
            prev_dvv = dvv
            prev_pmm = pmm[-1:]
            prev_ar = [ar]
            prev_xnd = xnd[-1:]
            st.update(l11=l11, pmm=pmm, kd_=kd_, qk=qk, br=br, sp1=sp1,
                      sp2=sp2, ar=ar, vvd=vvd, dvv=dvv)

        def a0_beta(cc):
            nonlocal chain, psA_free, psB_free, freeA, freeB01, freeB2A
            bi, c = cc // NCH, cc % NCH
            pc = cc % 2
            (trn, vvt, sx, sxx, mnt, vrt, rstd, mbn, bat,
             xt, wclT, pvT) = a0_tiles(cc)
            st = A0S[cc]
            l11, pmm, kd_, qk, br, sp1, sp2, ar, vvd, dvv = (
                st["l11"], st["pmm"], st["kd_"], st["qk"], st["br"],
                st["sp1"], st["sp2"], st["ar"], st["vvd"], st["dvv"])
            gc = bi * SCORE + c * C
            uc = slice(gc, gc + C)
            bt = mm(psB[0:8, 1024:1152], bat[:, 0:8], idf[:, :],
                    True, True, [br, dc_id] + freeB2A, tr=True)
            at = mm(psB[0:8, 1152:1280], bat[:, 8:16], idf[:, :],
                    True, True, [ar, dc_id] + freeB2A, tr=True)
            acp = vof(lambda v: v.tensor_copy(
                trn[0:8, 384:512], psB[0:8, 1152:1280]), [at])
            Acs = vof(lambda v: v.tensor_tensor_scan(
                trn[0:8, 0:128], trn[0:8, 384:512],
                trn[0:8, 384:512], 0.0, ALU.add, ALU.bypass), [acp])
            ucv = vof(lambda v: v.scalar_tensor_tensor(
                trn[0:8, 128:256], psB[0:8, 1024:1152], SCL_B,
                trn[0:8, 0:128], ALU.mult, ALU.subtract), [Acs, bt])
            bsc = vof(lambda v: v.tensor_scalar(
                trn[0:8, 512:640], psB[0:8, 1024:1152], SCL_B,
                None, ALU.mult), [bt])
            du = dmag(lambda e, u_=uc: e.dma_start(
                out=sm_u[0:8, u_], in_=trn[0:8, 128:256]),
                [ucv] + mz[0:2])
            db = dmag(lambda e, u_=uc: e.dma_start(
                out=sm_b[0:8, u_], in_=trn[0:8, 512:640]), [bsc])
            nl = vof(lambda v: v.tensor_scalar(
                trn[0:8, 640:641], trn[0:8, 255:256], -1.0, None,
                ALU.mult), [ucv])
            wx = so(lambda s: s.activation(
                out=trn[0:8, 256:384], in_=trn[0:8, 128:256],
                func=AF.Exp, bias=trn[0:8, 640:641]), [nl])
            rz = vof(lambda v: v.tensor_reduce(
                trn[0:8, 641:642], trn[0:8, 256:384], AX.X,
                ALU.add), [wx])
            ek = so(lambda s, b2=bi: s.activation(
                out=trn[0:8, 642:643], in_=lnZt[0:8, b2:b2 + 1],
                func=AF.Exp, bias=trn[0:8, 640:641]), [nl] + chain)
            dl = vof(lambda v: v.tensor_tensor(
                trn[0:8, 643:644], trn[0:8, 641:642],
                trn[0:8, 642:643], ALU.add), [rz, ek])
            rdl = vof(lambda v: v.reciprocal(
                trn[0:8, 644:645], trn[0:8, 643:644]), [dl])
            kd0 = vof(lambda v: v.tensor_tensor(
                trn[0:8, 645:646], trn[0:8, 642:643],
                trn[0:8, 644:645], ALU.mult), [rdl])
            wn = vof(lambda v: v.tensor_scalar(
                trn[0:8, 256:384], trn[0:8, 256:384],
                trn[0:8, 644:645], None, ALU.mult), [rdl, rz])
            wnt = mm(psB[:, 1280:1288], trn[0:8, 256:384], idf[0:8, 0:8],
                     True, True, [wn, dc_id, bsc, db] + freeB2A, tr=True)
            wne = vo(lambda v: v.tensor_copy(wclT[:, :], psB[:, 1280:1288]),
                     [wnt])
            pvm = []
            for dc in range(8):
                pvm.append(mm(psB[:, 1288 + dc * 8:1296 + dc * 8],
                              vvt[:, dc * 128:(dc + 1) * 128], wclT[:, :],
                              True, True, [wne] + vvd + freeB2A))
            pve = vo(lambda v: v.tensor_copy(pvT[:, :], psB[:, 1288:1352]),
                     pvm)
            ptm = []
            for dc in range(8):
                ptm.append(mm(psA[0:8, dc * 128:(dc + 1) * 128],
                              pvT[:, dc * 8:(dc + 1) * 8], idf[:, :],
                              True, True,
                              [pve, dc_id, kd_, qk, sp1] + vvd, tr=True))
            pup = vo(lambda v, b2=bi: v.scalar_tensor_tensor(
                Pst[0:8, b2 * D:(b2 + 1) * D], Pst[0:8, b2 * D:(b2 + 1) * D],
                trn[0:8, 645:646], psA[0:8, 0:1024], ALU.mult,
                ALU.add), ptm + [kd0] + chain)
            lnd = so(lambda s: s.activation(
                out=trn[0:8, 646:647], in_=trn[0:8, 643:644],
                func=AF.Ln), [dl])
            lnu = vo(lambda v, b2=bi: v.tensor_tensor(
                lnZt[0:8, b2:b2 + 1], trn[0:8, 639:640],
                trn[0:8, 646:647], ALU.add), [lnd, bsc, ek] + chain)
            dau = vo(lambda v, b2=bi: v.tensor_tensor(
                dAt[0:8, b2:b2 + 1], dAt[0:8, b2:b2 + 1],
                trn[0:8, 127:128], ALU.add), [Acs, ucv] + chain)
            chain = [pup, lnu, dau]
            freeA = [pup, kd_, qk, sp1]
            freeB01 = list(vvd)
            freeB2A = [Acs, ucv, bsc, wne, pve, acp, pup]
            psA_free = [pup, kd_, qk, sp1]
            psB_free = [Acs, ucv, bsc, wne, pve, acp] + vvd
            pend0[pc] = [pup, lnu, dau, dvv, du, db, pve, wne, l11, qk,
                         kd_, sp2, ar]

        NA0 = 0 if _os.environ.get('MK_NOA0') else (NCC if PHASE >= 3 else 0)
        for cc in range(NA0):
            if cc > 0:
                a0_beta(cc - 1)
            wq_pump((cc + 1.0) / max(NA0, 1))
            a0_alpha(cc)
        if NA0:
            a0_beta(NA0 - 1)
        wq_pump(1.0)
        prevq = wqs["q"]
        wq_last = wqs["pd"] if wqs["pd"] is not None else wA_last

        mzp = go(lambda g: g.memset(trn[0:8, 700:706], 0.0))
        dis = [] if PHASE < 4 else [
            dmag(lambda e: e.dma_start(out=cc_in[0:8, 1026:1032],
                                       in_=trn[0:8, 700:706]), [mzp] + chain),
            dmag(lambda e: e.dma_start(out=cc_in[8:16, 1026:1032],
                                       in_=trn[0:8, 700:706]), [mzp] + chain),
            dmag(lambda e: e.dma_start(out=cc_in[0:8, 0:1],
                                       in_=lnZt[0:8, 0:1],
                                       allow_slow_non_contiguous=True), chain),
            dmag(lambda e: e.dma_start(out=cc_in[8:16, 0:1],
                                       in_=lnZt[0:8, 1:2],
                                       allow_slow_non_contiguous=True), chain),
            dmag(lambda e: e.dma_start(out=cc_in[0:8, 1:2],
                                       in_=dAt[0:8, 0:1],
                                       allow_slow_non_contiguous=True), chain),
            dmag(lambda e: e.dma_start(out=cc_in[8:16, 1:2],
                                       in_=dAt[0:8, 1:2],
                                       allow_slow_non_contiguous=True), chain),
            dmag(lambda e: e.dma_start(out=cc_in[0:8, 2:1026],
                                       in_=Pst[0:8, 0:D]), chain),
            dmag(lambda e: e.dma_start(out=cc_in[8:16, 2:1026],
                                       in_=Pst[0:8, D:2 * D]), chain),
        ]
        if _os.environ.get("MK_NOCOL"):
            col = None if PHASE < 4 else go(
                lambda g: g.memset(ccs[0:1, 0:1], 0.0), dis)
        else:
            col = None if PHASE < 4 else S_.add(
                "gpsimd", lambda g: g.collective_compute(
                    "AllGather", ALU.bypass,
                    replica_groups=[list(range(NCORE))],
                    ins=[cc_in[:, :]], outs=[cc_out[:, :]]), dis, force=True)
        dco = None if PHASE < 4 else dmag(
            lambda e: e.dma_start(out=ccs[:, :], in_=cc_out[:, :]), [col])
        aj1 = None if PHASE < 4 else vof(lambda v: v.tensor_tensor(ccs[:, 0:1], ccs[:, 0:1],
                                           adjt[:, 0:1], ALU.add),
                 [dco, dc_aj])
        aj2 = None if PHASE < 4 else vof(lambda v: v.tensor_tensor(ccs[:, 1:2], ccs[:, 1:2],
                                           adjt[:, 1:2], ALU.mult),
                 [dco, dc_aj])
        rs1 = None if PHASE < 4 else go(lambda g: g.memset(lnZt[:, :], NEG),
                                        chain + dis)
        rs2 = None if PHASE < 4 else go(lambda g: g.memset(Pst[:, :], 0.0),
                                        chain + dis)
        fold = [rs1, rs2, aj1, aj2] if PHASE >= 4 else list(chain)
        for j in range(0 if _os.environ.get('MK_NOF') else (NCORE - 1 if PHASE >= 4 else 0)):
            dj1 = dmag(lambda e, jj=j: e.dma_start(
                out=ccj[0:8, 0:1032], in_=ccs[16 * jj:16 * jj + 8, :]), fold)
            dj2 = dmag(lambda e, jj=j: e.dma_start(
                out=ccj[0:8, 1032:2064],
                in_=ccs[16 * jj + 8:16 * jj + 16, :]), fold)
            nfold = []
            for bi in range(2):
                co = bi * 1032
                f1 = vof(lambda v, c_=co, b2=bi: v.tensor_tensor(
                    trn[0:8, 0:1], lnZt[0:8, b2:b2 + 1],
                    ccj[0:8, c_ + 1:c_ + 2], ALU.add), [dj1, dj2] + fold)
                f2 = vof(lambda v, c_=co: v.tensor_tensor(
                    trn[0:8, 1:2], trn[0:8, 0:1], ccj[0:8, c_:c_ + 1],
                    ALU.max), [f1])
                f3a = vof(lambda v: v.tensor_tensor(
                    trn[0:8, 2:3], trn[0:8, 0:1], trn[0:8, 1:2],
                    ALU.subtract), [f2])
                f3b = vof(lambda v, c_=co: v.tensor_tensor(
                    trn[0:8, 3:4], ccj[0:8, c_:c_ + 1], trn[0:8, 1:2],
                    ALU.subtract), [f2])
                f4 = so(lambda s: s.activation(out=trn[0:8, 4:5],
                                               in_=trn[0:8, 2:3],
                                               func=AF.Exp), [f3a])
                f5 = so(lambda s: s.activation(out=trn[0:8, 5:6],
                                               in_=trn[0:8, 3:4],
                                               func=AF.Exp), [f3b])
                f6 = vof(lambda v: v.tensor_tensor(trn[0:8, 6:7], trn[0:8, 4:5],
                                                  trn[0:8, 5:6], ALU.add),
                        [f4, f5])
                f7 = so(lambda s: s.activation(out=trn[0:8, 7:8],
                                               in_=trn[0:8, 6:7],
                                               func=AF.Ln), [f6])
                f8 = vof(lambda v, b2=bi: v.tensor_tensor(
                    lnZt[0:8, b2:b2 + 1], trn[0:8, 1:2], trn[0:8, 7:8],
                    ALU.add), [f7])
                f9a = vof(lambda v, b2=bi: v.tensor_tensor(
                    trn[0:8, 8:9], trn[0:8, 0:1], lnZt[0:8, b2:b2 + 1],
                    ALU.subtract), [f8])
                f9b = vof(lambda v, c_=co, b2=bi: v.tensor_tensor(
                    trn[0:8, 9:10], ccj[0:8, c_:c_ + 1],
                    lnZt[0:8, b2:b2 + 1], ALU.subtract), [f8])
                fa = so(lambda s: s.activation(out=trn[0:8, 10:11],
                                               in_=trn[0:8, 8:9],
                                               func=AF.Exp), [f9a])
                fb = so(lambda s: s.activation(out=trn[0:8, 11:12],
                                               in_=trn[0:8, 9:10],
                                               func=AF.Exp), [f9b])
                fc = vof(lambda v, c_=co: v.tensor_scalar(
                    ccj[0:8, c_ + 2:c_ + 1026], ccj[0:8, c_ + 2:c_ + 1026],
                    trn[0:8, 11:12], None, ALU.mult), [fb])
                fd = vo(lambda v, c_=co, b2=bi: v.scalar_tensor_tensor(
                    Pst[0:8, b2 * D:(b2 + 1) * D],
                    Pst[0:8, b2 * D:(b2 + 1) * D], trn[0:8, 10:11],
                    ccj[0:8, c_ + 2:c_ + 1026], ALU.mult, ALU.add), [fa, fc])
                nfold += [fd, f8]
            fold = nfold

        a0_tail = (list(chain) + psA_free + psB_free + pend0[0] + pend0[1]
                   + [prevq, wq_last])
        chain = list(fold)
        prev_x2L = list(psB_free)
        prev_x2 = None
        prev_sx2 = None
        prev_xq = None
        pend1 = [list(a0_tail), list(a0_tail)]
        prev_dxq = []
        prev_m2 = []

        freeA01 = list(psA_free)
        freeA2 = list(psA_free)
        freeB2 = list(psB_free)
        prev_pups = list(psB_free)
        prev_yg = []
        A1S = {}

        def a1_tiles(cc):
            pc = cc % 2
            return (trnP[pc], vvtP[pc], accRP[pc], DtokP[pc], rDtkP[pc],
                    eKTP[pc], wclTP[pc], pvTP[pc], RhP[pc], RTsP[pc],
                    sxP[pc], sxxP[pc], mntP[pc], vrtP[pc],
                    rstdP[pc], mbnP[pc], batP[pc], xtP[pc],
                    rhseP[pc], kdbfP[pc])

        def a1_alpha(cc):
            nonlocal chain
            bi, c = cc // NCH, cc % NCH
            pc = cc % 2
            (trn, vvt, accR, Dtok, rDtk, eKT, wclT, pvT, Rh, RTs, sx,
             sxx, mnt, vrt, rstd, mbn, bat,
             xt, rhse, kdbf) = a1_tiles(cc)
            pguard = pend1[pc]
            st = {}
            A1S[cc] = st
            gc = bi * SCORE + c * C
            uc = slice(gc, gc + C)
            dvv = dmag(lambda e, ccx=cc: e.dma_start(
                out=vvt[:, :], in_=s_vv[ccx, :, :]),
                [vv_st[cc]] + pguard)
            duf = dmag(lambda e, u_=uc: e.dma_start(
                out=rhse[8:9, 0:1024], in_=sm_u[0:8, u_]),
                [dc_re, dc_re2] + pguard)
            pbf = vo(lambda v: v.tensor_copy(Pbf[0:8, :], Pst[0:8, :]),
                     chain + prev_yg)
            nl1 = vof(lambda v, u_=uc: v.tensor_scalar(
                trn[0:8, 0:128], sm_u[0:8, u_], -1.0, None,
                ALU.mult), chain)
            emm = []
            for h in range(8):
                emm.append(mm(psA[:, h * 128:(h + 1) * 128],
                              sm_u[0:9, uc],
                              rhse[0:9, h * 128:(h + 1) * 128],
                              h % 4 == 0, False, [duf] + freeA01))
            for h in range(8):
                emm.append(mm(psA[:, h * 128:(h + 1) * 128], idf[:, :],
                              mskt[:, :], False, h % 4 == 3,
                              [dc_id, dc_mk, emm[h]]))
            exd = []
            for h in range(8):
                stopper = emm[8 + (3 if h < 4 else 7)]
                exd.append(so(lambda s, hh=h: s.activation(
                    out=Rh[:, hh * 128:(hh + 1) * 128],
                    in_=psA[:, hh * 128:(hh + 1) * 128], func=AF.Exp,
                    accum_out=accR[:, hh:hh + 1]), [emm[8 + h], stopper]))
            ekh = so(lambda s, b2=bi: s.activation(
                out=trn[0:8, 128:256], in_=trn[0:8, 0:128],
                func=AF.Exp, bias=lnZt[0:8, b2:b2 + 1]), [nl1] + chain)
            ekt = mm(psB[:, 1024:1032], trn[0:8, 128:256], idf[0:8, 0:8],
                     True, True, [ekh, dc_id] + freeB2, tr=True)
            eke = vof(lambda v: v.tensor_copy(eKT[:, :], psB[:, 1024:1032]),
                     [ekt])
            dtk = vof(lambda v: v.tensor_tensor(Dtok[:, :], accR[:, :],
                                               eKT[:, :], ALU.add),
                     exd + [eke])
            rdt = vof(lambda v: v.reciprocal(rDtk[:, :], Dtok[:, :]), [dtk])
            nrm = []
            for h in range(8):
                nrm.append(vo(lambda v, hh=h: v.tensor_scalar(
                    Rh[:, hh * 128:(hh + 1) * 128],
                    Rh[:, hh * 128:(hh + 1) * 128], rDtk[:, hh:hh + 1], None,
                    ALU.mult), [rdt, exd[h]]))
            dtt = mm(psB[0:8, 1032:1160], Dtok[:, :], idf[:, :],
                     True, True, [dtk, dc_id, rdt] + freeB2, tr=True)
            kdt = vof(lambda v: v.tensor_tensor(eKT[:, :], eKT[:, :],
                                               rDtk[:, :], ALU.mult),
                     [rdt, dtk])
            rdl = vof(lambda v: v.reciprocal(
                trn[0:8, 648:649], psB[0:8, 1159:1160]), [dtt])
            lnd = so(lambda s: s.activation(
                out=trn[0:8, 650:651], in_=psB[0:8, 1159:1160],
                func=AF.Ln), [dtt])
            kdm = mm(psB[0:8, 1160:1288], eKT[:, :], idf[:, :],
                     True, True, [kdt, dc_id, rdl, lnd] + freeB2, tr=True)
            kde = vo(lambda v: v.tensor_copy(
                kdbf[0:8, :], psB[0:8, 1160:1288]), [kdm])
            kdl = vof(lambda v: v.tensor_copy(
                trn[0:8, 647:648], psB[0:8, 1287:1288]), [kdm])
            rtm = []
            for h in range(8):
                rtm.append(mm(psA[:, 1024:1152],
                              Rh[:, h * 128:(h + 1) * 128], idf[:, :],
                              h == 0, h == 7,
                              [nrm[h], dc_id] + freeA2, tr=True))
            rte = so(lambda s: s.activation(out=RTs[:, :],
                                            in_=psA[:, 1024:1152],
                                            func=AF.Copy), rtm)
            lnu = vo(lambda v, u_=uc, b2=bi: v.tensor_tensor(
                lnZt[0:8, b2:b2 + 1],
                sm_b[0:8, u_.start + 127:u_.start + 128],
                trn[0:8, 650:651], ALU.add), [lnd, ekh] + chain)
            st.update(dvv=dvv, exd=exd, nrm=nrm, rte=rte, kde=kde, kdl=kdl,
                      rdl=rdl, lnd=lnd, eke=eke, dtk=dtk, kdt=kdt, lnu=lnu,
                      pbf=pbf, ekh=ekh)

        def a1_beta(cc):
            nonlocal chain, prev_sx2, prev_xq, prev_dxq, freeA01, freeA2, \
                freeB2, prev_pups, prev_yg, prev_m2
            bi, c = cc // NCH, cc % NCH
            pc = cc % 2
            (trn, vvt, accR, Dtok, rDtk, eKT, wclT, pvT, Rh, RTs, sx,
             sxx, mnt, vrt, rstd, mbn, bat,
             xt, rhse, kdbf) = a1_tiles(cc)
            pguard = pend1[pc]
            st = A1S[cc]
            dvv, exd, rte, kde, kdl, rdl, lnd, lnu, pbf = (
                st["dvv"], st["exd"], st["rte"], st["kde"], st["kdl"],
                st["rdl"], st["lnd"], st["lnu"], st["pbf"])
            row0 = bi * SCORE + c * C
            gc = bi * SCORE + c * C
            uc = slice(gc, gc + C)
            ymm = []
            yg = []
            for dc in range(8):
                m1 = mm(psB[:, dc * 128:(dc + 1) * 128],
                        Pbf[0:8, bi * D + dc * 128:bi * D + (dc + 1) * 128],
                        kdbf[0:8, :], dc % 4 == 0, False,
                        [pbf, kde, dvv] + prev_pups)
                m2_ = mm(psB[:, dc * 128:(dc + 1) * 128],
                         vvt[:, dc * 128:(dc + 1) * 128], RTs[:, :],
                         False, dc % 4 == 3, [m1, rte, dvv])
                ymm += [m1, m2_]
                yg.append(m2_)
            yev = []
            for dc in range(8):
                stopper = yg[3 if dc < 4 else 7]
                yev.append(so(lambda s, dcc=dc: s.activation(
                    out=yTs[:, dcc * 128:(dcc + 1) * 128],
                    in_=psB[:, dcc * 128:(dcc + 1) * 128], func=AF.Copy),
                    [yg[dc], stopper]))
            if c == 0:
                for dc in range(8):
                    yev.append(vo(lambda v, dcc=dc: v.tensor_scalar(
                        yTs[:, dcc * 128:dcc * 128 + 1],
                        yTs[:, dcc * 128:dcc * 128 + 1], y0t[:, 0:1], None,
                        ALU.mult), [yev[dc], dc_y0]))
            nl2 = vof(lambda v, u_=uc: v.tensor_scalar(
                trn[0:8, 649:650],
                sm_u[0:8, u_.start + 127:u_.start + 128],
                -1.0, None, ALU.mult), chain)
            wx1 = so(lambda s, u_=uc: s.activation(
                out=trn[0:8, 256:384], in_=sm_u[0:8, u_],
                func=AF.Exp, bias=trn[0:8, 649:650]), [nl2])
            wn1 = vof(lambda v: v.tensor_scalar(
                trn[0:8, 256:384], trn[0:8, 256:384],
                trn[0:8, 648:649], None, ALU.mult), [wx1, rdl])
            wt1 = mm(psB[:, 1280:1288], trn[0:8, 256:384], idf[0:8, 0:8],
                     True, True, [wn1, dc_id, lnd, kde, kdl] + freeB2,
                     tr=True)
            we1 = vo(lambda v: v.tensor_copy(wclT[:, :], psB[:, 1280:1288]),
                     [wt1])
            pvm = []
            for dc in range(8):
                pvm.append(mm(psB[:, 1288 + dc * 8:1296 + dc * 8],
                              vvt[:, dc * 128:(dc + 1) * 128], wclT[:, :],
                              True, True, [we1, dvv]))
            pve = vo(lambda v: v.tensor_copy(pvT[:, :], psB[:, 1288:1352]),
                     pvm)
            ptm2 = []
            for dc in range(8):
                ptm2.append(mm(psB[0:8, dc * 128:(dc + 1) * 128],
                               pvT[:, dc * 8:(dc + 1) * 8], idf[:, :],
                               True, True,
                               [pve, dc_id] + yev[0:8], tr=True))
            pups = [vo(lambda v, b2=bi: v.scalar_tensor_tensor(
                Pst[0:8, b2 * D:(b2 + 1) * D], Pst[0:8, b2 * D:(b2 + 1) * D],
                trn[0:8, 647:648], psB[0:8, 0:1024], ALU.mult, ALU.add),
                ptm2 + [pbf, kdl] + chain)]
            wmm = []
            for dc in range(8):
                lT = yTs[:, dc * 128:(dc + 1) * 128]
                for si in range(2):
                    wmm.append(mm(
                        psA[:, si * 512:(si + 1) * 512], lT,
                        WoT[:, dc * D + si * 512:dc * D + (si + 1) * 512],
                        dc == 0, dc == 7, yev + exd))
            dx2 = dmas(lambda e, r0=row0: e.dma_start(
                out=xt[:, :], in_=x_sl[r0:r0 + 128, :]), pguard)
            x2v = vo(lambda v: v.tensor_tensor(x2t[:, :], psA[:, 0:1024],
                                               xt[:, :], ALU.add),
                     wmm + [dx2] + prev_m2
                     + ([prev_sx2] if prev_sx2 else []))
            sx2 = dmag(lambda e, r0=row0: e.dma_start(
                out=s_x2[r0:r0 + 128, :], in_=x2t[:, :]), [x2v])
            prev_sx2 = sx2
            m1 = vof(lambda v: v.tensor_reduce(sx[:, :], x2t[:, :], AX.X,
                                              ALU.add), [x2v])
            m2 = so(lambda s: s.activation(out=q_sq[:, 1024:2048],
                                           in_=x2t[:, :],
                                           func=AF.Square,
                                           accum_out=sxx[:, :]),
                    [x2v] + ([prev_xq] if prev_xq else []))
            m3 = vof(lambda v: v.tensor_scalar(mnt[:, :], sx[:, :], 1.0 / D,
                                              None, ALU.mult), [m1])
            m4 = vof(lambda v: v.tensor_scalar(vrt[:, :], sxx[:, :], 1.0 / D,
                                              None, ALU.mult), [m2])
            m5 = vof(lambda v: v.tensor_tensor(sx[:, :], mnt[:, :], mnt[:, :],
                                              ALU.mult), [m3])
            m6 = vof(lambda v: v.tensor_tensor(vrt[:, :], vrt[:, :], sx[:, :],
                                              ALU.subtract), [m4, m5])
            m7 = so(lambda s: s.activation(out=vrt[:, :], in_=vrt[:, :],
                                           func=AF.Ln, bias=cst[:, 0:1]),
                    [m6] + mzc)
            m8 = so(lambda s: s.activation(out=rstd[:, :], in_=vrt[:, :],
                                           func=AF.Exp, scale=-0.5), [m7])
            m9 = vof(lambda v: v.tensor_tensor(mbn[:, :], mnt[:, :], rstd[:, :],
                                              ALU.mult), [m8])
            m10 = vof(lambda v: v.tensor_scalar(mbn[:, :], mbn[:, :], -1.0,
                                               None, ALU.mult), [m9])
            m11 = vo(lambda v: v.tensor_scalar(xt[:, :], x2t[:, :],
                                               rstd[:, :], mbn[:, :],
                                               ALU.mult, ALU.add),
                     [m10, m8, sx2])
            xq = emit_quant(1024, xt, xnt,
                            ([m11, m2] + a0_tail) if cc == 0
                            else ([m11, m2] + prev_dxq))
            prev_xq = xq
            xqd = twave([(xnt[:, dc * 128:(dc + 1) * 128],
                          (lambda s, ps, d2=dc: s.activation(
                              out=xqT[:, d2 * 128:(d2 + 1) * 128],
                              in_=ps, func=AF.Copy)))
                         for dc in range(8)],
                        [xq] + pguard + prev_dxq)
            dxq = dmag(lambda e, ccx=cc: e.dma_start(
                out=s_xqT[ccx, :, :], in_=xqT[:, :]), xqd)
            prev_dxq = [dxq]
            prev_m2 = [m2]
            chain = [lnu] + pups[-1:]
            freeA01 = [x2v]
            freeA2 = [rte]
            freeB2 = [st["eke"], kde, kdl, rdl, lnd, we1, pve]
            prev_pups = list(pups)
            prev_yg = yg[-1:]
            pend1[pc] = ([x2v, sx2, dxq, xq, pve, lnu, kde, kdl, rdl, lnd,
                          st["eke"], we1, rte] + pups + yev[0:8] + xqd
                         + pvm[-1:] + [st["dtk"], st["kdt"], st["nrm"][-1]])

        NA1 = NCC if PHASE >= 5 else 0
        for cc in range(NA1):
            if cc > 0:
                a1_beta(cc - 1)
            a1_alpha(cc)
        if NA1:
            a1_beta(NA1 - 1)

        if _os.environ.get("MK_DBG"):
            S_.add("sync", lambda e: e.dma_start(
                out=d_smub[:, :], in_=sm_u[:, :]), chain, is_dma=True)
            S_.add("sync", lambda e: e.dma_start(
                out=d_smbb[:, :], in_=sm_b[:, :]), chain, is_dma=True)
        prev_odma = None
        aend = [x for x in (pend1[0] + pend1[1]
                            + [prev_sx2, prev_dvv] + chain
                            + psA_free + psB_free + [prev_xq])
                if x is not None]
        esA.close()

        esB = ExitStack()

        def sbB(n, s, d=F32):
            return esB.enter_context(nc.sbuf_tensor(n, s, d)).ap()

        with esB:
            WguT = sbB("WguT", [128, 8 * 2 * I_FF], BF16)
            WdT = sbB("WdT", [128, 22 * D], BF16)
            hq = sbB("hq", [128, I_FF], BF16)
            hqT = sbB("hqT", [128, I_FF], BF16)
            silg = sbB("silg", [128, 704], F32)
            hhf = sbB("hhf", [128, 704], F32)
            outt = sbB("outt", [128, D], F32)
            xqL = sbB("xqL", [128, D], BF16)

            wld = []
            for dc in range(8 if PHASE >= 6 else 0):
                wld.append(dmas(lambda e, dcc=dc: e.dma_start(
                    out=WguT[:, dcc * 2 * I_FF:(dcc + 1) * 2 * I_FF],
                    in_=s_wgu[dcc, :, :]), [wq_last] + aend))
            for ic in range(22 if PHASE >= 6 else 0):
                wld.append(dmas(lambda e, icc=ic: e.dma_start(
                    out=WdT[:, icc * D:(icc + 1) * D], in_=s_wd[icc, :, :]),
                    [wq_last] + aend))

            prev_out = None
            prev_odma2 = None
            prev_hte = []
            psAf = aend
            psBf = aend
            for cc in range(NCC if PHASE >= 6 else 0):
                bi, c = cc // NCH, cc % NCH
                row0 = bi * SCORE + c * C
                dxq = dmas(lambda e, ccx=cc: e.dma_start(
                    out=xqL[:, :], in_=s_xqT[ccx, :, :]),
                    ([prev_out] if prev_out else aend))
                dx2 = dmas(lambda e, r0=row0: e.dma_start(
                    out=x2t[:, :], in_=s_x2[r0:r0 + 128, :]),
                    ([prev_out] if prev_out else aend))
                hqd = []
                hte = []
                hte_prev = list(prev_hte)
                isl = [(0, 512), (512, 512), (1024, 512), (1536, 512),
                       (2048, 512), (2560, 256)]
                for (io, wi) in isl:
                    gmm = []
                    for dc in range(8):
                        lT = xqL[:, dc * 128:(dc + 1) * 128]
                        wb = dc * 2 * I_FF
                        gmm.append(mm(psA[:, 0:wi], lT,
                                      WguT[:, wb + io:wb + io + wi],
                                      dc == 0, dc == 7,
                                      [dxq, wld[-1]] + psAf))
                        gmm.append(mm(psA[:, 512:512 + wi], lT,
                                      WguT[:, wb + I_FF + io:wb + I_FF + io + wi],
                                      dc == 0, dc == 7,
                                      [dxq, wld[-1]] + psAf))
                    sil = so(lambda s, wi_=wi: s.activation(
                        out=silg[:, 0:wi_], in_=psA[:, 0:wi_],
                        func=AF.Silu), gmm)
                    hmu = vo(lambda v, wi_=wi: v.tensor_tensor(
                        hhf[:, 0:wi_], psA[:, 512:512 + wi_],
                        silg[:, 0:wi_], ALU.mult), [sil] + gmm)
                    hqd.append(emit_quant(wi, hhf, hq, [hmu] + hte_prev,
                                          outoff=io))
                    hte_prev = []
                    psAf = [hmu, sil]
                for w0 in range(0, 22, 8):
                    pieces = []
                    for ic in range(w0, min(w0 + 8, 22)):
                        pieces.append((hq[:, ic * 128:(ic + 1) * 128],
                                       (lambda s, ps, i2=ic: s.activation(
                                           out=hqT[:, i2 * 128:(i2 + 1) * 128],
                                           in_=ps, func=AF.Copy))))
                    hte += twave(pieces, hqd)
                dmm = []
                for ic in range(22):
                    lT = hqT[:, ic * 128:(ic + 1) * 128]
                    for si in range(2):
                        dmm.append(mm(
                            psB[:, si * 512:(si + 1) * 512], lT,
                            WdT[:, ic * D + si * 512:ic * D + (si + 1) * 512],
                            ic == 0, ic == 21, hte + psBf))
                ov = vo(lambda v: v.tensor_tensor(outt[:, :], psB[:, 0:1024],
                                                  x2t[:, :], ALU.add),
                        dmm + [dx2]
                        + ([prev_odma2] if prev_odma2 else []))
                odma = dmas(lambda e, r0=row0: e.dma_start(
                    out=out_sl[r0:r0 + 128, :], in_=outt[:, :]), [ov])
                prev_out = ov
                prev_odma2 = odma
                prev_hte = [hte[-1]]
                psBf = [ov]

            if cutoff is not None:
                S_.items = S_.items[:cutoff]
            if _os.environ.get("MK_NITEMS"):
                print("SCHED items:", len(S_.items))
            S_.resolve()
            with (nc.semaphore("sd") as sem_sd, nc.semaphore("gd") as sem_gd,
                  nc.semaphore("v") as sem_v, nc.semaphore("s") as sem_s,
                  nc.semaphore("t") as sem_t, nc.semaphore("g") as sem_g,
                  nc.Block() as block):
                sems = {"sd": sem_sd, "gd": sem_gd, "vector": sem_v,
                        "scalar": sem_s, "tensor": sem_t, "gpsimd": sem_g}

                @block.sync
                def _(sync):
                    S_.emit("sync", sync, sems)
                    sync.wait_ge(sem_sd, S_.final.get("sd", 0))
                    if S_.final.get("gd"):
                        sync.wait_ge(sem_gd, S_.final["gd"])

                @block.gpsimd
                def _(g):
                    S_.emit("gpsimd", g, sems)

                @block.vector
                def _(v):
                    S_.emit("vector", v, sems)

                @block.scalar
                def _(s):
                    S_.emit("scalar", s, sems)

                @block.tensor
                def _(t):
                    S_.emit("tensor", t, sems)
    return nc


_NC_CACHE = [None]


def kernel(L0_attn_x, L0_attn_W_q, L0_attn_W_k, L0_attn_W_v, L0_attn_W_alpha_w,
           L0_attn_W_alpha_b, L0_attn_W_o, L0_ffn_w_gate, L0_ffn_w_up,
           L0_ffn_w_down):
    x = np.ascontiguousarray(np.asarray(L0_attn_x, np.float32))
    if _NC_CACHE[0] is None:
        _NC_CACHE[0] = build()
    nc = _NC_CACHE[0]
    iden = np.eye(128, dtype=np.float32)
    idenb = np.eye(128, dtype=ml_dtypes.bfloat16)
    maskut = np.where(np.arange(128)[:, None] < np.arange(128)[None, :],
                      np.float32(NEG), np.float32(0.0)).astype(np.float32)
    rhsec = np.zeros((9, 1024), np.float32)
    for h in range(8):
        rhsec[h, h * 128:(h + 1) * 128] = -1.0
    onesc = np.ones((1, 128), np.float32)
    wab2 = np.ascontiguousarray(
        np.asarray(L0_attn_W_alpha_b, np.float32).reshape(1, 512))
    in_maps = []
    for r in range(NCORE):
        adj = np.zeros((128, 2), np.float32)
        for j in range(NCORE):
            if j >= r:
                adj[16 * j:16 * (j + 1), 0] = NEG
            else:
                adj[16 * j:16 * (j + 1), 1] = 1.0
        y0 = np.full((128, 1), 1.0, np.float32)
        if r == 0:
            y0[:] = np.float32(1.0) / np.float32(1.0 + EPS)
        xs = np.ascontiguousarray(
            x[:, r * SCORE:(r + 1) * SCORE, :].reshape(2 * SCORE, D))
        in_maps.append({
            "x_sl": xs,
            "Wq": np.ascontiguousarray(np.asarray(L0_attn_W_q, np.float32)),
            "Wk": np.ascontiguousarray(np.asarray(L0_attn_W_k, np.float32)),
            "Wa": np.ascontiguousarray(
                np.asarray(L0_attn_W_alpha_w, np.float32)),
            "Wab": wab2,
            "Wv": np.ascontiguousarray(np.asarray(L0_attn_W_v, np.float32)),
            "Wo": np.ascontiguousarray(np.asarray(L0_attn_W_o, np.float32)),
            "Wg": np.ascontiguousarray(np.asarray(L0_ffn_w_gate, np.float32)),
            "Wu": np.ascontiguousarray(np.asarray(L0_ffn_w_up, np.float32)),
            "Wd": np.ascontiguousarray(np.asarray(L0_ffn_w_down, np.float32)),
            "idenf": iden, "idenb": idenb, "maskut": maskut,
            "rhsec": rhsec, "onesc": onesc, "adjc": adj, "y0fc": y0,
        })
    res = run_bass_kernel_spmd(nc, in_maps, core_ids=list(range(NCORE)))
    out = np.empty((B, S, D), np.float32)
    for r in range(NCORE):
        o = np.asarray(res.results[r]["out_sl"]).reshape(2, SCORE, D)
        out[:, r * SCORE:(r + 1) * SCORE, :] = o
    return out


# version 18
# speedup vs baseline: 2.3166x; 1.0478x over previous
import numpy as np
import ml_dtypes
from contextlib import ExitStack
import os as _os
import concourse.bass as bass
import concourse.mybir as mybir
from concourse.bass_utils import run_bass_kernel_spmd

F32 = mybir.dt.float32
BF16 = mybir.dt.bfloat16
AF = mybir.ActivationFunctionType
ALU = mybir.AluOpType
AX = mybir.AxisListType

B, S, D, H = 2, 16384, 1024, 8
NCORE = 8
SCORE = S // NCORE
C = 128
NCH = SCORE // C
NCC = 2 * NCH
I_FF = 2816
EPS = 1e-8
C2V = 1.0 + EPS
C1V = EPS * (1.0 + EPS)
SCL_B = 0.125
T2 = [0.0625, 0.5625, 1.5625, 3.0625, 6.25, 12.25, 25.0]
NEG = -1.0e30

DMA_SEM = {"sync": "sd", "gpsimd": "gd"}


class Item:
    __slots__ = ("eng", "fn", "deps", "is_dma", "idx", "sem_val", "needed",
                 "force", "dma_cum")

    def __init__(self, eng, fn, deps, is_dma, idx, force=False):
        self.eng, self.fn, self.is_dma, self.idx = eng, fn, is_dma, idx
        self.deps = [d for d in deps if d is not None]
        self.sem_val = None
        self.needed = force
        self.force = force


class Sched:
    def __init__(self):
        self.items = []
        self.final = {}

    def add(self, eng, fn, deps=(), is_dma=False, force=False):
        it = Item(eng, fn, deps, is_dma, len(self.items), force)
        self.items.append(it)
        return it

    def resolve(self):
        for it in self.items:
            for d in it.deps:
                assert d.idx < it.idx
                if d.eng != it.eng or d.is_dma:
                    d.needed = True
        cnt = {}
        dcum = {"sd": 0, "gd": 0}
        for it in self.items:
            it.dma_cum = dict(dcum)
            if it.is_dma:
                sem = DMA_SEM[it.eng]
                cnt[sem] = cnt.get(sem, 0) + 16
                it.sem_val = cnt[sem]
                dcum[sem] = cnt[sem]
            elif it.needed:
                cnt[it.eng] = cnt.get(it.eng, 0) + 1
                it.sem_val = cnt[it.eng]
        self.final = dict(cnt)

    def emit(self, eng_name, eng, sems):
        if not hasattr(self, "hw"):
            self.hw = {}
        for it in self.items:
            if it.eng != eng_name:
                continue
            waited = {}
            for d in it.deps:
                if d.eng == eng_name and not d.is_dma and not d.force:
                    continue
                if d.is_dma:
                    sem = DMA_SEM[d.eng]
                    val = it.dma_cum[sem]
                else:
                    sem = d.eng
                    val = d.sem_val
                if waited.get(sem, -1) < val:
                    waited[sem] = val
            for sem, val in sorted(waited.items()):
                if self.hw.get((eng_name, sem), -1) >= val:
                    continue
                self.hw[(eng_name, sem)] = val
                eng.wait_ge(sems[sem], val)
            ins = it.fn(eng)
            if it.is_dma:
                ins.then_inc(sems[DMA_SEM[it.eng]], 16)
            elif it.needed:
                ins.then_inc(sems[it.eng], 1)


def build(cutoff=None):
    PHASE = int(_os.environ.get("MK_PHASE", "9"))
    nc = bass.Bass("TRN2", target_bir_lowering=False, debug=False,
                   detect_race_conditions=False)
    dt = nc.dram_tensor
    x_sl = dt("x_sl", [2 * SCORE, D], F32, kind="ExternalInput").ap()
    Wq = dt("Wq", [512, D], F32, kind="ExternalInput").ap()
    Wk = dt("Wk", [512, D], F32, kind="ExternalInput").ap()
    Wa = dt("Wa", [512, D], F32, kind="ExternalInput").ap()
    Wab = dt("Wab", [1, 512], F32, kind="ExternalInput").ap()
    Wv = dt("Wv", [D, D], F32, kind="ExternalInput").ap()
    Wo = dt("Wo", [D, D], F32, kind="ExternalInput").ap()
    Wg = dt("Wg", [I_FF, D], F32, kind="ExternalInput").ap()
    Wu = dt("Wu", [I_FF, D], F32, kind="ExternalInput").ap()
    Wd = dt("Wd", [D, I_FF], F32, kind="ExternalInput").ap()
    idenf = dt("idenf", [128, 128], F32, kind="ExternalInput").ap()
    idenb = dt("idenb", [128, 128], BF16, kind="ExternalInput").ap()
    maskut = dt("maskut", [128, 128], F32, kind="ExternalInput").ap()
    rhsec = dt("rhsec", [9, 1024], F32, kind="ExternalInput").ap()
    onesc = dt("onesc", [1, 128], F32, kind="ExternalInput").ap()
    adjc = dt("adjc", [128, 2], F32, kind="ExternalInput").ap()
    y0fc = dt("y0fc", [128, 1], F32, kind="ExternalInput").ap()
    out_sl = dt("out_sl", [2 * SCORE, D], F32, kind="ExternalOutput").ap()
    _dbg = "ExternalOutput" if _os.environ.get("MK_DBG") else "Internal"
    s_x2 = dt("s_x2", [2 * SCORE, D], F32, kind=_dbg).ap()
    s_vv = dt("s_vv", [NCC, 128, D], BF16, kind=_dbg).ap()
    s_xqT = dt("s_xqT", [NCC, 128, D], BF16, kind=_dbg).ap()
    s_wgu = dt("s_wgu", [8, 128, 2 * I_FF], BF16, kind=_dbg).ap()
    s_wd = dt("s_wd", [22, 128, D], BF16, kind=_dbg).ap()
    if _os.environ.get("MK_DBG"):
        d_xnt = dt("d_xnt", [128, D], BF16, kind="ExternalOutput").ap()
        d_kev = dt("d_kev", [128, 512], F32, kind="ExternalOutput").ap()
        d_bat = dt("d_bat", [128, 16], F32, kind="ExternalOutput").ap()
        d_smu = dt("d_smu", [9, 128], F32, kind="ExternalOutput").ap()
        d_trn = dt("d_trn", [8, 768], F32, kind="ExternalOutput").ap()
        d_st = dt("d_st", [128, 8], F32, kind="ExternalOutput").ap()
        d_smub = dt("d_smub", [9, 2 * SCORE], F32, kind="ExternalOutput").ap()
        d_smbb = dt("d_smbb", [8, 2 * SCORE], F32, kind="ExternalOutput").ap()
        d_vvl = dt("d_vvl", [128, D], BF16, kind="ExternalOutput").ap()
        d_xt = dt("d_xt", [128, D], F32, kind="ExternalOutput").ap()
    cc_in = dt("cc_in", [16, 1032], F32).ap()
    cc_out = dt("cc_out", [128, 1032], F32, addr_space="Shared").ap()

    S_ = Sched()
    es = ExitStack()

    def sb(n, s, d=F32):
        return es.enter_context(nc.sbuf_tensor(n, s, d)).ap()

    with es:
        idf = sb("idf", [128, 128], F32)
        idb = sb("idb", [128, 128], BF16)
        mskt = sb("mskt", [128, 128], F32)
        onet = sb("onet", [1, 128], F32)
        wabt = sb("wabt", [1, 512], F32)
        adjt = sb("adjt", [128, 2], F32)
        y0t = sb("y0t", [128, 1], F32)
        cst = sb("cst", [128, 2], F32)
        xtP = [sb("xt0", [128, D], F32), sb("xt1", [128, D], F32)]
        x2t = sb("x2t", [128, D], F32)
        xt = xtP[0]
        q_u = sb("q_u", [128, 1024], F32)
        q_sq = sb("q_sq", [128, 2048], F32)
        q_z = sb("q_z", [128, 1024], F32)
        q_zc = sb("q_zc", [128, 1024], F32)
        q_sg = sb("q_sg", [128, 1536], F32)
        q_bm = sb("q_bm", [128, 88], F32)
        q_sc = sb("q_sc", [128, 88], F32)
        q_rs = sb("q_rs", [128, 88], F32)
        psA = es.enter_context(nc.psum_tensor("psA", [128, 1536], F32)).ap()
        psB = es.enter_context(nc.psum_tensor("psB", [128, 1536], F32)).ap()
        psT = es.enter_context(nc.psum_tensor("psT", [128, 2048], BF16)).ap()
        pstS = {"par": 0, "lastev": [None, None]}

        def twave(pieces, deps):
            b_ = pstS["par"]
            base = b_ * 1024
            lev = pstS["lastev"][b_]
            dms = []
            for i, (sap, _) in enumerate(pieces):
                dms.append(mm(psT[:, base + i * 128:base + (i + 1) * 128],
                              sap, idb[:, :], True, True,
                              deps + [dc_ib] + ([lev] if lev else []),
                              tr=True))
            evs = []
            for i, (_, efn) in enumerate(pieces):
                evs.append(so(lambda s, i_=i, e_=efn, bb=base: e_(
                    s, psT[:, bb + i_ * 128:bb + (i_ + 1) * 128]),
                    [dms[-1], dms[i]]))
            pstS["lastev"][b_] = evs[-1]
            pstS["par"] ^= 1
            return evs

        def mm(out, lhsT, rhs, start, stop, deps, tr=False):
            return S_.add("tensor", lambda t, o=out, l=lhsT, r=rhs, a=start,
                          b_=stop, c_=tr: t.matmul(o, l, r, start=a, stop=b_,
                                                   is_transpose=c_), deps)

        def vo(fn, deps):
            return S_.add("vector", fn, deps)

        def vof(fn, deps):
            return S_.add("vector", fn, deps, force=True)

        def so(fn, deps):
            return S_.add("scalar", fn, deps)

        def go(fn, deps=()):
            return S_.add("gpsimd", fn, deps)

        def dmas(fn, deps=()):
            return S_.add("sync", fn, deps, is_dma=True)

        def dmag(fn, deps=()):
            return S_.add("gpsimd", fn, deps, is_dma=True)

        def emit_quant(W, src, out, deps, srcoff=0, outoff=0,
                       outguard=(), track=None):
            nb = W // 16
            s_ap = src[:, srcoff:srcoff + W]
            s3 = s_ap.rearrange("p (b k) -> p b k", k=16)
            d0 = vof(lambda v: v.tensor_reduce(q_bm[:, 0:nb], s3, AX.X, ALU.max,
                                              apply_absolute_value=True), deps)
            d1 = vof(lambda v: v.tensor_scalar(q_sc[:, 0:nb], q_bm[:, 0:nb],
                                              1.0 / 6.0, 1e-12, ALU.mult,
                                              ALU.max), [d0])
            d2 = vof(lambda v: v.reciprocal(q_rs[:, 0:nb], q_sc[:, 0:nb]), [d1])
            rs3 = q_rs[:, 0:nb].rearrange("p (b o) -> p b o", o=1).broadcast_to(
                [128, nb, 16])
            u3 = q_u[:, 0:W].rearrange("p (b k) -> p b k", k=16)
            d3 = vo(lambda v: v.tensor_tensor(u3, s3, rs3, ALU.mult),
                    deps + [d2])
            if track is not None:
                track["d3"] = d3
            d4 = so(lambda s: s.activation(out=q_sq[:, 0:W], in_=q_u[:, 0:W],
                                           func=AF.Square), [d3])
            d5 = so(lambda s: s.activation(out=q_sg[:, 0:W], in_=q_u[:, 0:W],
                                           func=AF.Sign), [d3])
            e = vo(lambda v: v.tensor_scalar(q_z[:, 0:W], q_sq[:, 0:W], T2[0],
                                             None, ALU.is_gt), [d4])
            for th in T2[1:4]:
                e = vo(lambda v, t_=th: v.scalar_tensor_tensor(
                    q_z[:, 0:W], q_sq[:, 0:W], t_, q_z[:, 0:W], ALU.is_gt,
                    ALU.add), [e])
            f = vo(lambda v: v.tensor_scalar(q_zc[:, 0:W], q_sq[:, 0:W], T2[4],
                                             None, ALU.is_gt), [d4])
            for th in (T2[5], T2[6], T2[6]):
                f = vo(lambda v, t_=th: v.scalar_tensor_tensor(
                    q_zc[:, 0:W], q_sq[:, 0:W], t_, q_zc[:, 0:W], ALU.is_gt,
                    ALU.add), [f])
            g = vo(lambda v: v.scalar_tensor_tensor(
                q_sq[:, 0:W], q_z[:, 0:W], 0.5, q_zc[:, 0:W], ALU.mult,
                ALU.add), [e, f])
            g2 = vo(lambda v: v.tensor_tensor(q_u[:, 0:W], q_sq[:, 0:W],
                                              q_sg[:, 0:W], ALU.mult),
                    [g, d5])
            sc3 = q_sc[:, 0:nb].rearrange("p (b o) -> p b o", o=1).broadcast_to(
                [128, nb, 16])
            o3 = out[:, outoff:outoff + W].rearrange("p (b k) -> p b k", k=16)
            return vo(lambda v: v.tensor_tensor(o3, u3, sc3, ALU.mult),
                      [g2] + list(outguard))

        def emit_quant_pool(W, src, out, deps):
            nb = W // 16
            s_ap = src[:, 0:W]
            s3 = s_ap.rearrange("p (b k) -> p b k", k=16)
            d0 = vof(lambda v: v.tensor_reduce(q_bm[:, 0:nb], s3, AX.X, ALU.max,
                                              apply_absolute_value=True), deps)
            d1 = vof(lambda v: v.tensor_scalar(q_sc[:, 0:nb], q_bm[:, 0:nb],
                                              1.0 / 6.0, 1e-12, ALU.mult,
                                              ALU.max), [d0])
            d2 = vof(lambda v: v.reciprocal(q_rs[:, 0:nb], q_sc[:, 0:nb]), [d1])
            rs3 = q_rs[:, 0:nb].rearrange("p (b o) -> p b o", o=1).broadcast_to(
                [128, nb, 16])
            u3 = q_u[:, 0:W].rearrange("p (b k) -> p b k", k=16)
            d3 = go(lambda g: g.tensor_tensor(u3, s3, rs3, ALU.mult),
                    deps + [d2])
            d4 = so(lambda s: s.activation(out=q_sq[:, 0:W], in_=q_u[:, 0:W],
                                           func=AF.Square), [d3])
            d5 = so(lambda s: s.activation(out=q_sg[:, 0:W], in_=q_u[:, 0:W],
                                           func=AF.Sign), [d3])
            tmp = src
            e = go(lambda g: g.tensor_scalar(q_z[:, 0:W], q_sq[:, 0:W], T2[0],
                                             None, ALU.is_gt), [d4, d3])
            for th in T2[1:4]:
                t1 = go(lambda g, t_=th: g.tensor_scalar(
                    tmp[:, 0:W], q_sq[:, 0:W], t_, None, ALU.is_gt),
                    [d4, d3, e])
                e = go(lambda g: g.tensor_tensor(q_z[:, 0:W], q_z[:, 0:W],
                                                 tmp[:, 0:W], ALU.add), [t1, e])
            f = go(lambda g: g.tensor_scalar(q_zc[:, 0:W], q_sq[:, 0:W], T2[4],
                                             None, ALU.is_gt), [d4])
            for th in (T2[5], T2[6], T2[6]):
                t1 = go(lambda g, t_=th: g.tensor_scalar(
                    tmp[:, 0:W], q_sq[:, 0:W], t_, None, ALU.is_gt), [f])
                f = go(lambda g: g.tensor_tensor(q_zc[:, 0:W], q_zc[:, 0:W],
                                                 tmp[:, 0:W], ALU.add),
                       [t1, f])
            g_ = go(lambda g: g.tensor_scalar(q_z[:, 0:W], q_z[:, 0:W], 0.5,
                                              None, ALU.mult), [e, f])
            g1 = go(lambda g: g.tensor_tensor(q_sq[:, 0:W], q_z[:, 0:W],
                                              q_zc[:, 0:W], ALU.add), [g_])
            g2 = go(lambda g: g.tensor_tensor(q_u[:, 0:W], q_sq[:, 0:W],
                                              q_sg[:, 0:W], ALU.mult),
                    [g1, d5])
            sc3 = q_sc[:, 0:nb].rearrange("p (b o) -> p b o", o=1).broadcast_to(
                [128, nb, 16])
            o3 = out[:, 0:W].rearrange("p (b k) -> p b k", k=16)
            return go(lambda g: g.tensor_tensor(o3, u3, sc3, ALU.mult), [g2])

        dc_id = dmas(lambda e: e.dma_start(out=idf[:, :], in_=idenf[:, :]))
        dc_ib = dmas(lambda e: e.dma_start(out=idb[:, :], in_=idenb[:, :]))
        dc_mk = dmas(lambda e: e.dma_start(out=mskt[:, :], in_=maskut[:, :]))
        dc_on = dmas(lambda e: e.dma_start(out=onet[:, :], in_=onesc[:, :]))
        dc_wb = dmas(lambda e: e.dma_start(out=wabt[:, :], in_=Wab[:, :]))
        dc_aj = dmas(lambda e: e.dma_start(out=adjt[:, :], in_=adjc[:, :]))
        dc_y0 = dmas(lambda e: e.dma_start(out=y0t[:, :], in_=y0fc[:, :]))

        esA = ExitStack()

        def sbA(n, s, d=F32):
            return esA.enter_context(nc.sbuf_tensor(n, s, d)).ap()

        WT = sbA("WT", [128, 8 * 2560], BF16)
        WoT = sbA("WoT", [128, 8 * D], BF16)
        sm_b = sbA("sm_b", [8, 2 * SCORE], F32)
        sm_u = sbA("sm_u", [9, 2 * SCORE], F32)
        rhse = sbA("rhse", [9, 1024], F32)
        trnP = [sbA("trn0", [8, 768], F32), sbA("trn1", [8, 768], F32)]
        trn = trnP[0]
        lnZt = sbA("lnZt", [8, 2], F32)
        Pst = sbA("Pst", [8, 2 * D], F32)
        Pbf = sbA("Pbf", [8, 2 * D], BF16)
        dAt = sbA("dAt", [8, 2], F32)
        kdbfP = [sbA("kdbf0", [8, 128], BF16), sbA("kdbf1", [8, 128], BF16)]
        kdbf = kdbfP[0]
        ccs = sbA("ccs", [128, 1032], F32)
        ccj = sbA("ccj", [8, 2 * 1032], F32)
        vvtP = [sbA("vvt0", [128, D], BF16), sbA("vvt1", [128, D], BF16)]
        vvt = vvtP[0]
        accRP = [sbA("accR0", [128, 8], F32), sbA("accR1", [128, 8], F32)]
        accR = accRP[0]
        DtokP = [sbA("Dtok0", [128, 8], F32), sbA("Dtok1", [128, 8], F32)]
        Dtok = DtokP[0]
        rDtkP = [sbA("rDtk0", [128, 8], F32), sbA("rDtk1", [128, 8], F32)]
        rDtk = rDtkP[0]
        eKTP = [sbA("eKT0", [128, 8], F32), sbA("eKT1", [128, 8], F32)]
        eKT = eKTP[0]
        wclTP = [sbA("wclT0", [128, 8], BF16), sbA("wclT1", [128, 8], BF16)]
        wclT = wclTP[0]
        pvTP = [sbA("pvT0", [128, 64], F32), sbA("pvT1", [128, 64], F32)]
        pvT = pvTP[0]
        RhP = [sbA("Rh0", [128, 1024], F32), sbA("Rh1", [128, 1024], F32)]
        Rh = RhP[0]
        RTsP = [sbA("RTs0", [128, 128], BF16), sbA("RTs1", [128, 128], BF16)]
        RTs = RTsP[0]
        yTs = sbA("yTs", [128, D], BF16)
        sxP = [sbA("sx0", [128, 1], F32), sbA("sx1", [128, 1], F32)]
        sx = sxP[0]
        sxxP = [sbA("sxx0", [128, 1], F32), sbA("sxx1", [128, 1], F32)]
        sxx = sxxP[0]
        mntP = [sbA("mnt0", [128, 1], F32), sbA("mnt1", [128, 1], F32)]
        mnt = mntP[0]
        vrtP = [sbA("vrt0", [128, 1], F32), sbA("vrt1", [128, 1], F32)]
        vrt = vrtP[0]
        rstdP = [sbA("rstd0", [128, 1], F32), sbA("rstd1", [128, 1], F32)]
        rstd = rstdP[0]
        mbnP = [sbA("mbn0", [128, 1], F32), sbA("mbn1", [128, 1], F32)]
        mbn = mbnP[0]

        wsl = sbA("wsl", [128, 1024], F32)
        wq_oP = [sbA("wq_o0", [128, 1024], BF16),
                 sbA("wq_o1", [128, 1024], BF16)]
        wq_o = wq_oP[0]
        wq_p = sbA("wq_p", [128, 1024], BF16)
        xnt = sbA("xnt", [128, D], BF16)
        xnT = sbA("xnT", [128, D], BF16)
        kev = sbA("kev", [128, 512], F32)
        qkt = sbA("qkt", [128, 512], F32)
        spe = sbA("spe", [128, 512], F32)
        batP = [sbA("bat0", [128, 16], F32), sbA("bat1", [128, 16], F32)]
        xqT = sbA("xqT", [128, D], BF16)

        mzc = [go(lambda g: g.memset(cst[:, 0:1], 1e-5)),
               go(lambda g: g.memset(cst[:, 1:2], 1.0))]
        mz = [go(lambda g: g.memset(sm_u[:, :], 1.0)),
              go(lambda g: g.memset(lnZt[:, :], NEG)),
              go(lambda g: g.memset(Pst[:, :], 0.0)),
              go(lambda g: g.memset(dAt[:, :], 0.0))]
        dc_re = dmas(lambda e: e.dma_start(out=rhse[:, :],
                                           in_=rhsec[:, :]))

        prevw = None
        for (Wsrc, nr, kind) in (() if _os.environ.get("MK_SKIPW") else
                                 ((Wq, 512, 0), (Wk, 512, 1), (Wa, 512, 2),
                                  (Wv, D, 3), (Wo, D, 4))):
            for r in range(nr // 128):
                dld = dmas(lambda e, Ws=Wsrc, rr=r: e.dma_start(
                    out=wsl[:, 0:1024], in_=Ws[rr * 128:(rr + 1) * 128, :]),
                    [prevw])
                wcb = vo(lambda v: v.tensor_copy(wq_o[:, 0:1024],
                                                 wsl[:, 0:1024]), [dld])
                pieces = []
                for dc in range(8):
                    if kind < 4:
                        col = dc * 2560 + kind * 512 + r * 128
                        tgt = WT
                    else:
                        col = dc * D + r * 128
                        tgt = WoT
                    pieces.append((wq_o[:, dc * 128:(dc + 1) * 128],
                                   (lambda s, ps, cl=col, tg=tgt:
                                    s.activation(out=tg[:, cl:cl + 128],
                                                 in_=ps, func=AF.Copy))))
                evs = twave(pieces, [wcb])
                prevw = evs[-1]
        wA_last = prevw if prevw is not None else dc_id

        wqs = {"q": wA_last, "pd": None, "d3": None, "ev": None, "dq": {}}
        wq_emits = []

        def _wq_front(sl, Wsrc, rr, off, w_):
            wo = wq_oP[sl % 2]
            dld = dmas(lambda e, Ws=Wsrc: e.dma_start(
                out=wsl[:, 0:w_],
                in_=Ws[rr * 128:(rr + 1) * 128, off:off + w_]),
                [wqs["d3"] if wqs["d3"] is not None else wqs["q"]])
            dq = emit_quant(w_, wsl, wo, [dld],
                            outguard=([wqs["ev"]] if wqs["ev"] else []),
                            track=wqs)
            wqs["q"] = dq
            wqs["dq"][sl] = dq

        def _wq_back(sl, Wsrc, rr, off, w_, dst, colbase):
            wo = wq_oP[sl % 2]
            dq = wqs["dq"][sl]
            npc = w_ // 128
            evs = []
            for w0 in range(0, npc, 8):
                pieces = []
                for dcl in range(w0, min(w0 + 8, npc)):
                    pieces.append((wo[:, dcl * 128:(dcl + 1) * 128],
                                   (lambda s, ps, d2=dcl:
                                    s.activation(
                        out=wq_p[:, d2 * 128:(d2 + 1) * 128],
                        in_=ps, func=AF.Copy))))
                evs += twave(pieces,
                             [dq] + ([wqs["pd"]] if wqs["pd"] else []))
            wqs["ev"] = evs[-1]
            for dcl in range(npc):
                dc = off // 128 + dcl
                wqs["pd"] = dmag(lambda e, ds=dst, dcc=dc, d2=dcl,
                                 cb=colbase: e.dma_start(
                    out=ds[dcc, :, cb + rr * 128:cb + rr * 128 + 128],
                    in_=wq_p[:, d2 * 128:(d2 + 1) * 128]),
                    [evs[dcl]] + ([wqs["pd"]] if wqs["pd"] else []))

        for (Wsrc, nr, wc, dst, colbase) in () if (
                PHASE < 2 or _os.environ.get("MK_SKIPW")) else (
                (Wg, I_FF, D, s_wgu, 0), (Wu, I_FF, D, s_wgu, I_FF),
                (Wd, D, I_FF, s_wd, 0)):
            slices = ([(0, 1024), (1024, 1024), (2048, 768)]
                      if wc == 2816 else [(0, wc)])
            for r in range(nr // 128):
                for (off, w_) in slices:
                    wq_emits.append((Wsrc, r, off, w_, dst, colbase))

        wq_i = {"f": 0, "b": 0}

        def wq_pump(frac):
            tgt = min(len(wq_emits), int(frac * len(wq_emits) + 0.999))
            while wq_i["f"] < tgt:
                sl = wq_i["f"]
                (Ws, r, off, w_, ds, cb) = wq_emits[sl]
                _wq_front(sl, Ws, r, off, w_)
                wq_i["f"] += 1
                if wq_i["f"] >= 2 and wq_i["b"] < wq_i["f"] - 1:
                    sb_ = wq_i["b"]
                    (Ws, r, off, w_, ds, cb) = wq_emits[sb_]
                    _wq_back(sb_, Ws, r, off, w_, ds, cb)
                    wq_i["b"] += 1
            if frac >= 1.0:
                while wq_i["b"] < wq_i["f"]:
                    sb_ = wq_i["b"]
                    (Ws, r, off, w_, ds, cb) = wq_emits[sb_]
                    _wq_back(sb_, Ws, r, off, w_, ds, cb)
                    wq_i["b"] += 1

        psA_free = [wA_last]
        psB_free = []
        vv_st = {}
        chain = list(mz)
        chainA = {0: list(mz), 1: list(mz)}
        prev_dvv = None
        prev_pmm = []
        prev_ar = []
        prev_xnd = []
        pend0 = [[], []]
        first_l2 = mzc[0]
        A0S = {}
        freeA = list(psA_free)
        freeB01 = []
        freeB2A = []

        def a0_tiles(cc):
            pc = cc % 2
            return (trnP[pc], vvtP[pc], sxP[pc], sxxP[pc], mntP[pc],
                    vrtP[pc], rstdP[pc], mbnP[pc], batP[pc],
                    xtP[pc], wclTP[pc], pvTP[pc])

        def a0_alpha(cc, pc):
            nonlocal prev_dvv, prev_pmm, prev_ar, first_l2, prev_xnd
            bi, c = cc // NCH, cc % NCH
            chain = chainA[bi]
            (trn, vvt, sx, sxx, mnt, vrt, rstd, mbn, bat,
             xt, wclT, pvT) = a0_tiles(cc)
            pguard = pend0[pc]
            st = {}
            A0S[cc] = st
            row0 = bi * SCORE + c * C
            dx = dmas(lambda e, r0=row0: e.dma_start(
                out=xt[:, :], in_=x_sl[r0:r0 + 128, :]), pguard)
            l1 = vof(lambda v: v.tensor_reduce(sx[:, :], xt[:, :], AX.X,
                                              ALU.add), [dx])
            l2 = so(lambda s: s.activation(out=q_sq[:, 1024:2048],
                                           in_=xt[:, :],
                                           func=AF.Square,
                                           accum_out=sxx[:, :]),
                    [dx, first_l2] + chain[-1:])
            first_l2 = l2
            l3 = vof(lambda v: v.tensor_scalar(mnt[:, :], sx[:, :], 1.0 / D,
                                              None, ALU.mult), [l1])
            l4 = vof(lambda v: v.tensor_scalar(vrt[:, :], sxx[:, :], 1.0 / D,
                                              None, ALU.mult), [l2])
            l5 = vof(lambda v: v.tensor_tensor(sx[:, :], mnt[:, :], mnt[:, :],
                                              ALU.mult), [l3])
            l6 = vof(lambda v: v.tensor_tensor(vrt[:, :], vrt[:, :], sx[:, :],
                                              ALU.subtract), [l4, l5])
            l7 = so(lambda s: s.activation(out=vrt[:, :], in_=vrt[:, :],
                                           func=AF.Ln, bias=cst[:, 0:1]),
                    [l6] + mzc)
            l8 = so(lambda s: s.activation(out=rstd[:, :], in_=vrt[:, :],
                                           func=AF.Exp, scale=-0.5), [l7])
            l9 = vof(lambda v: v.tensor_tensor(mbn[:, :], mnt[:, :],
                                              rstd[:, :], ALU.mult), [l8])
            l10 = vof(lambda v: v.tensor_scalar(mbn[:, :], mbn[:, :], -1.0,
                                               None, ALU.mult), [l9])
            l11 = vo(lambda v: v.tensor_scalar(xnt[:, :], xt[:, :],
                                               rstd[:, :], mbn[:, :],
                                               ALU.mult, ALU.add),
                      [l10, l8] + prev_xnd)
            xnd = twave([(xnt[:, dc * 128:(dc + 1) * 128],
                          (lambda s, ps, d2=dc: s.activation(
                              out=xnT[:, d2 * 128:(d2 + 1) * 128],
                              in_=ps, func=AF.Copy)))
                         for dc in range(8)], [l11] + prev_pmm)
            pb = mm(psA[:, 1024:1536], onet[0:1, :], wabt[0:1, :], True, False,
                    [dc_on, dc_wb] + xnd + freeA)
            pmm = []
            for dc in range(8):
                lT = xnT[:, dc * 128:(dc + 1) * 128]
                bs = dc * 2560
                for si, (oap, wcl) in enumerate((
                        (psA[:, 0:512], bs), (psA[:, 512:1024], bs + 512),
                        (psA[:, 1024:1536], bs + 1024),
                        (psB[:, 0:512], bs + 1536),
                        (psB[:, 512:1024], bs + 2048))):
                    dd = [pb] + xnd + (freeB01 if si >= 3 else [])
                    pmm.append(mm(oap, lT, WT[:, wcl:wcl + 512],
                                  (dc == 0) and si != 2, dc == 7, dd))
            kd_ = so(lambda s: s.activation(out=kev[:, :],
                                            in_=psA[:, 512:1024],
                                            func=AF.Copy), pmm)
            qk = vo(lambda v: v.tensor_tensor(qkt[:, :], psA[:, 0:512],
                                              kev[:, :], ALU.mult),
                    [kd_] + pmm)
            qk3 = qkt[:, :].rearrange("p (h k) -> p h k", k=64)
            br = vof(lambda v: v.tensor_reduce(bat[:, 0:8], qk3, AX.X, ALU.add),
                    [qk])
            sp1 = so(lambda s: s.activation(out=spe[:, :],
                                            in_=psA[:, 1024:1536],
                                            func=AF.Exp), pmm + prev_ar)
            sp2 = so(lambda s: s.activation(out=spe[:, :], in_=spe[:, :],
                                            func=AF.Ln, bias=cst[:, 1:2]),
                     [sp1] + mzc)
            sp3 = spe[:, :].rearrange("p (h k) -> p h k", k=64)
            ar = vof(lambda v: v.tensor_reduce(bat[:, 8:16], sp3, AX.X, ALU.add,
                                              negate=True), [sp2])
            vvd = []
            for (pp, c0, o0) in ((psB, 0, 0), (psB, 512, 512)):
                sg1 = so(lambda s, p_=pp, a_=c0: s.activation(
                    out=q_sg[:, 1024:1536], in_=p_[:, a_:a_ + 512],
                    func=AF.Sign),
                    pmm + [qk, sp1, kd_] + vvd +
                    ([prev_dvv] if prev_dvv else []))
                sg2 = so(lambda s: s.mul(q_sg[:, 1024:1536],
                                         q_sg[:, 1024:1536], C1V), [sg1])
                vvd.append(vo(lambda v, p_=pp, a_=c0, b_=o0:
                              v.scalar_tensor_tensor(
                    vvt[:, b_:b_ + 512], p_[:, a_:a_ + 512], C2V,
                    q_sg[:, 1024:1536], ALU.mult, ALU.add), [sg2]))
            dvv = dmag(lambda e, ccx=cc: e.dma_start(
                out=s_vv[ccx, :, :], in_=vvt[:, :]), vvd)
            if cc == NCC - 1 and _os.environ.get("MK_DBG"):
                S_.add("sync", lambda e: e.dma_start(
                    out=d_vvl[:, :], in_=vvt[:, :]), vvd + [dvv], is_dma=True)
            vv_st[cc] = dvv
            prev_dvv = dvv
            prev_pmm = pmm[-1:]
            prev_ar = [ar]
            prev_xnd = xnd[-1:]
            st.update(l11=l11, pmm=pmm, kd_=kd_, qk=qk, br=br, sp1=sp1,
                      sp2=sp2, ar=ar, vvd=vvd, dvv=dvv)

        def a0_beta(cc, pc):
            nonlocal psA_free, psB_free, freeA, freeB01, freeB2A
            bi, c = cc // NCH, cc % NCH
            chain = chainA[bi]
            (trn, vvt, sx, sxx, mnt, vrt, rstd, mbn, bat,
             xt, wclT, pvT) = a0_tiles(cc)
            st = A0S[cc]
            l11, pmm, kd_, qk, br, sp1, sp2, ar, vvd, dvv = (
                st["l11"], st["pmm"], st["kd_"], st["qk"], st["br"],
                st["sp1"], st["sp2"], st["ar"], st["vvd"], st["dvv"])
            gc = bi * SCORE + c * C
            uc = slice(gc, gc + C)
            bt = mm(psB[0:8, 1024:1152], bat[:, 0:8], idf[:, :],
                    True, True, [br, dc_id] + freeB2A, tr=True)
            at = mm(psB[0:8, 1152:1280], bat[:, 8:16], idf[:, :],
                    True, True, [ar, dc_id] + freeB2A, tr=True)
            acp = vof(lambda v: v.tensor_copy(
                trn[0:8, 384:512], psB[0:8, 1152:1280]), [at])
            Acs = vof(lambda v: v.tensor_tensor_scan(
                trn[0:8, 0:128], trn[0:8, 384:512],
                trn[0:8, 384:512], 0.0, ALU.add, ALU.bypass), [acp])
            ucv = vof(lambda v: v.scalar_tensor_tensor(
                trn[0:8, 128:256], psB[0:8, 1024:1152], SCL_B,
                trn[0:8, 0:128], ALU.mult, ALU.subtract), [Acs, bt])
            bsc = vof(lambda v: v.tensor_scalar(
                trn[0:8, 512:640], psB[0:8, 1024:1152], SCL_B,
                None, ALU.mult), [bt])
            du = dmag(lambda e, u_=uc: e.dma_start(
                out=sm_u[0:8, u_], in_=trn[0:8, 128:256]),
                [ucv] + mz[0:2])
            db = dmag(lambda e, u_=uc: e.dma_start(
                out=sm_b[0:8, u_], in_=trn[0:8, 512:640]), [bsc])
            nl = vof(lambda v: v.tensor_scalar(
                trn[0:8, 640:641], trn[0:8, 255:256], -1.0, None,
                ALU.mult), [ucv])
            wx = so(lambda s: s.activation(
                out=trn[0:8, 256:384], in_=trn[0:8, 128:256],
                func=AF.Exp, bias=trn[0:8, 640:641]), [nl])
            rz = vof(lambda v: v.tensor_reduce(
                trn[0:8, 641:642], trn[0:8, 256:384], AX.X,
                ALU.add), [wx])
            ek = so(lambda s, b2=bi: s.activation(
                out=trn[0:8, 642:643], in_=lnZt[0:8, b2:b2 + 1],
                func=AF.Exp, bias=trn[0:8, 640:641]), [nl] + chain)
            dl = vof(lambda v: v.tensor_tensor(
                trn[0:8, 643:644], trn[0:8, 641:642],
                trn[0:8, 642:643], ALU.add), [rz, ek])
            rdl = vof(lambda v: v.reciprocal(
                trn[0:8, 644:645], trn[0:8, 643:644]), [dl])
            kd0 = vof(lambda v: v.tensor_tensor(
                trn[0:8, 645:646], trn[0:8, 642:643],
                trn[0:8, 644:645], ALU.mult), [rdl])
            wn = vof(lambda v: v.tensor_scalar(
                trn[0:8, 256:384], trn[0:8, 256:384],
                trn[0:8, 644:645], None, ALU.mult), [rdl, rz])
            wnt = mm(psB[:, 1280:1288], trn[0:8, 256:384], idf[0:8, 0:8],
                     True, True, [wn, dc_id, bsc, db] + freeB2A, tr=True)
            wne = vo(lambda v: v.tensor_copy(wclT[:, :], psB[:, 1280:1288]),
                     [wnt])
            pvm = []
            for dc in range(8):
                pvm.append(mm(psB[:, 1288 + dc * 8:1296 + dc * 8],
                              vvt[:, dc * 128:(dc + 1) * 128], wclT[:, :],
                              True, True, [wne] + vvd + freeB2A))
            pve = vo(lambda v: v.tensor_copy(pvT[:, :], psB[:, 1288:1352]),
                     pvm)
            ptm = []
            for dc in range(8):
                ptm.append(mm(psA[0:8, dc * 128:(dc + 1) * 128],
                              pvT[:, dc * 8:(dc + 1) * 8], idf[:, :],
                              True, True,
                              [pve, dc_id, kd_, qk, sp1] + vvd, tr=True))
            pup = vo(lambda v, b2=bi: v.scalar_tensor_tensor(
                Pst[0:8, b2 * D:(b2 + 1) * D], Pst[0:8, b2 * D:(b2 + 1) * D],
                trn[0:8, 645:646], psA[0:8, 0:1024], ALU.mult,
                ALU.add), ptm + [kd0] + chain)
            lnd = so(lambda s: s.activation(
                out=trn[0:8, 646:647], in_=trn[0:8, 643:644],
                func=AF.Ln), [dl])
            lnu = vo(lambda v, b2=bi: v.tensor_tensor(
                lnZt[0:8, b2:b2 + 1], trn[0:8, 639:640],
                trn[0:8, 646:647], ALU.add), [lnd, bsc, ek] + chain)
            dau = vo(lambda v, b2=bi: v.tensor_tensor(
                dAt[0:8, b2:b2 + 1], dAt[0:8, b2:b2 + 1],
                trn[0:8, 127:128], ALU.add), [Acs, ucv] + chain)
            chainA[bi] = [pup, lnu, dau]
            freeA = [pup, kd_, qk, sp1]
            freeB01 = list(vvd)
            freeB2A = [Acs, ucv, bsc, wne, pve, acp, pup]
            psA_free = [pup, kd_, qk, sp1]
            psB_free = [Acs, ucv, bsc, wne, pve, acp] + vvd
            pend0[pc] = [pup, lnu, dau, dvv, du, db, pve, wne, l11, qk,
                         kd_, sp2, ar]

        NA0 = 0 if _os.environ.get('MK_NOA0') else (NCC if PHASE >= 3 else 0)
        seq0 = [bi * NCH + c for c in range(NCH) for bi in (0, 1)][:NA0]
        for k, cc in enumerate(seq0):
            if k > 0:
                a0_beta(seq0[k - 1], (k - 1) % 2)
            wq_pump((k + 1.0) / max(NA0, 1))
            a0_alpha(cc, k % 2)
        if NA0:
            a0_beta(seq0[-1], (len(seq0) - 1) % 2)
        wq_pump(1.0)
        chain = chainA[0][-3:] + chainA[1][-3:]
        prevq = wqs["q"]
        wq_last = wqs["pd"] if wqs["pd"] is not None else wA_last

        mzp = go(lambda g: g.memset(trn[0:8, 700:706], 0.0))
        dis = [] if PHASE < 4 else [
            dmag(lambda e: e.dma_start(out=cc_in[0:8, 1026:1032],
                                       in_=trn[0:8, 700:706]), [mzp] + chain),
            dmag(lambda e: e.dma_start(out=cc_in[8:16, 1026:1032],
                                       in_=trn[0:8, 700:706]), [mzp] + chain),
            dmag(lambda e: e.dma_start(out=cc_in[0:8, 0:1],
                                       in_=lnZt[0:8, 0:1],
                                       allow_slow_non_contiguous=True), chain),
            dmag(lambda e: e.dma_start(out=cc_in[8:16, 0:1],
                                       in_=lnZt[0:8, 1:2],
                                       allow_slow_non_contiguous=True), chain),
            dmag(lambda e: e.dma_start(out=cc_in[0:8, 1:2],
                                       in_=dAt[0:8, 0:1],
                                       allow_slow_non_contiguous=True), chain),
            dmag(lambda e: e.dma_start(out=cc_in[8:16, 1:2],
                                       in_=dAt[0:8, 1:2],
                                       allow_slow_non_contiguous=True), chain),
            dmag(lambda e: e.dma_start(out=cc_in[0:8, 2:1026],
                                       in_=Pst[0:8, 0:D]), chain),
            dmag(lambda e: e.dma_start(out=cc_in[8:16, 2:1026],
                                       in_=Pst[0:8, D:2 * D]), chain),
        ]
        if _os.environ.get("MK_NOCOL"):
            col = None if PHASE < 4 else go(
                lambda g: g.memset(ccs[0:1, 0:1], 0.0), dis)
        else:
            col = None if PHASE < 4 else S_.add(
                "gpsimd", lambda g: g.collective_compute(
                    "AllGather", ALU.bypass,
                    replica_groups=[list(range(NCORE))],
                    ins=[cc_in[:, :]], outs=[cc_out[:, :]]), dis, force=True)
        dco = None if PHASE < 4 else dmag(
            lambda e: e.dma_start(out=ccs[:, :], in_=cc_out[:, :]), [col])
        aj1 = None if PHASE < 4 else vof(lambda v: v.tensor_tensor(ccs[:, 0:1], ccs[:, 0:1],
                                           adjt[:, 0:1], ALU.add),
                 [dco, dc_aj])
        aj2 = None if PHASE < 4 else vof(lambda v: v.tensor_tensor(ccs[:, 1:2], ccs[:, 1:2],
                                           adjt[:, 1:2], ALU.mult),
                 [dco, dc_aj])
        rs1 = None if PHASE < 4 else go(lambda g: g.memset(lnZt[:, :], NEG),
                                        chain + dis)
        rs2 = None if PHASE < 4 else go(lambda g: g.memset(Pst[:, :], 0.0),
                                        chain + dis)
        fold = [rs1, rs2, aj1, aj2] if PHASE >= 4 else list(chain)
        for j in range(0 if _os.environ.get('MK_NOF') else (NCORE - 1 if PHASE >= 4 else 0)):
            dj1 = dmag(lambda e, jj=j: e.dma_start(
                out=ccj[0:8, 0:1032], in_=ccs[16 * jj:16 * jj + 8, :]), fold)
            dj2 = dmag(lambda e, jj=j: e.dma_start(
                out=ccj[0:8, 1032:2064],
                in_=ccs[16 * jj + 8:16 * jj + 16, :]), fold)
            nfold = []
            for bi in range(2):
                co = bi * 1032
                f1 = vof(lambda v, c_=co, b2=bi: v.tensor_tensor(
                    trn[0:8, 0:1], lnZt[0:8, b2:b2 + 1],
                    ccj[0:8, c_ + 1:c_ + 2], ALU.add), [dj1, dj2] + fold)
                f2 = vof(lambda v, c_=co: v.tensor_tensor(
                    trn[0:8, 1:2], trn[0:8, 0:1], ccj[0:8, c_:c_ + 1],
                    ALU.max), [f1])
                f3a = vof(lambda v: v.tensor_tensor(
                    trn[0:8, 2:3], trn[0:8, 0:1], trn[0:8, 1:2],
                    ALU.subtract), [f2])
                f3b = vof(lambda v, c_=co: v.tensor_tensor(
                    trn[0:8, 3:4], ccj[0:8, c_:c_ + 1], trn[0:8, 1:2],
                    ALU.subtract), [f2])
                f4 = so(lambda s: s.activation(out=trn[0:8, 4:5],
                                               in_=trn[0:8, 2:3],
                                               func=AF.Exp), [f3a])
                f5 = so(lambda s: s.activation(out=trn[0:8, 5:6],
                                               in_=trn[0:8, 3:4],
                                               func=AF.Exp), [f3b])
                f6 = vof(lambda v: v.tensor_tensor(trn[0:8, 6:7], trn[0:8, 4:5],
                                                  trn[0:8, 5:6], ALU.add),
                        [f4, f5])
                f7 = so(lambda s: s.activation(out=trn[0:8, 7:8],
                                               in_=trn[0:8, 6:7],
                                               func=AF.Ln), [f6])
                f8 = vof(lambda v, b2=bi: v.tensor_tensor(
                    lnZt[0:8, b2:b2 + 1], trn[0:8, 1:2], trn[0:8, 7:8],
                    ALU.add), [f7])
                f9a = vof(lambda v, b2=bi: v.tensor_tensor(
                    trn[0:8, 8:9], trn[0:8, 0:1], lnZt[0:8, b2:b2 + 1],
                    ALU.subtract), [f8])
                f9b = vof(lambda v, c_=co, b2=bi: v.tensor_tensor(
                    trn[0:8, 9:10], ccj[0:8, c_:c_ + 1],
                    lnZt[0:8, b2:b2 + 1], ALU.subtract), [f8])
                fa = so(lambda s: s.activation(out=trn[0:8, 10:11],
                                               in_=trn[0:8, 8:9],
                                               func=AF.Exp), [f9a])
                fb = so(lambda s: s.activation(out=trn[0:8, 11:12],
                                               in_=trn[0:8, 9:10],
                                               func=AF.Exp), [f9b])
                fc = vof(lambda v, c_=co: v.tensor_scalar(
                    ccj[0:8, c_ + 2:c_ + 1026], ccj[0:8, c_ + 2:c_ + 1026],
                    trn[0:8, 11:12], None, ALU.mult), [fb])
                fd = vo(lambda v, c_=co, b2=bi: v.scalar_tensor_tensor(
                    Pst[0:8, b2 * D:(b2 + 1) * D],
                    Pst[0:8, b2 * D:(b2 + 1) * D], trn[0:8, 10:11],
                    ccj[0:8, c_ + 2:c_ + 1026], ALU.mult, ALU.add), [fa, fc])
                nfold += [fd, f8]
            fold = nfold

        a0_tail = (list(chain) + psA_free + psB_free + pend0[0] + pend0[1]
                   + [prevq, wq_last])
        chain = list(fold)
        chainB = {0: list(fold), 1: list(fold)}
        prev_emm = []
        prev_x2L = list(psB_free)
        prev_x2 = None
        prev_sx2 = None
        prev_xq = None
        pend1 = [list(a0_tail), list(a0_tail)]
        prev_dxq = []
        prev_m2 = []

        freeA01 = list(psA_free)
        freeA2 = list(psA_free)
        freeB2 = list(psB_free)
        prev_pups = list(psB_free)
        prev_yg = []
        A1S = {}

        def a1_tiles(pc):
            return (trnP[pc], vvtP[pc], accRP[pc], DtokP[pc], rDtkP[pc],
                    eKTP[pc], wclTP[pc], pvTP[pc], RhP[pc], RTsP[pc],
                    sxP[pc], sxxP[pc], mntP[pc], vrtP[pc],
                    rstdP[pc], mbnP[pc], batP[pc], xtP[pc],
                    kdbfP[pc])

        def a1_alpha(cc, pc):
            bi, c = cc // NCH, cc % NCH
            chain = chainB[bi]
            (trn, vvt, accR, Dtok, rDtk, eKT, wclT, pvT, Rh, RTs, sx,
             sxx, mnt, vrt, rstd, mbn, bat,
             xt, kdbf) = a1_tiles(pc)
            pguard = pend1[pc]
            st = {}
            A1S[cc] = st
            gc = bi * SCORE + c * C
            uc = slice(gc, gc + C)
            dvv = dmag(lambda e, ccx=cc: e.dma_start(
                out=vvt[:, :], in_=s_vv[ccx, :, :]),
                [vv_st[cc]] + pguard)
            duf = dmag(lambda e, u_=uc: e.dma_start(
                out=rhse[8:9, 0:1024], in_=sm_u[0:8, u_]),
                [dc_re] + prev_emm + pguard)
            pbf = vo(lambda v: v.tensor_copy(Pbf[0:8, :], Pst[0:8, :]),
                     chain + prev_yg)
            nl1 = vof(lambda v, u_=uc: v.tensor_scalar(
                trn[0:8, 0:128], sm_u[0:8, u_], -1.0, None,
                ALU.mult), chain)
            emm = []
            for h in range(8):
                emm.append(mm(psA[:, h * 128:(h + 1) * 128],
                              sm_u[0:9, uc],
                              rhse[0:9, h * 128:(h + 1) * 128],
                              h % 4 == 0, False, [duf] + freeA01))
            for h in range(8):
                emm.append(mm(psA[:, h * 128:(h + 1) * 128], idf[:, :],
                              mskt[:, :], False, h % 4 == 3,
                              [dc_id, dc_mk, emm[h]]))
            exd = []
            for h in range(8):
                stopper = emm[8 + (3 if h < 4 else 7)]
                exd.append(so(lambda s, hh=h: s.activation(
                    out=Rh[:, hh * 128:(hh + 1) * 128],
                    in_=psA[:, hh * 128:(hh + 1) * 128], func=AF.Exp,
                    accum_out=accR[:, hh:hh + 1]), [emm[8 + h], stopper]))
            ekh = so(lambda s, b2=bi: s.activation(
                out=trn[0:8, 128:256], in_=trn[0:8, 0:128],
                func=AF.Exp, bias=lnZt[0:8, b2:b2 + 1]), [nl1] + chain)
            ekt = mm(psB[:, 1024:1032], trn[0:8, 128:256], idf[0:8, 0:8],
                     True, True, [ekh, dc_id] + freeB2, tr=True)
            eke = vof(lambda v: v.tensor_copy(eKT[:, :], psB[:, 1024:1032]),
                     [ekt])
            dtk = vof(lambda v: v.tensor_tensor(Dtok[:, :], accR[:, :],
                                               eKT[:, :], ALU.add),
                     exd + [eke])
            rdt = vof(lambda v: v.reciprocal(rDtk[:, :], Dtok[:, :]), [dtk])
            nrm = []
            for h in range(8):
                nrm.append(vo(lambda v, hh=h: v.tensor_scalar(
                    Rh[:, hh * 128:(hh + 1) * 128],
                    Rh[:, hh * 128:(hh + 1) * 128], rDtk[:, hh:hh + 1], None,
                    ALU.mult), [rdt, exd[h]]))
            dtt = mm(psB[0:8, 1032:1160], Dtok[:, :], idf[:, :],
                     True, True, [dtk, dc_id, rdt] + freeB2, tr=True)
            kdt = vof(lambda v: v.tensor_tensor(eKT[:, :], eKT[:, :],
                                               rDtk[:, :], ALU.mult),
                     [rdt, dtk])
            rdl = vof(lambda v: v.reciprocal(
                trn[0:8, 648:649], psB[0:8, 1159:1160]), [dtt])
            lnd = so(lambda s: s.activation(
                out=trn[0:8, 650:651], in_=psB[0:8, 1159:1160],
                func=AF.Ln), [dtt])
            kdm = mm(psB[0:8, 1160:1288], eKT[:, :], idf[:, :],
                     True, True, [kdt, dc_id, rdl, lnd] + freeB2, tr=True)
            kde = vo(lambda v: v.tensor_copy(
                kdbf[0:8, :], psB[0:8, 1160:1288]), [kdm])
            kdl = vof(lambda v: v.tensor_copy(
                trn[0:8, 647:648], psB[0:8, 1287:1288]), [kdm])
            rtm = []
            for h in range(8):
                rtm.append(mm(psA[:, 1024:1152],
                              Rh[:, h * 128:(h + 1) * 128], idf[:, :],
                              h == 0, h == 7,
                              [nrm[h], dc_id] + freeA2, tr=True))
            rte = so(lambda s: s.activation(out=RTs[:, :],
                                            in_=psA[:, 1024:1152],
                                            func=AF.Copy), rtm)
            lnu = vo(lambda v, u_=uc, b2=bi: v.tensor_tensor(
                lnZt[0:8, b2:b2 + 1],
                sm_b[0:8, u_.start + 127:u_.start + 128],
                trn[0:8, 650:651], ALU.add), [lnd, ekh] + chain)
            prev_emm[:] = emm[7:8]
            st.update(dvv=dvv, exd=exd, nrm=nrm, rte=rte, kde=kde, kdl=kdl,
                      rdl=rdl, lnd=lnd, eke=eke, dtk=dtk, kdt=kdt, lnu=lnu,
                      pbf=pbf, ekh=ekh)

        def a1_beta(cc, pc):
            nonlocal prev_sx2, prev_xq, prev_dxq, freeA01, freeA2, \
                freeB2, prev_pups, prev_yg, prev_m2
            bi, c = cc // NCH, cc % NCH
            chain = chainB[bi]
            (trn, vvt, accR, Dtok, rDtk, eKT, wclT, pvT, Rh, RTs, sx,
             sxx, mnt, vrt, rstd, mbn, bat,
             xt, kdbf) = a1_tiles(pc)
            pguard = pend1[pc]
            st = A1S[cc]
            dvv, exd, rte, kde, kdl, rdl, lnd, lnu, pbf = (
                st["dvv"], st["exd"], st["rte"], st["kde"], st["kdl"],
                st["rdl"], st["lnd"], st["lnu"], st["pbf"])
            row0 = bi * SCORE + c * C
            gc = bi * SCORE + c * C
            uc = slice(gc, gc + C)
            ymm = []
            yg = []
            for dc in range(8):
                m1 = mm(psB[:, dc * 128:(dc + 1) * 128],
                        Pbf[0:8, bi * D + dc * 128:bi * D + (dc + 1) * 128],
                        kdbf[0:8, :], dc % 4 == 0, False,
                        [pbf, kde, dvv] + prev_pups)
                m2_ = mm(psB[:, dc * 128:(dc + 1) * 128],
                         vvt[:, dc * 128:(dc + 1) * 128], RTs[:, :],
                         False, dc % 4 == 3, [m1, rte, dvv])
                ymm += [m1, m2_]
                yg.append(m2_)
            yev = []
            for dc in range(8):
                stopper = yg[3 if dc < 4 else 7]
                yev.append(so(lambda s, dcc=dc: s.activation(
                    out=yTs[:, dcc * 128:(dcc + 1) * 128],
                    in_=psB[:, dcc * 128:(dcc + 1) * 128], func=AF.Copy),
                    [yg[dc], stopper]))
            if c == 0:
                for dc in range(8):
                    yev.append(vo(lambda v, dcc=dc: v.tensor_scalar(
                        yTs[:, dcc * 128:dcc * 128 + 1],
                        yTs[:, dcc * 128:dcc * 128 + 1], y0t[:, 0:1], None,
                        ALU.mult), [yev[dc], dc_y0]))
            nl2 = vof(lambda v, u_=uc: v.tensor_scalar(
                trn[0:8, 649:650],
                sm_u[0:8, u_.start + 127:u_.start + 128],
                -1.0, None, ALU.mult), chain)
            wx1 = so(lambda s, u_=uc: s.activation(
                out=trn[0:8, 256:384], in_=sm_u[0:8, u_],
                func=AF.Exp, bias=trn[0:8, 649:650]), [nl2])
            wn1 = vof(lambda v: v.tensor_scalar(
                trn[0:8, 256:384], trn[0:8, 256:384],
                trn[0:8, 648:649], None, ALU.mult), [wx1, rdl])
            wt1 = mm(psB[:, 1280:1288], trn[0:8, 256:384], idf[0:8, 0:8],
                     True, True, [wn1, dc_id, lnd, kde, kdl] + freeB2,
                     tr=True)
            we1 = vo(lambda v: v.tensor_copy(wclT[:, :], psB[:, 1280:1288]),
                     [wt1])
            pvm = []
            for dc in range(8):
                pvm.append(mm(psB[:, 1288 + dc * 8:1296 + dc * 8],
                              vvt[:, dc * 128:(dc + 1) * 128], wclT[:, :],
                              True, True, [we1, dvv]))
            pve = vo(lambda v: v.tensor_copy(pvT[:, :], psB[:, 1288:1352]),
                     pvm)
            ptm2 = []
            for dc in range(8):
                ptm2.append(mm(psB[0:8, dc * 128:(dc + 1) * 128],
                               pvT[:, dc * 8:(dc + 1) * 8], idf[:, :],
                               True, True,
                               [pve, dc_id] + yev[0:8], tr=True))
            pups = [vo(lambda v, b2=bi: v.scalar_tensor_tensor(
                Pst[0:8, b2 * D:(b2 + 1) * D], Pst[0:8, b2 * D:(b2 + 1) * D],
                trn[0:8, 647:648], psB[0:8, 0:1024], ALU.mult, ALU.add),
                ptm2 + [pbf, kdl] + chain)]
            wmm = []
            for dc in range(8):
                lT = yTs[:, dc * 128:(dc + 1) * 128]
                for si in range(2):
                    wmm.append(mm(
                        psA[:, si * 512:(si + 1) * 512], lT,
                        WoT[:, dc * D + si * 512:dc * D + (si + 1) * 512],
                        dc == 0, dc == 7, yev + exd))
            dx2 = dmas(lambda e, r0=row0: e.dma_start(
                out=xt[:, :], in_=x_sl[r0:r0 + 128, :]), pguard)
            x2v = vo(lambda v: v.tensor_tensor(x2t[:, :], psA[:, 0:1024],
                                               xt[:, :], ALU.add),
                     wmm + [dx2] + prev_m2
                     + ([prev_sx2] if prev_sx2 else []))
            sx2 = dmag(lambda e, r0=row0: e.dma_start(
                out=s_x2[r0:r0 + 128, :], in_=x2t[:, :]), [x2v])
            prev_sx2 = sx2
            m1 = vof(lambda v: v.tensor_reduce(sx[:, :], x2t[:, :], AX.X,
                                              ALU.add), [x2v])
            m2 = so(lambda s: s.activation(out=q_sq[:, 1024:2048],
                                           in_=x2t[:, :],
                                           func=AF.Square,
                                           accum_out=sxx[:, :]),
                    [x2v] + ([prev_xq] if prev_xq else []))
            m3 = vof(lambda v: v.tensor_scalar(mnt[:, :], sx[:, :], 1.0 / D,
                                              None, ALU.mult), [m1])
            m4 = vof(lambda v: v.tensor_scalar(vrt[:, :], sxx[:, :], 1.0 / D,
                                              None, ALU.mult), [m2])
            m5 = vof(lambda v: v.tensor_tensor(sx[:, :], mnt[:, :], mnt[:, :],
                                              ALU.mult), [m3])
            m6 = vof(lambda v: v.tensor_tensor(vrt[:, :], vrt[:, :], sx[:, :],
                                              ALU.subtract), [m4, m5])
            m7 = so(lambda s: s.activation(out=vrt[:, :], in_=vrt[:, :],
                                           func=AF.Ln, bias=cst[:, 0:1]),
                    [m6] + mzc)
            m8 = so(lambda s: s.activation(out=rstd[:, :], in_=vrt[:, :],
                                           func=AF.Exp, scale=-0.5), [m7])
            m9 = vof(lambda v: v.tensor_tensor(mbn[:, :], mnt[:, :], rstd[:, :],
                                              ALU.mult), [m8])
            m10 = vof(lambda v: v.tensor_scalar(mbn[:, :], mbn[:, :], -1.0,
                                               None, ALU.mult), [m9])
            m11 = vo(lambda v: v.tensor_scalar(xt[:, :], x2t[:, :],
                                               rstd[:, :], mbn[:, :],
                                               ALU.mult, ALU.add),
                     [m10, m8, sx2])
            xq = emit_quant(1024, xt, xnt,
                            ([m11, m2] + a0_tail) if cc == 0
                            else ([m11, m2] + prev_dxq))
            prev_xq = xq
            xqd = twave([(xnt[:, dc * 128:(dc + 1) * 128],
                          (lambda s, ps, d2=dc: s.activation(
                              out=xqT[:, d2 * 128:(d2 + 1) * 128],
                              in_=ps, func=AF.Copy)))
                         for dc in range(8)],
                        [xq] + pguard + prev_dxq)
            dxq = dmag(lambda e, ccx=cc: e.dma_start(
                out=s_xqT[ccx, :, :], in_=xqT[:, :]), xqd)
            prev_dxq = [dxq]
            prev_m2 = [m2]
            chainB[bi] = [lnu] + pups[-1:]
            freeA01 = [x2v]
            freeA2 = [rte]
            freeB2 = [st["eke"], kde, kdl, rdl, lnd, we1, pve]
            prev_pups = list(pups)
            prev_yg = yg[-1:]
            pend1[pc] = ([x2v, sx2, dxq, xq, pve, lnu, kde, kdl, rdl, lnd,
                          st["eke"], we1, rte] + pups + yev[0:8] + xqd
                         + pvm[-1:] + [st["dtk"], st["kdt"], st["nrm"][-1]])

        NA1 = NCC if PHASE >= 5 else 0
        seq1 = [bi * NCH + c for c in range(NCH) for bi in (0, 1)][:NA1]
        for k, cc in enumerate(seq1):
            if k > 0:
                a1_beta(seq1[k - 1], (k - 1) % 2)
            a1_alpha(cc, k % 2)
        if NA1:
            a1_beta(seq1[-1], (len(seq1) - 1) % 2)
        chain = chainB[0][-2:] + chainB[1][-2:]

        if _os.environ.get("MK_DBG"):
            S_.add("sync", lambda e: e.dma_start(
                out=d_smub[:, :], in_=sm_u[:, :]), chain, is_dma=True)
            S_.add("sync", lambda e: e.dma_start(
                out=d_smbb[:, :], in_=sm_b[:, :]), chain, is_dma=True)
        prev_odma = None
        aend = [x for x in (pend1[0] + pend1[1]
                            + [prev_sx2, prev_dvv] + chain
                            + psA_free + psB_free + [prev_xq])
                if x is not None]
        esA.close()

        esB = ExitStack()

        def sbB(n, s, d=F32):
            return esB.enter_context(nc.sbuf_tensor(n, s, d)).ap()

        with esB:
            WguT = sbB("WguT", [128, 8 * 2 * I_FF], BF16)
            WdT = sbB("WdT", [128, 22 * D], BF16)
            hq = sbB("hq", [128, I_FF], BF16)
            hqT = sbB("hqT", [128, I_FF], BF16)
            silg = sbB("silg", [128, 704], F32)
            hhf = sbB("hhf", [128, 704], F32)
            outt = sbB("outt", [128, D], F32)
            xqL = sbB("xqL", [128, D], BF16)

            wld = []
            for dc in range(8 if PHASE >= 6 else 0):
                wld.append(dmas(lambda e, dcc=dc: e.dma_start(
                    out=WguT[:, dcc * 2 * I_FF:(dcc + 1) * 2 * I_FF],
                    in_=s_wgu[dcc, :, :]), [wq_last] + aend))
            for ic in range(22 if PHASE >= 6 else 0):
                wld.append(dmas(lambda e, icc=ic: e.dma_start(
                    out=WdT[:, icc * D:(icc + 1) * D], in_=s_wd[icc, :, :]),
                    [wq_last] + aend))

            prev_out = None
            prev_odma2 = None
            prev_hte = []
            psAf = aend
            psBf = aend
            for cc in range(NCC if PHASE >= 6 else 0):
                bi, c = cc // NCH, cc % NCH
                row0 = bi * SCORE + c * C
                dxq = dmas(lambda e, ccx=cc: e.dma_start(
                    out=xqL[:, :], in_=s_xqT[ccx, :, :]),
                    ([prev_out] if prev_out else aend))
                dx2 = dmas(lambda e, r0=row0: e.dma_start(
                    out=x2t[:, :], in_=s_x2[r0:r0 + 128, :]),
                    ([prev_out] if prev_out else aend))
                hqd = []
                hte = []
                hte_prev = list(prev_hte)
                isl = [(0, 512), (512, 512), (1024, 512), (1536, 512),
                       (2048, 512), (2560, 256)]
                for (io, wi) in isl:
                    gmm = []
                    for dc in range(8):
                        lT = xqL[:, dc * 128:(dc + 1) * 128]
                        wb = dc * 2 * I_FF
                        gmm.append(mm(psA[:, 0:wi], lT,
                                      WguT[:, wb + io:wb + io + wi],
                                      dc == 0, dc == 7,
                                      [dxq, wld[-1]] + psAf))
                        gmm.append(mm(psA[:, 512:512 + wi], lT,
                                      WguT[:, wb + I_FF + io:wb + I_FF + io + wi],
                                      dc == 0, dc == 7,
                                      [dxq, wld[-1]] + psAf))
                    sil = so(lambda s, wi_=wi: s.activation(
                        out=silg[:, 0:wi_], in_=psA[:, 0:wi_],
                        func=AF.Silu), gmm)
                    hmu = vo(lambda v, wi_=wi: v.tensor_tensor(
                        hhf[:, 0:wi_], psA[:, 512:512 + wi_],
                        silg[:, 0:wi_], ALU.mult), [sil] + gmm)
                    hqd.append(emit_quant(wi, hhf, hq, [hmu] + hte_prev,
                                          outoff=io))
                    hte_prev = []
                    psAf = [hmu, sil]
                for w0 in range(0, 22, 8):
                    pieces = []
                    for ic in range(w0, min(w0 + 8, 22)):
                        pieces.append((hq[:, ic * 128:(ic + 1) * 128],
                                       (lambda s, ps, i2=ic: s.activation(
                                           out=hqT[:, i2 * 128:(i2 + 1) * 128],
                                           in_=ps, func=AF.Copy))))
                    hte += twave(pieces, hqd)
                dmm = []
                for ic in range(22):
                    lT = hqT[:, ic * 128:(ic + 1) * 128]
                    for si in range(2):
                        dmm.append(mm(
                            psB[:, si * 512:(si + 1) * 512], lT,
                            WdT[:, ic * D + si * 512:ic * D + (si + 1) * 512],
                            ic == 0, ic == 21, hte + psBf))
                ov = vo(lambda v: v.tensor_tensor(outt[:, :], psB[:, 0:1024],
                                                  x2t[:, :], ALU.add),
                        dmm + [dx2]
                        + ([prev_odma2] if prev_odma2 else []))
                odma = dmas(lambda e, r0=row0: e.dma_start(
                    out=out_sl[r0:r0 + 128, :], in_=outt[:, :]), [ov])
                prev_out = ov
                prev_odma2 = odma
                prev_hte = [hte[-1]]
                psBf = [ov]

            if cutoff is not None:
                S_.items = S_.items[:cutoff]
            if _os.environ.get("MK_NITEMS"):
                print("SCHED items:", len(S_.items))
            S_.resolve()
            with (nc.semaphore("sd") as sem_sd, nc.semaphore("gd") as sem_gd,
                  nc.semaphore("v") as sem_v, nc.semaphore("s") as sem_s,
                  nc.semaphore("t") as sem_t, nc.semaphore("g") as sem_g,
                  nc.Block() as block):
                sems = {"sd": sem_sd, "gd": sem_gd, "vector": sem_v,
                        "scalar": sem_s, "tensor": sem_t, "gpsimd": sem_g}

                @block.sync
                def _(sync):
                    S_.emit("sync", sync, sems)
                    sync.wait_ge(sem_sd, S_.final.get("sd", 0))
                    if S_.final.get("gd"):
                        sync.wait_ge(sem_gd, S_.final["gd"])

                @block.gpsimd
                def _(g):
                    S_.emit("gpsimd", g, sems)

                @block.vector
                def _(v):
                    S_.emit("vector", v, sems)

                @block.scalar
                def _(s):
                    S_.emit("scalar", s, sems)

                @block.tensor
                def _(t):
                    S_.emit("tensor", t, sems)
    return nc


_NC_CACHE = [None]


def kernel(L0_attn_x, L0_attn_W_q, L0_attn_W_k, L0_attn_W_v, L0_attn_W_alpha_w,
           L0_attn_W_alpha_b, L0_attn_W_o, L0_ffn_w_gate, L0_ffn_w_up,
           L0_ffn_w_down):
    x = np.ascontiguousarray(np.asarray(L0_attn_x, np.float32))
    if _NC_CACHE[0] is None:
        _NC_CACHE[0] = build()
    nc = _NC_CACHE[0]
    iden = np.eye(128, dtype=np.float32)
    idenb = np.eye(128, dtype=ml_dtypes.bfloat16)
    maskut = np.where(np.arange(128)[:, None] < np.arange(128)[None, :],
                      np.float32(NEG), np.float32(0.0)).astype(np.float32)
    rhsec = np.zeros((9, 1024), np.float32)
    for h in range(8):
        rhsec[h, h * 128:(h + 1) * 128] = -1.0
    onesc = np.ones((1, 128), np.float32)
    wab2 = np.ascontiguousarray(
        np.asarray(L0_attn_W_alpha_b, np.float32).reshape(1, 512))
    in_maps = []
    for r in range(NCORE):
        adj = np.zeros((128, 2), np.float32)
        for j in range(NCORE):
            if j >= r:
                adj[16 * j:16 * (j + 1), 0] = NEG
            else:
                adj[16 * j:16 * (j + 1), 1] = 1.0
        y0 = np.full((128, 1), 1.0, np.float32)
        if r == 0:
            y0[:] = np.float32(1.0) / np.float32(1.0 + EPS)
        xs = np.ascontiguousarray(
            x[:, r * SCORE:(r + 1) * SCORE, :].reshape(2 * SCORE, D))
        in_maps.append({
            "x_sl": xs,
            "Wq": np.ascontiguousarray(np.asarray(L0_attn_W_q, np.float32)),
            "Wk": np.ascontiguousarray(np.asarray(L0_attn_W_k, np.float32)),
            "Wa": np.ascontiguousarray(
                np.asarray(L0_attn_W_alpha_w, np.float32)),
            "Wab": wab2,
            "Wv": np.ascontiguousarray(np.asarray(L0_attn_W_v, np.float32)),
            "Wo": np.ascontiguousarray(np.asarray(L0_attn_W_o, np.float32)),
            "Wg": np.ascontiguousarray(np.asarray(L0_ffn_w_gate, np.float32)),
            "Wu": np.ascontiguousarray(np.asarray(L0_ffn_w_up, np.float32)),
            "Wd": np.ascontiguousarray(np.asarray(L0_ffn_w_down, np.float32)),
            "idenf": iden, "idenb": idenb, "maskut": maskut,
            "rhsec": rhsec, "onesc": onesc, "adjc": adj, "y0fc": y0,
        })
    res = run_bass_kernel_spmd(nc, in_maps, core_ids=list(range(NCORE)))
    out = np.empty((B, S, D), np.float32)
    for r in range(NCORE):
        o = np.asarray(res.results[r]["out_sl"]).reshape(2, SCORE, D)
        out[:, r * SCORE:(r + 1) * SCORE, :] = o
    return out


# version 19 (final)
# speedup vs baseline: 2.3202x; 1.0016x over previous
import numpy as np
import ml_dtypes
from contextlib import ExitStack
import os as _os
import concourse.bass as bass
import concourse.mybir as mybir
from concourse.bass_utils import run_bass_kernel_spmd

F32 = mybir.dt.float32
BF16 = mybir.dt.bfloat16
AF = mybir.ActivationFunctionType
ALU = mybir.AluOpType
AX = mybir.AxisListType

B, S, D, H = 2, 16384, 1024, 8
NCORE = 8
SCORE = S // NCORE
C = 128
NCH = SCORE // C
NCC = 2 * NCH
I_FF = 2816
EPS = 1e-8
C2V = 1.0 + EPS
C1V = EPS * (1.0 + EPS)
SCL_B = 0.125
T2 = [0.0625, 0.5625, 1.5625, 3.0625, 6.25, 12.25, 25.0]
NEG = -1.0e30

DMA_SEM = {"sync": "sd", "gpsimd": "gd"}


class Item:
    __slots__ = ("eng", "fn", "deps", "is_dma", "idx", "sem_val", "needed",
                 "force", "dma_cum")

    def __init__(self, eng, fn, deps, is_dma, idx, force=False):
        self.eng, self.fn, self.is_dma, self.idx = eng, fn, is_dma, idx
        self.deps = [d for d in deps if d is not None]
        self.sem_val = None
        self.needed = force
        self.force = force


class Sched:
    def __init__(self):
        self.items = []
        self.final = {}

    def add(self, eng, fn, deps=(), is_dma=False, force=False):
        it = Item(eng, fn, deps, is_dma, len(self.items), force)
        self.items.append(it)
        return it

    def resolve(self):
        for it in self.items:
            for d in it.deps:
                assert d.idx < it.idx
                if d.eng != it.eng or d.is_dma:
                    d.needed = True
        cnt = {}
        dcum = {"sd": 0, "gd": 0}
        for it in self.items:
            it.dma_cum = dict(dcum)
            if it.is_dma:
                sem = DMA_SEM[it.eng]
                cnt[sem] = cnt.get(sem, 0) + 16
                it.sem_val = cnt[sem]
                dcum[sem] = cnt[sem]
            elif it.needed:
                cnt[it.eng] = cnt.get(it.eng, 0) + 1
                it.sem_val = cnt[it.eng]
        self.final = dict(cnt)

    def emit(self, eng_name, eng, sems):
        if not hasattr(self, "hw"):
            self.hw = {}
        for it in self.items:
            if it.eng != eng_name:
                continue
            waited = {}
            for d in it.deps:
                if d.eng == eng_name and not d.is_dma and not d.force:
                    continue
                if d.is_dma:
                    sem = DMA_SEM[d.eng]
                    val = it.dma_cum[sem]
                else:
                    sem = d.eng
                    val = d.sem_val
                if waited.get(sem, -1) < val:
                    waited[sem] = val
            for sem, val in sorted(waited.items()):
                if self.hw.get((eng_name, sem), -1) >= val:
                    continue
                self.hw[(eng_name, sem)] = val
                eng.wait_ge(sems[sem], val)
            ins = it.fn(eng)
            if it.is_dma:
                ins.then_inc(sems[DMA_SEM[it.eng]], 16)
            elif it.needed:
                ins.then_inc(sems[it.eng], 1)


def build(cutoff=None):
    PHASE = int(_os.environ.get("MK_PHASE", "9"))
    nc = bass.Bass("TRN2", target_bir_lowering=False, debug=False,
                   detect_race_conditions=False)
    dt = nc.dram_tensor
    x_sl = dt("x_sl", [2 * SCORE, D], F32, kind="ExternalInput").ap()
    Wq = dt("Wq", [512, D], F32, kind="ExternalInput").ap()
    Wk = dt("Wk", [512, D], F32, kind="ExternalInput").ap()
    Wa = dt("Wa", [512, D], F32, kind="ExternalInput").ap()
    Wab = dt("Wab", [1, 512], F32, kind="ExternalInput").ap()
    Wv = dt("Wv", [D, D], F32, kind="ExternalInput").ap()
    Wo = dt("Wo", [D, D], F32, kind="ExternalInput").ap()
    Wg = dt("Wg", [I_FF, D], F32, kind="ExternalInput").ap()
    Wu = dt("Wu", [I_FF, D], F32, kind="ExternalInput").ap()
    Wd = dt("Wd", [D, I_FF], F32, kind="ExternalInput").ap()
    idenf = dt("idenf", [128, 128], F32, kind="ExternalInput").ap()
    idenb = dt("idenb", [128, 128], BF16, kind="ExternalInput").ap()
    maskut = dt("maskut", [128, 128], F32, kind="ExternalInput").ap()
    rhsec = dt("rhsec", [9, 1024], F32, kind="ExternalInput").ap()
    onesc = dt("onesc", [1, 128], F32, kind="ExternalInput").ap()
    adjc = dt("adjc", [128, 2], F32, kind="ExternalInput").ap()
    y0fc = dt("y0fc", [128, 1], F32, kind="ExternalInput").ap()
    out_sl = dt("out_sl", [2 * SCORE, D], F32, kind="ExternalOutput").ap()
    _dbg = "ExternalOutput" if _os.environ.get("MK_DBG") else "Internal"
    s_x2 = dt("s_x2", [2 * SCORE, D], F32, kind=_dbg).ap()
    s_vv = dt("s_vv", [NCC, 128, D], BF16, kind=_dbg).ap()
    s_xqT = dt("s_xqT", [NCC, 128, D], BF16, kind=_dbg).ap()
    s_wgu = dt("s_wgu", [8, 128, 2 * I_FF], BF16, kind=_dbg).ap()
    s_wd = dt("s_wd", [22, 128, D], BF16, kind=_dbg).ap()
    if _os.environ.get("MK_DBG"):
        d_xnt = dt("d_xnt", [128, D], BF16, kind="ExternalOutput").ap()
        d_kev = dt("d_kev", [128, 512], F32, kind="ExternalOutput").ap()
        d_bat = dt("d_bat", [128, 16], F32, kind="ExternalOutput").ap()
        d_smu = dt("d_smu", [9, 128], F32, kind="ExternalOutput").ap()
        d_trn = dt("d_trn", [8, 768], F32, kind="ExternalOutput").ap()
        d_st = dt("d_st", [128, 8], F32, kind="ExternalOutput").ap()
        d_smub = dt("d_smub", [9, 2 * SCORE], F32, kind="ExternalOutput").ap()
        d_smbb = dt("d_smbb", [8, 2 * SCORE], F32, kind="ExternalOutput").ap()
        d_vvl = dt("d_vvl", [128, D], BF16, kind="ExternalOutput").ap()
        d_xt = dt("d_xt", [128, D], F32, kind="ExternalOutput").ap()
    cc_in = dt("cc_in", [16, 1032], F32).ap()
    cc_out = dt("cc_out", [128, 1032], F32, addr_space="Shared").ap()

    S_ = Sched()
    es = ExitStack()

    def sb(n, s, d=F32):
        return es.enter_context(nc.sbuf_tensor(n, s, d)).ap()

    with es:
        idf = sb("idf", [128, 128], F32)
        idb = sb("idb", [128, 128], BF16)
        mskt = sb("mskt", [128, 128], F32)
        onet = sb("onet", [1, 128], F32)
        wabt = sb("wabt", [1, 512], F32)
        adjt = sb("adjt", [128, 2], F32)
        y0t = sb("y0t", [128, 1], F32)
        cst = sb("cst", [128, 2], F32)
        xtP = [sb("xt0", [128, D], F32), sb("xt1", [128, D], F32)]
        x2t = sb("x2t", [128, D], F32)
        xt = xtP[0]
        q_u = sb("q_u", [128, 1024], F32)
        q_sq = sb("q_sq", [128, 2048], F32)
        q_z = sb("q_z", [128, 1024], F32)
        q_zc = sb("q_zc", [128, 1024], F32)
        q_sg = sb("q_sg", [128, 1536], F32)
        q_bm = sb("q_bm", [128, 88], F32)
        q_sc = sb("q_sc", [128, 88], F32)
        q_rs = sb("q_rs", [128, 88], F32)
        psA = es.enter_context(nc.psum_tensor("psA", [128, 1536], F32)).ap()
        psB = es.enter_context(nc.psum_tensor("psB", [128, 1536], F32)).ap()
        psT = es.enter_context(nc.psum_tensor("psT", [128, 2048], BF16)).ap()
        pstS = {"par": 0, "lastev": [None, None]}

        def twave(pieces, deps):
            b_ = pstS["par"]
            base = b_ * 1024
            lev = pstS["lastev"][b_]
            dms = []
            for i, (sap, _) in enumerate(pieces):
                dms.append(mm(psT[:, base + i * 128:base + (i + 1) * 128],
                              sap, idb[:, :], True, True,
                              deps + [dc_ib] + ([lev] if lev else []),
                              tr=True))
            evs = []
            for i, (_, efn) in enumerate(pieces):
                evs.append(so(lambda s, i_=i, e_=efn, bb=base: e_(
                    s, psT[:, bb + i_ * 128:bb + (i_ + 1) * 128]),
                    [dms[-1], dms[i]]))
            pstS["lastev"][b_] = evs[-1]
            pstS["par"] ^= 1
            return evs

        def mm(out, lhsT, rhs, start, stop, deps, tr=False):
            return S_.add("tensor", lambda t, o=out, l=lhsT, r=rhs, a=start,
                          b_=stop, c_=tr: t.matmul(o, l, r, start=a, stop=b_,
                                                   is_transpose=c_), deps)

        def vo(fn, deps):
            return S_.add("vector", fn, deps)

        def vof(fn, deps):
            return S_.add("vector", fn, deps, force=True)

        def so(fn, deps):
            return S_.add("scalar", fn, deps)

        def go(fn, deps=()):
            return S_.add("gpsimd", fn, deps)

        def dmas(fn, deps=()):
            return S_.add("sync", fn, deps, is_dma=True)

        def dmag(fn, deps=()):
            return S_.add("gpsimd", fn, deps, is_dma=True)

        def emit_quant(W, src, out, deps, srcoff=0, outoff=0,
                       outguard=(), track=None):
            nb = W // 16
            s_ap = src[:, srcoff:srcoff + W]
            s3 = s_ap.rearrange("p (b k) -> p b k", k=16)
            d0 = vof(lambda v: v.tensor_reduce(q_bm[:, 0:nb], s3, AX.X, ALU.max,
                                              apply_absolute_value=True), deps)
            d1 = vof(lambda v: v.tensor_scalar(q_sc[:, 0:nb], q_bm[:, 0:nb],
                                              1.0 / 6.0, 1e-12, ALU.mult,
                                              ALU.max), [d0])
            d2 = vof(lambda v: v.reciprocal(q_rs[:, 0:nb], q_sc[:, 0:nb]), [d1])
            rs3 = q_rs[:, 0:nb].rearrange("p (b o) -> p b o", o=1).broadcast_to(
                [128, nb, 16])
            u3 = q_u[:, 0:W].rearrange("p (b k) -> p b k", k=16)
            d3 = vo(lambda v: v.tensor_tensor(u3, s3, rs3, ALU.mult),
                    deps + [d2])
            if track is not None:
                track["d3"] = d3
            d4 = so(lambda s: s.activation(out=q_sq[:, 0:W], in_=q_u[:, 0:W],
                                           func=AF.Square), [d3])
            d5 = so(lambda s: s.activation(out=q_sg[:, 0:W], in_=q_u[:, 0:W],
                                           func=AF.Sign), [d3])
            e = vo(lambda v: v.tensor_scalar(q_z[:, 0:W], q_sq[:, 0:W], T2[0],
                                             None, ALU.is_gt), [d4])
            for th in T2[1:4]:
                e = vo(lambda v, t_=th: v.scalar_tensor_tensor(
                    q_z[:, 0:W], q_sq[:, 0:W], t_, q_z[:, 0:W], ALU.is_gt,
                    ALU.add), [e])
            f = vo(lambda v: v.tensor_scalar(q_zc[:, 0:W], q_sq[:, 0:W], T2[4],
                                             None, ALU.is_gt), [d4])
            for th in (T2[5], T2[6], T2[6]):
                f = vo(lambda v, t_=th: v.scalar_tensor_tensor(
                    q_zc[:, 0:W], q_sq[:, 0:W], t_, q_zc[:, 0:W], ALU.is_gt,
                    ALU.add), [f])
            g = vo(lambda v: v.scalar_tensor_tensor(
                q_sq[:, 0:W], q_z[:, 0:W], 0.5, q_zc[:, 0:W], ALU.mult,
                ALU.add), [e, f])
            g2 = vo(lambda v: v.tensor_tensor(q_u[:, 0:W], q_sq[:, 0:W],
                                              q_sg[:, 0:W], ALU.mult),
                    [g, d5])
            sc3 = q_sc[:, 0:nb].rearrange("p (b o) -> p b o", o=1).broadcast_to(
                [128, nb, 16])
            o3 = out[:, outoff:outoff + W].rearrange("p (b k) -> p b k", k=16)
            return vo(lambda v: v.tensor_tensor(o3, u3, sc3, ALU.mult),
                      [g2] + list(outguard))

        def emit_quant_pool(W, src, out, deps):
            nb = W // 16
            s_ap = src[:, 0:W]
            s3 = s_ap.rearrange("p (b k) -> p b k", k=16)
            d0 = vof(lambda v: v.tensor_reduce(q_bm[:, 0:nb], s3, AX.X, ALU.max,
                                              apply_absolute_value=True), deps)
            d1 = vof(lambda v: v.tensor_scalar(q_sc[:, 0:nb], q_bm[:, 0:nb],
                                              1.0 / 6.0, 1e-12, ALU.mult,
                                              ALU.max), [d0])
            d2 = vof(lambda v: v.reciprocal(q_rs[:, 0:nb], q_sc[:, 0:nb]), [d1])
            rs3 = q_rs[:, 0:nb].rearrange("p (b o) -> p b o", o=1).broadcast_to(
                [128, nb, 16])
            u3 = q_u[:, 0:W].rearrange("p (b k) -> p b k", k=16)
            d3 = go(lambda g: g.tensor_tensor(u3, s3, rs3, ALU.mult),
                    deps + [d2])
            d4 = so(lambda s: s.activation(out=q_sq[:, 0:W], in_=q_u[:, 0:W],
                                           func=AF.Square), [d3])
            d5 = so(lambda s: s.activation(out=q_sg[:, 0:W], in_=q_u[:, 0:W],
                                           func=AF.Sign), [d3])
            tmp = src
            e = go(lambda g: g.tensor_scalar(q_z[:, 0:W], q_sq[:, 0:W], T2[0],
                                             None, ALU.is_gt), [d4, d3])
            for th in T2[1:4]:
                t1 = go(lambda g, t_=th: g.tensor_scalar(
                    tmp[:, 0:W], q_sq[:, 0:W], t_, None, ALU.is_gt),
                    [d4, d3, e])
                e = go(lambda g: g.tensor_tensor(q_z[:, 0:W], q_z[:, 0:W],
                                                 tmp[:, 0:W], ALU.add), [t1, e])
            f = go(lambda g: g.tensor_scalar(q_zc[:, 0:W], q_sq[:, 0:W], T2[4],
                                             None, ALU.is_gt), [d4])
            for th in (T2[5], T2[6], T2[6]):
                t1 = go(lambda g, t_=th: g.tensor_scalar(
                    tmp[:, 0:W], q_sq[:, 0:W], t_, None, ALU.is_gt), [f])
                f = go(lambda g: g.tensor_tensor(q_zc[:, 0:W], q_zc[:, 0:W],
                                                 tmp[:, 0:W], ALU.add),
                       [t1, f])
            g_ = go(lambda g: g.tensor_scalar(q_z[:, 0:W], q_z[:, 0:W], 0.5,
                                              None, ALU.mult), [e, f])
            g1 = go(lambda g: g.tensor_tensor(q_sq[:, 0:W], q_z[:, 0:W],
                                              q_zc[:, 0:W], ALU.add), [g_])
            g2 = go(lambda g: g.tensor_tensor(q_u[:, 0:W], q_sq[:, 0:W],
                                              q_sg[:, 0:W], ALU.mult),
                    [g1, d5])
            sc3 = q_sc[:, 0:nb].rearrange("p (b o) -> p b o", o=1).broadcast_to(
                [128, nb, 16])
            o3 = out[:, 0:W].rearrange("p (b k) -> p b k", k=16)
            return go(lambda g: g.tensor_tensor(o3, u3, sc3, ALU.mult), [g2])

        dc_id = dmas(lambda e: e.dma_start(out=idf[:, :], in_=idenf[:, :]))
        dc_ib = dmas(lambda e: e.dma_start(out=idb[:, :], in_=idenb[:, :]))
        dc_mk = dmas(lambda e: e.dma_start(out=mskt[:, :], in_=maskut[:, :]))
        dc_on = dmas(lambda e: e.dma_start(out=onet[:, :], in_=onesc[:, :]))
        dc_wb = dmas(lambda e: e.dma_start(out=wabt[:, :], in_=Wab[:, :]))
        dc_aj = dmas(lambda e: e.dma_start(out=adjt[:, :], in_=adjc[:, :]))
        dc_y0 = dmas(lambda e: e.dma_start(out=y0t[:, :], in_=y0fc[:, :]))

        esA = ExitStack()

        def sbA(n, s, d=F32):
            return esA.enter_context(nc.sbuf_tensor(n, s, d)).ap()

        WT = sbA("WT", [128, 8 * 2560], BF16)
        WoT = sbA("WoT", [128, 8 * D], BF16)
        sm_b = sbA("sm_b", [8, 2 * SCORE], F32)
        sm_u = sbA("sm_u", [9, 2 * SCORE], F32)
        rhse = sbA("rhse", [9, 1024], F32)
        trnP = [sbA("trn0", [8, 768], F32), sbA("trn1", [8, 768], F32)]
        trn = trnP[0]
        lnZt = sbA("lnZt", [8, 2], F32)
        Pst = sbA("Pst", [8, 2 * D], F32)
        Pbf = sbA("Pbf", [8, 2 * D], BF16)
        dAt = sbA("dAt", [8, 2], F32)
        kdbfP = [sbA("kdbf0", [8, 128], BF16), sbA("kdbf1", [8, 128], BF16)]
        kdbf = kdbfP[0]
        ccs = sbA("ccs", [128, 1032], F32)
        ccj = sbA("ccj", [8, 2 * 1032], F32)
        vvtP = [sbA("vvt0", [128, D], BF16), sbA("vvt1", [128, D], BF16)]
        vvt = vvtP[0]
        accRP = [sbA("accR0", [128, 8], F32), sbA("accR1", [128, 8], F32)]
        accR = accRP[0]
        DtokP = [sbA("Dtok0", [128, 8], F32), sbA("Dtok1", [128, 8], F32)]
        Dtok = DtokP[0]
        rDtkP = [sbA("rDtk0", [128, 8], F32), sbA("rDtk1", [128, 8], F32)]
        rDtk = rDtkP[0]
        eKTP = [sbA("eKT0", [128, 8], F32), sbA("eKT1", [128, 8], F32)]
        eKT = eKTP[0]
        wclTP = [sbA("wclT0", [128, 8], BF16), sbA("wclT1", [128, 8], BF16)]
        wclT = wclTP[0]
        pvTP = [sbA("pvT0", [128, 64], F32), sbA("pvT1", [128, 64], F32)]
        pvT = pvTP[0]
        RhP = [sbA("Rh0", [128, 1024], F32), sbA("Rh1", [128, 1024], F32)]
        Rh = RhP[0]
        RTsP = [sbA("RTs0", [128, 128], BF16), sbA("RTs1", [128, 128], BF16)]
        RTs = RTsP[0]
        yTs = sbA("yTs", [128, D], BF16)
        sxP = [sbA("sx0", [128, 1], F32), sbA("sx1", [128, 1], F32)]
        sx = sxP[0]
        sxxP = [sbA("sxx0", [128, 1], F32), sbA("sxx1", [128, 1], F32)]
        sxx = sxxP[0]
        mntP = [sbA("mnt0", [128, 1], F32), sbA("mnt1", [128, 1], F32)]
        mnt = mntP[0]
        vrtP = [sbA("vrt0", [128, 1], F32), sbA("vrt1", [128, 1], F32)]
        vrt = vrtP[0]
        rstdP = [sbA("rstd0", [128, 1], F32), sbA("rstd1", [128, 1], F32)]
        rstd = rstdP[0]
        mbnP = [sbA("mbn0", [128, 1], F32), sbA("mbn1", [128, 1], F32)]
        mbn = mbnP[0]

        wsl = sbA("wsl", [128, 1024], F32)
        wq_oP = [sbA("wq_o0", [128, 1024], BF16),
                 sbA("wq_o1", [128, 1024], BF16)]
        wq_o = wq_oP[0]
        wq_p = sbA("wq_p", [128, 1024], BF16)
        xnt = sbA("xnt", [128, D], BF16)
        xnT = sbA("xnT", [128, D], BF16)
        kev = sbA("kev", [128, 512], F32)
        qkt = sbA("qkt", [128, 512], F32)
        spe = sbA("spe", [128, 512], F32)
        batP = [sbA("bat0", [128, 16], F32), sbA("bat1", [128, 16], F32)]
        xqT = sbA("xqT", [128, D], BF16)

        mzc = [go(lambda g: g.memset(cst[:, 0:1], 1e-5)),
               go(lambda g: g.memset(cst[:, 1:2], 1.0))]
        mz = [go(lambda g: g.memset(sm_u[:, :], 1.0)),
              go(lambda g: g.memset(lnZt[:, :], NEG)),
              go(lambda g: g.memset(Pst[:, :], 0.0)),
              go(lambda g: g.memset(dAt[:, :], 0.0))]
        dc_re = dmas(lambda e: e.dma_start(out=rhse[:, :],
                                           in_=rhsec[:, :]))

        prevw = None
        prev_wcb = None
        for (Wsrc, nr, kind) in (() if _os.environ.get("MK_SKIPW") else
                                 ((Wq, 512, 0), (Wk, 512, 1), (Wa, 512, 2),
                                  (Wv, D, 3), (Wo, D, 4))):
            for r in range(nr // 128):
                dld = dmas(lambda e, Ws=Wsrc, rr=r: e.dma_start(
                    out=wsl[:, 0:1024], in_=Ws[rr * 128:(rr + 1) * 128, :]),
                    [prev_wcb])
                wcb = vo(lambda v: v.tensor_copy(wq_o[:, 0:1024],
                                                 wsl[:, 0:1024]),
                         [dld, prevw])
                prev_wcb = wcb
                pieces = []
                for dc in range(8):
                    if kind < 4:
                        col = dc * 2560 + kind * 512 + r * 128
                        tgt = WT
                    else:
                        col = dc * D + r * 128
                        tgt = WoT
                    pieces.append((wq_o[:, dc * 128:(dc + 1) * 128],
                                   (lambda s, ps, cl=col, tg=tgt:
                                    s.activation(out=tg[:, cl:cl + 128],
                                                 in_=ps, func=AF.Copy))))
                evs = twave(pieces, [wcb])
                prevw = evs[-1]
        wA_last = prevw if prevw is not None else dc_id

        wqs = {"q": wA_last, "pd": None, "d3": None, "ev": None, "dq": {}}
        wq_emits = []

        def _wq_front(sl, Wsrc, rr, off, w_):
            wo = wq_oP[sl % 2]
            dld = dmas(lambda e, Ws=Wsrc: e.dma_start(
                out=wsl[:, 0:w_],
                in_=Ws[rr * 128:(rr + 1) * 128, off:off + w_]),
                [wqs["d3"] if wqs["d3"] is not None else wqs["q"]])
            dq = emit_quant(w_, wsl, wo, [dld],
                            outguard=([wqs["ev"]] if wqs["ev"] else []),
                            track=wqs)
            wqs["q"] = dq
            wqs["dq"][sl] = dq

        def _wq_back(sl, Wsrc, rr, off, w_, dst, colbase):
            wo = wq_oP[sl % 2]
            dq = wqs["dq"][sl]
            npc = w_ // 128
            evs = []
            for w0 in range(0, npc, 8):
                pieces = []
                for dcl in range(w0, min(w0 + 8, npc)):
                    pieces.append((wo[:, dcl * 128:(dcl + 1) * 128],
                                   (lambda s, ps, d2=dcl:
                                    s.activation(
                        out=wq_p[:, d2 * 128:(d2 + 1) * 128],
                        in_=ps, func=AF.Copy))))
                evs += twave(pieces,
                             [dq] + ([wqs["pd"]] if wqs["pd"] else []))
            wqs["ev"] = evs[-1]
            for dcl in range(npc):
                dc = off // 128 + dcl
                wqs["pd"] = dmag(lambda e, ds=dst, dcc=dc, d2=dcl,
                                 cb=colbase: e.dma_start(
                    out=ds[dcc, :, cb + rr * 128:cb + rr * 128 + 128],
                    in_=wq_p[:, d2 * 128:(d2 + 1) * 128]),
                    [evs[dcl]] + ([wqs["pd"]] if wqs["pd"] else []))

        for (Wsrc, nr, wc, dst, colbase) in () if (
                PHASE < 2 or _os.environ.get("MK_SKIPW")) else (
                (Wg, I_FF, D, s_wgu, 0), (Wu, I_FF, D, s_wgu, I_FF),
                (Wd, D, I_FF, s_wd, 0)):
            slices = ([(0, 1024), (1024, 1024), (2048, 768)]
                      if wc == 2816 else [(0, wc)])
            for r in range(nr // 128):
                for (off, w_) in slices:
                    wq_emits.append((Wsrc, r, off, w_, dst, colbase))

        wq_i = {"f": 0, "b": 0}

        def wq_pump(frac):
            tgt = min(len(wq_emits), int(frac * len(wq_emits) + 0.999))
            while wq_i["f"] < tgt:
                sl = wq_i["f"]
                (Ws, r, off, w_, ds, cb) = wq_emits[sl]
                _wq_front(sl, Ws, r, off, w_)
                wq_i["f"] += 1
                if wq_i["f"] >= 2 and wq_i["b"] < wq_i["f"] - 1:
                    sb_ = wq_i["b"]
                    (Ws, r, off, w_, ds, cb) = wq_emits[sb_]
                    _wq_back(sb_, Ws, r, off, w_, ds, cb)
                    wq_i["b"] += 1
            if frac >= 1.0:
                while wq_i["b"] < wq_i["f"]:
                    sb_ = wq_i["b"]
                    (Ws, r, off, w_, ds, cb) = wq_emits[sb_]
                    _wq_back(sb_, Ws, r, off, w_, ds, cb)
                    wq_i["b"] += 1

        psA_free = [wA_last]
        psB_free = []
        vv_st = {}
        chain = list(mz)
        chainA = {0: list(mz), 1: list(mz)}
        prev_dvv = None
        prev_pmm = []
        prev_ar = []
        prev_xnd = []
        pend0 = [[], []]
        first_l2 = mzc[0]
        A0S = {}
        freeA = list(psA_free)
        freeB01 = []
        freeB2A = []

        def a0_tiles(cc):
            pc = cc % 2
            return (trnP[pc], vvtP[pc], sxP[pc], sxxP[pc], mntP[pc],
                    vrtP[pc], rstdP[pc], mbnP[pc], batP[pc],
                    xtP[pc], wclTP[pc], pvTP[pc])

        def a0_alpha(cc, pc):
            nonlocal prev_dvv, prev_pmm, prev_ar, first_l2, prev_xnd
            bi, c = cc // NCH, cc % NCH
            chain = chainA[bi]
            (trn, vvt, sx, sxx, mnt, vrt, rstd, mbn, bat,
             xt, wclT, pvT) = a0_tiles(cc)
            pguard = pend0[pc]
            st = {}
            A0S[cc] = st
            row0 = bi * SCORE + c * C
            dx = dmas(lambda e, r0=row0: e.dma_start(
                out=xt[:, :], in_=x_sl[r0:r0 + 128, :]), pguard)
            l1 = vof(lambda v: v.tensor_reduce(sx[:, :], xt[:, :], AX.X,
                                              ALU.add), [dx])
            l2 = so(lambda s: s.activation(out=q_sq[:, 1024:2048],
                                           in_=xt[:, :],
                                           func=AF.Square,
                                           accum_out=sxx[:, :]),
                    [dx, first_l2] + chain[-1:])
            first_l2 = l2
            l3 = vof(lambda v: v.tensor_scalar(mnt[:, :], sx[:, :], 1.0 / D,
                                              None, ALU.mult), [l1])
            l4 = vof(lambda v: v.tensor_scalar(vrt[:, :], sxx[:, :], 1.0 / D,
                                              None, ALU.mult), [l2])
            l5 = vof(lambda v: v.tensor_tensor(sx[:, :], mnt[:, :], mnt[:, :],
                                              ALU.mult), [l3])
            l6 = vof(lambda v: v.tensor_tensor(vrt[:, :], vrt[:, :], sx[:, :],
                                              ALU.subtract), [l4, l5])
            l7 = so(lambda s: s.activation(out=vrt[:, :], in_=vrt[:, :],
                                           func=AF.Ln, bias=cst[:, 0:1]),
                    [l6] + mzc)
            l8 = so(lambda s: s.activation(out=rstd[:, :], in_=vrt[:, :],
                                           func=AF.Exp, scale=-0.5), [l7])
            l9 = vof(lambda v: v.tensor_tensor(mbn[:, :], mnt[:, :],
                                              rstd[:, :], ALU.mult), [l8])
            l10 = vof(lambda v: v.tensor_scalar(mbn[:, :], mbn[:, :], -1.0,
                                               None, ALU.mult), [l9])
            l11 = vo(lambda v: v.tensor_scalar(xnt[:, :], xt[:, :],
                                               rstd[:, :], mbn[:, :],
                                               ALU.mult, ALU.add),
                      [l10, l8] + prev_xnd)
            xnd = twave([(xnt[:, dc * 128:(dc + 1) * 128],
                          (lambda s, ps, d2=dc: s.activation(
                              out=xnT[:, d2 * 128:(d2 + 1) * 128],
                              in_=ps, func=AF.Copy)))
                         for dc in range(8)], [l11] + prev_pmm)
            pb = mm(psA[:, 1024:1536], onet[0:1, :], wabt[0:1, :], True, False,
                    [dc_on, dc_wb] + xnd + freeA)
            pmm = []
            for dc in range(8):
                lT = xnT[:, dc * 128:(dc + 1) * 128]
                bs = dc * 2560
                for si, (oap, wcl) in enumerate((
                        (psA[:, 0:512], bs), (psA[:, 512:1024], bs + 512),
                        (psA[:, 1024:1536], bs + 1024),
                        (psB[:, 0:512], bs + 1536),
                        (psB[:, 512:1024], bs + 2048))):
                    dd = [pb] + xnd + (freeB01 if si >= 3 else [])
                    pmm.append(mm(oap, lT, WT[:, wcl:wcl + 512],
                                  (dc == 0) and si != 2, dc == 7, dd))
            kd_ = so(lambda s: s.activation(out=kev[:, :],
                                            in_=psA[:, 512:1024],
                                            func=AF.Copy), pmm)
            qk = vo(lambda v: v.tensor_tensor(qkt[:, :], psA[:, 0:512],
                                              kev[:, :], ALU.mult),
                    [kd_] + pmm)
            qk3 = qkt[:, :].rearrange("p (h k) -> p h k", k=64)
            br = vof(lambda v: v.tensor_reduce(bat[:, 0:8], qk3, AX.X, ALU.add),
                    [qk])
            sp1 = so(lambda s: s.activation(out=spe[:, :],
                                            in_=psA[:, 1024:1536],
                                            func=AF.Exp), pmm + prev_ar)
            sp2 = so(lambda s: s.activation(out=spe[:, :], in_=spe[:, :],
                                            func=AF.Ln, bias=cst[:, 1:2]),
                     [sp1] + mzc)
            sp3 = spe[:, :].rearrange("p (h k) -> p h k", k=64)
            ar = vof(lambda v: v.tensor_reduce(bat[:, 8:16], sp3, AX.X, ALU.add,
                                              negate=True), [sp2])
            vvd = []
            for (pp, c0, o0) in ((psB, 0, 0), (psB, 512, 512)):
                sg1 = so(lambda s, p_=pp, a_=c0: s.activation(
                    out=q_sg[:, 1024:1536], in_=p_[:, a_:a_ + 512],
                    func=AF.Sign),
                    pmm + [qk, sp1, kd_] + vvd +
                    ([prev_dvv] if prev_dvv else []))
                sg2 = so(lambda s: s.mul(q_sg[:, 1024:1536],
                                         q_sg[:, 1024:1536], C1V), [sg1])
                vvd.append(vo(lambda v, p_=pp, a_=c0, b_=o0:
                              v.scalar_tensor_tensor(
                    vvt[:, b_:b_ + 512], p_[:, a_:a_ + 512], C2V,
                    q_sg[:, 1024:1536], ALU.mult, ALU.add), [sg2]))
            dvv = dmag(lambda e, ccx=cc: e.dma_start(
                out=s_vv[ccx, :, :], in_=vvt[:, :]), vvd)
            if cc == NCC - 1 and _os.environ.get("MK_DBG"):
                S_.add("sync", lambda e: e.dma_start(
                    out=d_vvl[:, :], in_=vvt[:, :]), vvd + [dvv], is_dma=True)
            vv_st[cc] = dvv
            prev_dvv = dvv
            prev_pmm = pmm[-1:]
            prev_ar = [ar]
            prev_xnd = xnd[-1:]
            st.update(l11=l11, pmm=pmm, kd_=kd_, qk=qk, br=br, sp1=sp1,
                      sp2=sp2, ar=ar, vvd=vvd, dvv=dvv)

        def a0_beta(cc, pc):
            nonlocal psA_free, psB_free, freeA, freeB01, freeB2A
            bi, c = cc // NCH, cc % NCH
            chain = chainA[bi]
            (trn, vvt, sx, sxx, mnt, vrt, rstd, mbn, bat,
             xt, wclT, pvT) = a0_tiles(cc)
            st = A0S[cc]
            l11, pmm, kd_, qk, br, sp1, sp2, ar, vvd, dvv = (
                st["l11"], st["pmm"], st["kd_"], st["qk"], st["br"],
                st["sp1"], st["sp2"], st["ar"], st["vvd"], st["dvv"])
            gc = bi * SCORE + c * C
            uc = slice(gc, gc + C)
            bt = mm(psB[0:8, 1024:1152], bat[:, 0:8], idf[:, :],
                    True, True, [br, dc_id] + freeB2A, tr=True)
            at = mm(psB[0:8, 1152:1280], bat[:, 8:16], idf[:, :],
                    True, True, [ar, dc_id] + freeB2A, tr=True)
            acp = vof(lambda v: v.tensor_copy(
                trn[0:8, 384:512], psB[0:8, 1152:1280]), [at])
            Acs = vof(lambda v: v.tensor_tensor_scan(
                trn[0:8, 0:128], trn[0:8, 384:512],
                trn[0:8, 384:512], 0.0, ALU.add, ALU.bypass), [acp])
            ucv = vof(lambda v: v.scalar_tensor_tensor(
                trn[0:8, 128:256], psB[0:8, 1024:1152], SCL_B,
                trn[0:8, 0:128], ALU.mult, ALU.subtract), [Acs, bt])
            bsc = vof(lambda v: v.tensor_scalar(
                trn[0:8, 512:640], psB[0:8, 1024:1152], SCL_B,
                None, ALU.mult), [bt])
            du = dmag(lambda e, u_=uc: e.dma_start(
                out=sm_u[0:8, u_], in_=trn[0:8, 128:256]),
                [ucv] + mz[0:2])
            db = dmag(lambda e, u_=uc: e.dma_start(
                out=sm_b[0:8, u_], in_=trn[0:8, 512:640]), [bsc])
            nl = vof(lambda v: v.tensor_scalar(
                trn[0:8, 640:641], trn[0:8, 255:256], -1.0, None,
                ALU.mult), [ucv])
            wx = so(lambda s: s.activation(
                out=trn[0:8, 256:384], in_=trn[0:8, 128:256],
                func=AF.Exp, bias=trn[0:8, 640:641]), [nl])
            rz = vof(lambda v: v.tensor_reduce(
                trn[0:8, 641:642], trn[0:8, 256:384], AX.X,
                ALU.add), [wx])
            ek = so(lambda s, b2=bi: s.activation(
                out=trn[0:8, 642:643], in_=lnZt[0:8, b2:b2 + 1],
                func=AF.Exp, bias=trn[0:8, 640:641]), [nl] + chain)
            dl = vof(lambda v: v.tensor_tensor(
                trn[0:8, 643:644], trn[0:8, 641:642],
                trn[0:8, 642:643], ALU.add), [rz, ek])
            rdl = vof(lambda v: v.reciprocal(
                trn[0:8, 644:645], trn[0:8, 643:644]), [dl])
            kd0 = vof(lambda v: v.tensor_tensor(
                trn[0:8, 645:646], trn[0:8, 642:643],
                trn[0:8, 644:645], ALU.mult), [rdl])
            wn = vof(lambda v: v.tensor_scalar(
                trn[0:8, 256:384], trn[0:8, 256:384],
                trn[0:8, 644:645], None, ALU.mult), [rdl, rz])
            wnt = mm(psB[:, 1280:1288], trn[0:8, 256:384], idf[0:8, 0:8],
                     True, True, [wn, dc_id, bsc, db] + freeB2A, tr=True)
            wne = vo(lambda v: v.tensor_copy(wclT[:, :], psB[:, 1280:1288]),
                     [wnt])
            pvm = []
            for dc in range(8):
                pvm.append(mm(psB[:, 1288 + dc * 8:1296 + dc * 8],
                              vvt[:, dc * 128:(dc + 1) * 128], wclT[:, :],
                              True, True, [wne] + vvd + freeB2A))
            pve = vo(lambda v: v.tensor_copy(pvT[:, :], psB[:, 1288:1352]),
                     pvm)
            ptm = []
            for dc in range(8):
                ptm.append(mm(psA[0:8, dc * 128:(dc + 1) * 128],
                              pvT[:, dc * 8:(dc + 1) * 8], idf[:, :],
                              True, True,
                              [pve, dc_id, kd_, qk, sp1] + vvd, tr=True))
            pup = vo(lambda v, b2=bi: v.scalar_tensor_tensor(
                Pst[0:8, b2 * D:(b2 + 1) * D], Pst[0:8, b2 * D:(b2 + 1) * D],
                trn[0:8, 645:646], psA[0:8, 0:1024], ALU.mult,
                ALU.add), ptm + [kd0] + chain)
            lnd = so(lambda s: s.activation(
                out=trn[0:8, 646:647], in_=trn[0:8, 643:644],
                func=AF.Ln), [dl])
            lnu = vo(lambda v, b2=bi: v.tensor_tensor(
                lnZt[0:8, b2:b2 + 1], trn[0:8, 639:640],
                trn[0:8, 646:647], ALU.add), [lnd, bsc, ek] + chain)
            dau = vo(lambda v, b2=bi: v.tensor_tensor(
                dAt[0:8, b2:b2 + 1], dAt[0:8, b2:b2 + 1],
                trn[0:8, 127:128], ALU.add), [Acs, ucv] + chain)
            chainA[bi] = [pup, lnu, dau]
            freeA = [pup, kd_, qk, sp1]
            freeB01 = list(vvd)
            freeB2A = [Acs, ucv, bsc, wne, pve, acp, pup]
            psA_free = [pup, kd_, qk, sp1]
            psB_free = [Acs, ucv, bsc, wne, pve, acp] + vvd
            pend0[pc] = [pup, lnu, dau, dvv, du, db, pve, wne, l11, qk,
                         kd_, sp2, ar]

        NA0 = 0 if _os.environ.get('MK_NOA0') else (NCC if PHASE >= 3 else 0)
        seq0 = [bi * NCH + c for c in range(NCH) for bi in (0, 1)][:NA0]
        for k, cc in enumerate(seq0):
            if k > 0:
                a0_beta(seq0[k - 1], (k - 1) % 2)
            wq_pump((k + 1.0) / max(NA0, 1))
            a0_alpha(cc, k % 2)
        if NA0:
            a0_beta(seq0[-1], (len(seq0) - 1) % 2)
        wq_pump(1.0)
        chain = chainA[0][-3:] + chainA[1][-3:]
        prevq = wqs["q"]
        wq_last = wqs["pd"] if wqs["pd"] is not None else wA_last

        mzp = go(lambda g: g.memset(trn[0:8, 700:706], 0.0))
        dis = [] if PHASE < 4 else [
            dmag(lambda e: e.dma_start(out=cc_in[0:8, 1026:1032],
                                       in_=trn[0:8, 700:706]), [mzp] + chain),
            dmag(lambda e: e.dma_start(out=cc_in[8:16, 1026:1032],
                                       in_=trn[0:8, 700:706]), [mzp] + chain),
            dmag(lambda e: e.dma_start(out=cc_in[0:8, 0:1],
                                       in_=lnZt[0:8, 0:1],
                                       allow_slow_non_contiguous=True), chain),
            dmag(lambda e: e.dma_start(out=cc_in[8:16, 0:1],
                                       in_=lnZt[0:8, 1:2],
                                       allow_slow_non_contiguous=True), chain),
            dmag(lambda e: e.dma_start(out=cc_in[0:8, 1:2],
                                       in_=dAt[0:8, 0:1],
                                       allow_slow_non_contiguous=True), chain),
            dmag(lambda e: e.dma_start(out=cc_in[8:16, 1:2],
                                       in_=dAt[0:8, 1:2],
                                       allow_slow_non_contiguous=True), chain),
            dmag(lambda e: e.dma_start(out=cc_in[0:8, 2:1026],
                                       in_=Pst[0:8, 0:D]), chain),
            dmag(lambda e: e.dma_start(out=cc_in[8:16, 2:1026],
                                       in_=Pst[0:8, D:2 * D]), chain),
        ]
        if _os.environ.get("MK_NOCOL"):
            col = None if PHASE < 4 else go(
                lambda g: g.memset(ccs[0:1, 0:1], 0.0), dis)
        else:
            col = None if PHASE < 4 else S_.add(
                "gpsimd", lambda g: g.collective_compute(
                    "AllGather", ALU.bypass,
                    replica_groups=[list(range(NCORE))],
                    ins=[cc_in[:, :]], outs=[cc_out[:, :]]), dis, force=True)
        dco = None if PHASE < 4 else dmag(
            lambda e: e.dma_start(out=ccs[:, :], in_=cc_out[:, :]), [col])
        aj1 = None if PHASE < 4 else vof(lambda v: v.tensor_tensor(ccs[:, 0:1], ccs[:, 0:1],
                                           adjt[:, 0:1], ALU.add),
                 [dco, dc_aj])
        aj2 = None if PHASE < 4 else vof(lambda v: v.tensor_tensor(ccs[:, 1:2], ccs[:, 1:2],
                                           adjt[:, 1:2], ALU.mult),
                 [dco, dc_aj])
        rs1 = None if PHASE < 4 else go(lambda g: g.memset(lnZt[:, :], NEG),
                                        chain + dis)
        rs2 = None if PHASE < 4 else go(lambda g: g.memset(Pst[:, :], 0.0),
                                        chain + dis)
        fold = [rs1, rs2, aj1, aj2] if PHASE >= 4 else list(chain)
        for j in range(0 if _os.environ.get('MK_NOF') else (NCORE - 1 if PHASE >= 4 else 0)):
            dj1 = dmag(lambda e, jj=j: e.dma_start(
                out=ccj[0:8, 0:1032], in_=ccs[16 * jj:16 * jj + 8, :]), fold)
            dj2 = dmag(lambda e, jj=j: e.dma_start(
                out=ccj[0:8, 1032:2064],
                in_=ccs[16 * jj + 8:16 * jj + 16, :]), fold)
            nfold = []
            for bi in range(2):
                co = bi * 1032
                f1 = vof(lambda v, c_=co, b2=bi: v.tensor_tensor(
                    trn[0:8, 0:1], lnZt[0:8, b2:b2 + 1],
                    ccj[0:8, c_ + 1:c_ + 2], ALU.add), [dj1, dj2] + fold)
                f2 = vof(lambda v, c_=co: v.tensor_tensor(
                    trn[0:8, 1:2], trn[0:8, 0:1], ccj[0:8, c_:c_ + 1],
                    ALU.max), [f1])
                f3a = vof(lambda v: v.tensor_tensor(
                    trn[0:8, 2:3], trn[0:8, 0:1], trn[0:8, 1:2],
                    ALU.subtract), [f2])
                f3b = vof(lambda v, c_=co: v.tensor_tensor(
                    trn[0:8, 3:4], ccj[0:8, c_:c_ + 1], trn[0:8, 1:2],
                    ALU.subtract), [f2])
                f4 = so(lambda s: s.activation(out=trn[0:8, 4:5],
                                               in_=trn[0:8, 2:3],
                                               func=AF.Exp), [f3a])
                f5 = so(lambda s: s.activation(out=trn[0:8, 5:6],
                                               in_=trn[0:8, 3:4],
                                               func=AF.Exp), [f3b])
                f6 = vof(lambda v: v.tensor_tensor(trn[0:8, 6:7], trn[0:8, 4:5],
                                                  trn[0:8, 5:6], ALU.add),
                        [f4, f5])
                f7 = so(lambda s: s.activation(out=trn[0:8, 7:8],
                                               in_=trn[0:8, 6:7],
                                               func=AF.Ln), [f6])
                f8 = vof(lambda v, b2=bi: v.tensor_tensor(
                    lnZt[0:8, b2:b2 + 1], trn[0:8, 1:2], trn[0:8, 7:8],
                    ALU.add), [f7])
                f9a = vof(lambda v, b2=bi: v.tensor_tensor(
                    trn[0:8, 8:9], trn[0:8, 0:1], lnZt[0:8, b2:b2 + 1],
                    ALU.subtract), [f8])
                f9b = vof(lambda v, c_=co, b2=bi: v.tensor_tensor(
                    trn[0:8, 9:10], ccj[0:8, c_:c_ + 1],
                    lnZt[0:8, b2:b2 + 1], ALU.subtract), [f8])
                fa = so(lambda s: s.activation(out=trn[0:8, 10:11],
                                               in_=trn[0:8, 8:9],
                                               func=AF.Exp), [f9a])
                fb = so(lambda s: s.activation(out=trn[0:8, 11:12],
                                               in_=trn[0:8, 9:10],
                                               func=AF.Exp), [f9b])
                fc = vof(lambda v, c_=co: v.tensor_scalar(
                    ccj[0:8, c_ + 2:c_ + 1026], ccj[0:8, c_ + 2:c_ + 1026],
                    trn[0:8, 11:12], None, ALU.mult), [fb])
                fd = vo(lambda v, c_=co, b2=bi: v.scalar_tensor_tensor(
                    Pst[0:8, b2 * D:(b2 + 1) * D],
                    Pst[0:8, b2 * D:(b2 + 1) * D], trn[0:8, 10:11],
                    ccj[0:8, c_ + 2:c_ + 1026], ALU.mult, ALU.add), [fa, fc])
                nfold += [fd, f8]
            fold = nfold

        a0_tail = (list(chain) + psA_free + psB_free + pend0[0] + pend0[1]
                   + [prevq, wq_last])
        chain = list(fold)
        chainB = {0: list(fold), 1: list(fold)}
        prev_emm = []
        prev_x2L = list(psB_free)
        prev_x2 = None
        prev_sx2 = None
        prev_xq = None
        pend1 = [list(a0_tail), list(a0_tail)]
        prev_dxq = []
        prev_m2 = []

        freeA01 = list(psA_free)
        freeA2 = list(psA_free)
        freeB2 = list(psB_free)
        prev_pups = list(psB_free)
        prev_yg = []
        A1S = {}

        def a1_tiles(pc):
            return (trnP[pc], vvtP[pc], accRP[pc], DtokP[pc], rDtkP[pc],
                    eKTP[pc], wclTP[pc], pvTP[pc], RhP[pc], RTsP[pc],
                    sxP[pc], sxxP[pc], mntP[pc], vrtP[pc],
                    rstdP[pc], mbnP[pc], batP[pc], xtP[pc],
                    kdbfP[pc])

        def a1_alpha(cc, pc):
            bi, c = cc // NCH, cc % NCH
            chain = chainB[bi]
            (trn, vvt, accR, Dtok, rDtk, eKT, wclT, pvT, Rh, RTs, sx,
             sxx, mnt, vrt, rstd, mbn, bat,
             xt, kdbf) = a1_tiles(pc)
            pguard = pend1[pc]
            st = {}
            A1S[cc] = st
            gc = bi * SCORE + c * C
            uc = slice(gc, gc + C)
            dvv = dmag(lambda e, ccx=cc: e.dma_start(
                out=vvt[:, :], in_=s_vv[ccx, :, :]),
                [vv_st[cc]] + pguard)
            duf = dmag(lambda e, u_=uc: e.dma_start(
                out=rhse[8:9, 0:1024], in_=sm_u[0:8, u_]),
                [dc_re] + prev_emm + pguard)
            pbf = vo(lambda v: v.tensor_copy(Pbf[0:8, :], Pst[0:8, :]),
                     chain + prev_yg)
            nl1 = vof(lambda v, u_=uc: v.tensor_scalar(
                trn[0:8, 0:128], sm_u[0:8, u_], -1.0, None,
                ALU.mult), chain)
            emm = []
            for h in range(8):
                emm.append(mm(psA[:, h * 128:(h + 1) * 128],
                              sm_u[0:9, uc],
                              rhse[0:9, h * 128:(h + 1) * 128],
                              h % 4 == 0, False, [duf] + freeA01))
            for h in range(8):
                emm.append(mm(psA[:, h * 128:(h + 1) * 128], idf[:, :],
                              mskt[:, :], False, h % 4 == 3,
                              [dc_id, dc_mk, emm[h]]))
            exd = []
            for h in range(8):
                stopper = emm[8 + (3 if h < 4 else 7)]
                exd.append(so(lambda s, hh=h: s.activation(
                    out=Rh[:, hh * 128:(hh + 1) * 128],
                    in_=psA[:, hh * 128:(hh + 1) * 128], func=AF.Exp,
                    accum_out=accR[:, hh:hh + 1]), [emm[8 + h], stopper]))
            ekh = so(lambda s, b2=bi: s.activation(
                out=trn[0:8, 128:256], in_=trn[0:8, 0:128],
                func=AF.Exp, bias=lnZt[0:8, b2:b2 + 1]), [nl1] + chain)
            ekt = mm(psB[:, 1024:1032], trn[0:8, 128:256], idf[0:8, 0:8],
                     True, True, [ekh, dc_id] + freeB2, tr=True)
            eke = vof(lambda v: v.tensor_copy(eKT[:, :], psB[:, 1024:1032]),
                     [ekt])
            dtk = vof(lambda v: v.tensor_tensor(Dtok[:, :], accR[:, :],
                                               eKT[:, :], ALU.add),
                     exd + [eke])
            rdt = vof(lambda v: v.reciprocal(rDtk[:, :], Dtok[:, :]), [dtk])
            nrm = []
            for h in range(8):
                nrm.append(vo(lambda v, hh=h: v.tensor_scalar(
                    Rh[:, hh * 128:(hh + 1) * 128],
                    Rh[:, hh * 128:(hh + 1) * 128], rDtk[:, hh:hh + 1], None,
                    ALU.mult), [rdt, exd[h]]))
            dtt = mm(psB[0:8, 1032:1160], Dtok[:, :], idf[:, :],
                     True, True, [dtk, dc_id, rdt] + freeB2, tr=True)
            kdt = vof(lambda v: v.tensor_tensor(eKT[:, :], eKT[:, :],
                                               rDtk[:, :], ALU.mult),
                     [rdt, dtk])
            rdl = vof(lambda v: v.reciprocal(
                trn[0:8, 648:649], psB[0:8, 1159:1160]), [dtt])
            lnd = so(lambda s: s.activation(
                out=trn[0:8, 650:651], in_=psB[0:8, 1159:1160],
                func=AF.Ln), [dtt])
            kdm = mm(psB[0:8, 1160:1288], eKT[:, :], idf[:, :],
                     True, True, [kdt, dc_id, rdl, lnd] + freeB2, tr=True)
            kde = vo(lambda v: v.tensor_copy(
                kdbf[0:8, :], psB[0:8, 1160:1288]), [kdm])
            kdl = vof(lambda v: v.tensor_copy(
                trn[0:8, 647:648], psB[0:8, 1287:1288]), [kdm])
            rtm = []
            for h in range(8):
                rtm.append(mm(psA[:, 1024:1152],
                              Rh[:, h * 128:(h + 1) * 128], idf[:, :],
                              h == 0, h == 7,
                              [nrm[h], dc_id] + freeA2, tr=True))
            rte = so(lambda s: s.activation(out=RTs[:, :],
                                            in_=psA[:, 1024:1152],
                                            func=AF.Copy), rtm)
            lnu = vo(lambda v, u_=uc, b2=bi: v.tensor_tensor(
                lnZt[0:8, b2:b2 + 1],
                sm_b[0:8, u_.start + 127:u_.start + 128],
                trn[0:8, 650:651], ALU.add), [lnd, ekh] + chain)
            prev_emm[:] = emm[7:8]
            st.update(dvv=dvv, exd=exd, nrm=nrm, rte=rte, kde=kde, kdl=kdl,
                      rdl=rdl, lnd=lnd, eke=eke, dtk=dtk, kdt=kdt, lnu=lnu,
                      pbf=pbf, ekh=ekh)

        def a1_beta(cc, pc):
            nonlocal prev_sx2, prev_xq, prev_dxq, freeA01, freeA2, \
                freeB2, prev_pups, prev_yg, prev_m2
            bi, c = cc // NCH, cc % NCH
            chain = chainB[bi]
            (trn, vvt, accR, Dtok, rDtk, eKT, wclT, pvT, Rh, RTs, sx,
             sxx, mnt, vrt, rstd, mbn, bat,
             xt, kdbf) = a1_tiles(pc)
            pguard = pend1[pc]
            st = A1S[cc]
            dvv, exd, rte, kde, kdl, rdl, lnd, lnu, pbf = (
                st["dvv"], st["exd"], st["rte"], st["kde"], st["kdl"],
                st["rdl"], st["lnd"], st["lnu"], st["pbf"])
            row0 = bi * SCORE + c * C
            gc = bi * SCORE + c * C
            uc = slice(gc, gc + C)
            ymm = []
            yg = []
            for dc in range(8):
                m1 = mm(psB[:, dc * 128:(dc + 1) * 128],
                        Pbf[0:8, bi * D + dc * 128:bi * D + (dc + 1) * 128],
                        kdbf[0:8, :], dc % 4 == 0, False,
                        [pbf, kde, dvv] + prev_pups)
                m2_ = mm(psB[:, dc * 128:(dc + 1) * 128],
                         vvt[:, dc * 128:(dc + 1) * 128], RTs[:, :],
                         False, dc % 4 == 3, [m1, rte, dvv])
                ymm += [m1, m2_]
                yg.append(m2_)
            yev = []
            for dc in range(8):
                stopper = yg[3 if dc < 4 else 7]
                yev.append(so(lambda s, dcc=dc: s.activation(
                    out=yTs[:, dcc * 128:(dcc + 1) * 128],
                    in_=psB[:, dcc * 128:(dcc + 1) * 128], func=AF.Copy),
                    [yg[dc], stopper]))
            if c == 0:
                for dc in range(8):
                    yev.append(vo(lambda v, dcc=dc: v.tensor_scalar(
                        yTs[:, dcc * 128:dcc * 128 + 1],
                        yTs[:, dcc * 128:dcc * 128 + 1], y0t[:, 0:1], None,
                        ALU.mult), [yev[dc], dc_y0]))
            nl2 = vof(lambda v, u_=uc: v.tensor_scalar(
                trn[0:8, 649:650],
                sm_u[0:8, u_.start + 127:u_.start + 128],
                -1.0, None, ALU.mult), chain)
            wx1 = so(lambda s, u_=uc: s.activation(
                out=trn[0:8, 256:384], in_=sm_u[0:8, u_],
                func=AF.Exp, bias=trn[0:8, 649:650]), [nl2])
            wn1 = vof(lambda v: v.tensor_scalar(
                trn[0:8, 256:384], trn[0:8, 256:384],
                trn[0:8, 648:649], None, ALU.mult), [wx1, rdl])
            wt1 = mm(psB[:, 1280:1288], trn[0:8, 256:384], idf[0:8, 0:8],
                     True, True, [wn1, dc_id, lnd, kde, kdl] + freeB2,
                     tr=True)
            we1 = vo(lambda v: v.tensor_copy(wclT[:, :], psB[:, 1280:1288]),
                     [wt1])
            pvm = []
            for dc in range(8):
                pvm.append(mm(psB[:, 1288 + dc * 8:1296 + dc * 8],
                              vvt[:, dc * 128:(dc + 1) * 128], wclT[:, :],
                              True, True, [we1, dvv]))
            pve = vo(lambda v: v.tensor_copy(pvT[:, :], psB[:, 1288:1352]),
                     pvm)
            ptm2 = []
            for dc in range(8):
                ptm2.append(mm(psB[0:8, dc * 128:(dc + 1) * 128],
                               pvT[:, dc * 8:(dc + 1) * 8], idf[:, :],
                               True, True,
                               [pve, dc_id] + yev[0:8], tr=True))
            pups = [vo(lambda v, b2=bi: v.scalar_tensor_tensor(
                Pst[0:8, b2 * D:(b2 + 1) * D], Pst[0:8, b2 * D:(b2 + 1) * D],
                trn[0:8, 647:648], psB[0:8, 0:1024], ALU.mult, ALU.add),
                ptm2 + [pbf, kdl] + chain)]
            wmm = []
            for dc in range(8):
                lT = yTs[:, dc * 128:(dc + 1) * 128]
                for si in range(2):
                    wmm.append(mm(
                        psA[:, si * 512:(si + 1) * 512], lT,
                        WoT[:, dc * D + si * 512:dc * D + (si + 1) * 512],
                        dc == 0, dc == 7, yev + exd))
            dx2 = dmas(lambda e, r0=row0: e.dma_start(
                out=xt[:, :], in_=x_sl[r0:r0 + 128, :]), pguard)
            x2v = vo(lambda v: v.tensor_tensor(x2t[:, :], psA[:, 0:1024],
                                               xt[:, :], ALU.add),
                     wmm + [dx2] + prev_m2
                     + ([prev_sx2] if prev_sx2 else []))
            sx2 = dmag(lambda e, r0=row0: e.dma_start(
                out=s_x2[r0:r0 + 128, :], in_=x2t[:, :]), [x2v])
            prev_sx2 = sx2
            m1 = vof(lambda v: v.tensor_reduce(sx[:, :], x2t[:, :], AX.X,
                                              ALU.add), [x2v])
            m2 = so(lambda s: s.activation(out=q_sq[:, 1024:2048],
                                           in_=x2t[:, :],
                                           func=AF.Square,
                                           accum_out=sxx[:, :]),
                    [x2v] + ([prev_xq] if prev_xq else []))
            m3 = vof(lambda v: v.tensor_scalar(mnt[:, :], sx[:, :], 1.0 / D,
                                              None, ALU.mult), [m1])
            m4 = vof(lambda v: v.tensor_scalar(vrt[:, :], sxx[:, :], 1.0 / D,
                                              None, ALU.mult), [m2])
            m5 = vof(lambda v: v.tensor_tensor(sx[:, :], mnt[:, :], mnt[:, :],
                                              ALU.mult), [m3])
            m6 = vof(lambda v: v.tensor_tensor(vrt[:, :], vrt[:, :], sx[:, :],
                                              ALU.subtract), [m4, m5])
            m7 = so(lambda s: s.activation(out=vrt[:, :], in_=vrt[:, :],
                                           func=AF.Ln, bias=cst[:, 0:1]),
                    [m6] + mzc)
            m8 = so(lambda s: s.activation(out=rstd[:, :], in_=vrt[:, :],
                                           func=AF.Exp, scale=-0.5), [m7])
            m9 = vof(lambda v: v.tensor_tensor(mbn[:, :], mnt[:, :], rstd[:, :],
                                              ALU.mult), [m8])
            m10 = vof(lambda v: v.tensor_scalar(mbn[:, :], mbn[:, :], -1.0,
                                               None, ALU.mult), [m9])
            m11 = vo(lambda v: v.tensor_scalar(xt[:, :], x2t[:, :],
                                               rstd[:, :], mbn[:, :],
                                               ALU.mult, ALU.add),
                     [m10, m8, sx2])
            xq = emit_quant(1024, xt, xnt,
                            ([m11, m2] + a0_tail) if cc == 0
                            else ([m11, m2] + prev_dxq))
            prev_xq = xq
            xqd = twave([(xnt[:, dc * 128:(dc + 1) * 128],
                          (lambda s, ps, d2=dc: s.activation(
                              out=xqT[:, d2 * 128:(d2 + 1) * 128],
                              in_=ps, func=AF.Copy)))
                         for dc in range(8)],
                        [xq] + pguard + prev_dxq)
            dxq = dmag(lambda e, ccx=cc: e.dma_start(
                out=s_xqT[ccx, :, :], in_=xqT[:, :]), xqd)
            prev_dxq = [dxq]
            prev_m2 = [m2]
            chainB[bi] = [lnu] + pups[-1:]
            freeA01 = [x2v]
            freeA2 = [rte]
            freeB2 = [st["eke"], kde, kdl, rdl, lnd, we1, pve]
            prev_pups = list(pups)
            prev_yg = yg[-1:]
            pend1[pc] = ([x2v, sx2, dxq, xq, pve, lnu, kde, kdl, rdl, lnd,
                          st["eke"], we1, rte] + pups + yev[0:8] + xqd
                         + pvm[-1:] + [st["dtk"], st["kdt"], st["nrm"][-1]])

        NA1 = NCC if PHASE >= 5 else 0
        seq1 = [bi * NCH + c for c in range(NCH) for bi in (0, 1)][:NA1]
        for k, cc in enumerate(seq1):
            if k > 0:
                a1_beta(seq1[k - 1], (k - 1) % 2)
            a1_alpha(cc, k % 2)
        if NA1:
            a1_beta(seq1[-1], (len(seq1) - 1) % 2)
        chain = chainB[0][-2:] + chainB[1][-2:]

        if _os.environ.get("MK_DBG"):
            S_.add("sync", lambda e: e.dma_start(
                out=d_smub[:, :], in_=sm_u[:, :]), chain, is_dma=True)
            S_.add("sync", lambda e: e.dma_start(
                out=d_smbb[:, :], in_=sm_b[:, :]), chain, is_dma=True)
        prev_odma = None
        aend = [x for x in (pend1[0] + pend1[1]
                            + [prev_sx2, prev_dvv] + chain
                            + psA_free + psB_free + [prev_xq])
                if x is not None]
        esA.close()

        esB = ExitStack()

        def sbB(n, s, d=F32):
            return esB.enter_context(nc.sbuf_tensor(n, s, d)).ap()

        with esB:
            WguT = sbB("WguT", [128, 8 * 2 * I_FF], BF16)
            WdT = sbB("WdT", [128, 22 * D], BF16)
            hq = sbB("hq", [128, I_FF], BF16)
            hqT = sbB("hqT", [128, I_FF], BF16)
            silg = sbB("silg", [128, 704], F32)
            hhf = sbB("hhf", [128, 704], F32)
            outt = sbB("outt", [128, D], F32)
            xqL = sbB("xqL", [128, D], BF16)

            wld = []
            for dc in range(8 if PHASE >= 6 else 0):
                wld.append(dmas(lambda e, dcc=dc: e.dma_start(
                    out=WguT[:, dcc * 2 * I_FF:(dcc + 1) * 2 * I_FF],
                    in_=s_wgu[dcc, :, :]), [wq_last] + aend))
            for ic in range(22 if PHASE >= 6 else 0):
                wld.append(dmas(lambda e, icc=ic: e.dma_start(
                    out=WdT[:, icc * D:(icc + 1) * D], in_=s_wd[icc, :, :]),
                    [wq_last] + aend))

            prev_out = None
            prev_odma2 = None
            prev_hte = []
            psAf = aend
            psBf = aend
            for cc in range(NCC if PHASE >= 6 else 0):
                bi, c = cc // NCH, cc % NCH
                row0 = bi * SCORE + c * C
                dxq = dmas(lambda e, ccx=cc: e.dma_start(
                    out=xqL[:, :], in_=s_xqT[ccx, :, :]),
                    ([prev_out] if prev_out else aend))
                dx2 = dmas(lambda e, r0=row0: e.dma_start(
                    out=x2t[:, :], in_=s_x2[r0:r0 + 128, :]),
                    ([prev_out] if prev_out else aend))
                hqd = []
                hte = []
                hte_prev = list(prev_hte)
                isl = [(0, 512), (512, 512), (1024, 512), (1536, 512),
                       (2048, 512), (2560, 256)]
                for (io, wi) in isl:
                    gmm = []
                    for dc in range(8):
                        lT = xqL[:, dc * 128:(dc + 1) * 128]
                        wb = dc * 2 * I_FF
                        gmm.append(mm(psA[:, 0:wi], lT,
                                      WguT[:, wb + io:wb + io + wi],
                                      dc == 0, dc == 7,
                                      [dxq, wld[-1]] + psAf))
                        gmm.append(mm(psA[:, 512:512 + wi], lT,
                                      WguT[:, wb + I_FF + io:wb + I_FF + io + wi],
                                      dc == 0, dc == 7,
                                      [dxq, wld[-1]] + psAf))
                    sil = so(lambda s, wi_=wi: s.activation(
                        out=silg[:, 0:wi_], in_=psA[:, 0:wi_],
                        func=AF.Silu), gmm)
                    hmu = vo(lambda v, wi_=wi: v.tensor_tensor(
                        hhf[:, 0:wi_], psA[:, 512:512 + wi_],
                        silg[:, 0:wi_], ALU.mult), [sil] + gmm)
                    hqd.append(emit_quant(wi, hhf, hq, [hmu] + hte_prev,
                                          outoff=io))
                    hte_prev = []
                    psAf = [hmu, sil]
                for w0 in range(0, 22, 8):
                    pieces = []
                    for ic in range(w0, min(w0 + 8, 22)):
                        pieces.append((hq[:, ic * 128:(ic + 1) * 128],
                                       (lambda s, ps, i2=ic: s.activation(
                                           out=hqT[:, i2 * 128:(i2 + 1) * 128],
                                           in_=ps, func=AF.Copy))))
                    hte += twave(pieces, hqd)
                dmm = []
                for ic in range(22):
                    lT = hqT[:, ic * 128:(ic + 1) * 128]
                    for si in range(2):
                        dmm.append(mm(
                            psB[:, si * 512:(si + 1) * 512], lT,
                            WdT[:, ic * D + si * 512:ic * D + (si + 1) * 512],
                            ic == 0, ic == 21, hte + psBf))
                ov = vo(lambda v: v.tensor_tensor(outt[:, :], psB[:, 0:1024],
                                                  x2t[:, :], ALU.add),
                        dmm + [dx2]
                        + ([prev_odma2] if prev_odma2 else []))
                odma = dmas(lambda e, r0=row0: e.dma_start(
                    out=out_sl[r0:r0 + 128, :], in_=outt[:, :]), [ov])
                prev_out = ov
                prev_odma2 = odma
                prev_hte = [hte[-1]]
                psBf = [ov]

            if cutoff is not None:
                S_.items = S_.items[:cutoff]
            if _os.environ.get("MK_NITEMS"):
                print("SCHED items:", len(S_.items))
            S_.resolve()
            with (nc.semaphore("sd") as sem_sd, nc.semaphore("gd") as sem_gd,
                  nc.semaphore("v") as sem_v, nc.semaphore("s") as sem_s,
                  nc.semaphore("t") as sem_t, nc.semaphore("g") as sem_g,
                  nc.Block() as block):
                sems = {"sd": sem_sd, "gd": sem_gd, "vector": sem_v,
                        "scalar": sem_s, "tensor": sem_t, "gpsimd": sem_g}

                @block.sync
                def _(sync):
                    S_.emit("sync", sync, sems)
                    sync.wait_ge(sem_sd, S_.final.get("sd", 0))
                    if S_.final.get("gd"):
                        sync.wait_ge(sem_gd, S_.final["gd"])

                @block.gpsimd
                def _(g):
                    S_.emit("gpsimd", g, sems)

                @block.vector
                def _(v):
                    S_.emit("vector", v, sems)

                @block.scalar
                def _(s):
                    S_.emit("scalar", s, sems)

                @block.tensor
                def _(t):
                    S_.emit("tensor", t, sems)
    return nc


_NC_CACHE = [None]


def kernel(L0_attn_x, L0_attn_W_q, L0_attn_W_k, L0_attn_W_v, L0_attn_W_alpha_w,
           L0_attn_W_alpha_b, L0_attn_W_o, L0_ffn_w_gate, L0_ffn_w_up,
           L0_ffn_w_down):
    x = np.ascontiguousarray(np.asarray(L0_attn_x, np.float32))
    if _NC_CACHE[0] is None:
        _NC_CACHE[0] = build()
    nc = _NC_CACHE[0]
    iden = np.eye(128, dtype=np.float32)
    idenb = np.eye(128, dtype=ml_dtypes.bfloat16)
    maskut = np.where(np.arange(128)[:, None] < np.arange(128)[None, :],
                      np.float32(NEG), np.float32(0.0)).astype(np.float32)
    rhsec = np.zeros((9, 1024), np.float32)
    for h in range(8):
        rhsec[h, h * 128:(h + 1) * 128] = -1.0
    onesc = np.ones((1, 128), np.float32)
    wab2 = np.ascontiguousarray(
        np.asarray(L0_attn_W_alpha_b, np.float32).reshape(1, 512))
    in_maps = []
    for r in range(NCORE):
        adj = np.zeros((128, 2), np.float32)
        for j in range(NCORE):
            if j >= r:
                adj[16 * j:16 * (j + 1), 0] = NEG
            else:
                adj[16 * j:16 * (j + 1), 1] = 1.0
        y0 = np.full((128, 1), 1.0, np.float32)
        if r == 0:
            y0[:] = np.float32(1.0) / np.float32(1.0 + EPS)
        xs = np.ascontiguousarray(
            x[:, r * SCORE:(r + 1) * SCORE, :].reshape(2 * SCORE, D))
        in_maps.append({
            "x_sl": xs,
            "Wq": np.ascontiguousarray(np.asarray(L0_attn_W_q, np.float32)),
            "Wk": np.ascontiguousarray(np.asarray(L0_attn_W_k, np.float32)),
            "Wa": np.ascontiguousarray(
                np.asarray(L0_attn_W_alpha_w, np.float32)),
            "Wab": wab2,
            "Wv": np.ascontiguousarray(np.asarray(L0_attn_W_v, np.float32)),
            "Wo": np.ascontiguousarray(np.asarray(L0_attn_W_o, np.float32)),
            "Wg": np.ascontiguousarray(np.asarray(L0_ffn_w_gate, np.float32)),
            "Wu": np.ascontiguousarray(np.asarray(L0_ffn_w_up, np.float32)),
            "Wd": np.ascontiguousarray(np.asarray(L0_ffn_w_down, np.float32)),
            "idenf": iden, "idenb": idenb, "maskut": maskut,
            "rhsec": rhsec, "onesc": onesc, "adjc": adj, "y0fc": y0,
        })
    res = run_bass_kernel_spmd(nc, in_maps, core_ids=list(range(NCORE)))
    out = np.empty((B, S, D), np.float32)
    for r in range(NCORE):
        o = np.asarray(res.results[r]["out_sl"]).reshape(2, SCORE, D)
        out[:, r * SCORE:(r + 1) * SCORE, :] = o
    return out


# version 20
# speedup vs baseline: 2.3306x; 1.0045x over previous
import numpy as np
import ml_dtypes
from contextlib import ExitStack
import os as _os
import concourse.bass as bass
import concourse.mybir as mybir
from concourse.bass_utils import run_bass_kernel_spmd

F32 = mybir.dt.float32
BF16 = mybir.dt.bfloat16
AF = mybir.ActivationFunctionType
ALU = mybir.AluOpType
AX = mybir.AxisListType

B, S, D, H = 2, 16384, 1024, 8
NCORE = 8
SCORE = S // NCORE
C = 128
NCH = SCORE // C
NCC = 2 * NCH
I_FF = 2816
EPS = 1e-8
C2V = 1.0 + EPS
C1V = EPS * (1.0 + EPS)
SCL_B = 0.125
T2 = [0.0625, 0.5625, 1.5625, 3.0625, 6.25, 12.25, 25.0]
NEG = -1.0e30

DMA_SEM = {"sync": "sd", "gpsimd": "gd"}


class Item:
    __slots__ = ("eng", "fn", "deps", "is_dma", "idx", "sem_val", "needed",
                 "force", "dma_cum")

    def __init__(self, eng, fn, deps, is_dma, idx, force=False):
        self.eng, self.fn, self.is_dma, self.idx = eng, fn, is_dma, idx
        self.deps = [d for d in deps if d is not None]
        self.sem_val = None
        self.needed = force
        self.force = force


class Sched:
    def __init__(self):
        self.items = []
        self.final = {}

    def add(self, eng, fn, deps=(), is_dma=False, force=False):
        it = Item(eng, fn, deps, is_dma, len(self.items), force)
        self.items.append(it)
        return it

    def resolve(self):
        for it in self.items:
            for d in it.deps:
                assert d.idx < it.idx
                if d.eng != it.eng or d.is_dma:
                    d.needed = True
        cnt = {}
        dcum = {"sd": 0, "gd": 0}
        for it in self.items:
            it.dma_cum = dict(dcum)
            if it.is_dma:
                sem = DMA_SEM[it.eng]
                cnt[sem] = cnt.get(sem, 0) + 16
                it.sem_val = cnt[sem]
                dcum[sem] = cnt[sem]
            elif it.needed:
                cnt[it.eng] = cnt.get(it.eng, 0) + 1
                it.sem_val = cnt[it.eng]
        self.final = dict(cnt)

    def emit(self, eng_name, eng, sems):
        if not hasattr(self, "hw"):
            self.hw = {}
        for it in self.items:
            if it.eng != eng_name:
                continue
            waited = {}
            for d in it.deps:
                if d.eng == eng_name and not d.is_dma and not d.force:
                    continue
                if d.is_dma:
                    sem = DMA_SEM[d.eng]
                    val = it.dma_cum[sem]
                else:
                    sem = d.eng
                    val = d.sem_val
                if waited.get(sem, -1) < val:
                    waited[sem] = val
            for sem, val in sorted(waited.items()):
                if self.hw.get((eng_name, sem), -1) >= val:
                    continue
                self.hw[(eng_name, sem)] = val
                eng.wait_ge(sems[sem], val)
            ins = it.fn(eng)
            if it.is_dma:
                ins.then_inc(sems[DMA_SEM[it.eng]], 16)
            elif it.needed:
                ins.then_inc(sems[it.eng], 1)


def build(cutoff=None):
    PHASE = int(_os.environ.get("MK_PHASE", "9"))
    nc = bass.Bass("TRN2", target_bir_lowering=False, debug=False,
                   detect_race_conditions=False)
    dt = nc.dram_tensor
    x_sl = dt("x_sl", [2 * SCORE, D], F32, kind="ExternalInput").ap()
    Wq = dt("Wq", [512, D], F32, kind="ExternalInput").ap()
    Wk = dt("Wk", [512, D], F32, kind="ExternalInput").ap()
    Wa = dt("Wa", [512, D], F32, kind="ExternalInput").ap()
    Wab = dt("Wab", [1, 512], F32, kind="ExternalInput").ap()
    Wv = dt("Wv", [D, D], F32, kind="ExternalInput").ap()
    Wo = dt("Wo", [D, D], F32, kind="ExternalInput").ap()
    Wg = dt("Wg", [I_FF, D], F32, kind="ExternalInput").ap()
    Wu = dt("Wu", [I_FF, D], F32, kind="ExternalInput").ap()
    Wd = dt("Wd", [D, I_FF], F32, kind="ExternalInput").ap()
    idenf = dt("idenf", [128, 128], F32, kind="ExternalInput").ap()
    idenb = dt("idenb", [128, 128], BF16, kind="ExternalInput").ap()
    maskut = dt("maskut", [128, 128], F32, kind="ExternalInput").ap()
    rhsec = dt("rhsec", [9, 1024], F32, kind="ExternalInput").ap()
    onesc = dt("onesc", [1, 128], F32, kind="ExternalInput").ap()
    adjc = dt("adjc", [128, 2], F32, kind="ExternalInput").ap()
    y0fc = dt("y0fc", [128, 1], F32, kind="ExternalInput").ap()
    out_sl = dt("out_sl", [2 * SCORE, D], F32, kind="ExternalOutput").ap()
    _dbg = "ExternalOutput" if _os.environ.get("MK_DBG") else "Internal"
    s_x2 = dt("s_x2", [2 * SCORE, D], F32, kind=_dbg).ap()
    s_vv = dt("s_vv", [NCC, 128, D], BF16, kind=_dbg).ap()
    s_xqT = dt("s_xqT", [NCC, 128, D], BF16, kind=_dbg).ap()
    s_wgu = dt("s_wgu", [8, 128, 2 * I_FF], BF16, kind=_dbg).ap()
    s_wd = dt("s_wd", [22, 128, D], BF16, kind=_dbg).ap()
    if _os.environ.get("MK_DBG"):
        d_xnt = dt("d_xnt", [128, D], BF16, kind="ExternalOutput").ap()
        d_kev = dt("d_kev", [128, 512], F32, kind="ExternalOutput").ap()
        d_bat = dt("d_bat", [128, 16], F32, kind="ExternalOutput").ap()
        d_smu = dt("d_smu", [9, 128], F32, kind="ExternalOutput").ap()
        d_trn = dt("d_trn", [8, 768], F32, kind="ExternalOutput").ap()
        d_st = dt("d_st", [128, 8], F32, kind="ExternalOutput").ap()
        d_smub = dt("d_smub", [9, 2 * SCORE], F32, kind="ExternalOutput").ap()
        d_smbb = dt("d_smbb", [8, 2 * SCORE], F32, kind="ExternalOutput").ap()
        d_vvl = dt("d_vvl", [128, D], BF16, kind="ExternalOutput").ap()
        d_xt = dt("d_xt", [128, D], F32, kind="ExternalOutput").ap()
    cc_in = dt("cc_in", [16, 1032], F32).ap()
    cc_out = dt("cc_out", [128, 1032], F32, addr_space="Shared").ap()

    S_ = Sched()
    es = ExitStack()

    def sb(n, s, d=F32):
        return es.enter_context(nc.sbuf_tensor(n, s, d)).ap()

    with es:
        idf = sb("idf", [128, 128], F32)
        idb = sb("idb", [128, 128], BF16)
        mskt = sb("mskt", [128, 128], F32)
        onet = sb("onet", [1, 128], F32)
        wabt = sb("wabt", [1, 512], F32)
        adjt = sb("adjt", [128, 2], F32)
        y0t = sb("y0t", [128, 1], F32)
        cst = sb("cst", [128, 2], F32)
        xtP = [sb("xt0", [128, D], F32), sb("xt1", [128, D], F32)]
        x2t = sb("x2t", [128, D], F32)
        xt = xtP[0]
        q_u = sb("q_u", [128, 1024], F32)
        q_sq = sb("q_sq", [128, 2048], F32)
        q_z = sb("q_z", [128, 1024], F32)
        q_zc = sb("q_zc", [128, 1024], F32)
        q_sg = sb("q_sg", [128, 1536], F32)
        q_bm = sb("q_bm", [128, 88], F32)
        q_sc = sb("q_sc", [128, 88], F32)
        q_rs = sb("q_rs", [128, 88], F32)
        psA = es.enter_context(nc.psum_tensor("psA", [128, 1536], F32)).ap()
        psB = es.enter_context(nc.psum_tensor("psB", [128, 1536], F32)).ap()
        psT = es.enter_context(nc.psum_tensor("psT", [128, 2048], BF16)).ap()
        pstS = {"par": 0, "lastev": [None, None]}

        def twave(pieces, deps):
            b_ = pstS["par"]
            base = b_ * 1024
            lev = pstS["lastev"][b_]
            dms = []
            for i, (sap, _) in enumerate(pieces):
                dms.append(mm(psT[:, base + i * 128:base + (i + 1) * 128],
                              sap, idb[:, :], True, True,
                              deps + [dc_ib] + ([lev] if lev else []),
                              tr=True))
            evs = []
            for i, (_, efn) in enumerate(pieces):
                evs.append(so(lambda s, i_=i, e_=efn, bb=base: e_(
                    s, psT[:, bb + i_ * 128:bb + (i_ + 1) * 128]),
                    [dms[-1], dms[i]]))
            pstS["lastev"][b_] = evs[-1]
            pstS["par"] ^= 1
            return evs

        def mm(out, lhsT, rhs, start, stop, deps, tr=False):
            return S_.add("tensor", lambda t, o=out, l=lhsT, r=rhs, a=start,
                          b_=stop, c_=tr: t.matmul(o, l, r, start=a, stop=b_,
                                                   is_transpose=c_), deps)

        def vo(fn, deps):
            return S_.add("vector", fn, deps)

        def vof(fn, deps):
            return S_.add("vector", fn, deps, force=True)

        def so(fn, deps):
            return S_.add("scalar", fn, deps)

        def go(fn, deps=()):
            return S_.add("gpsimd", fn, deps)

        def dmas(fn, deps=()):
            return S_.add("sync", fn, deps, is_dma=True)

        def dmag(fn, deps=()):
            return S_.add("gpsimd", fn, deps, is_dma=True)

        def emit_quant(W, src, out, deps, srcoff=0, outoff=0,
                       outguard=(), track=None):
            nb = W // 16
            s_ap = src[:, srcoff:srcoff + W]
            s3 = s_ap.rearrange("p (b k) -> p b k", k=16)
            d0 = vof(lambda v: v.tensor_reduce(q_bm[:, 0:nb], s3, AX.X, ALU.max,
                                              apply_absolute_value=True), deps)
            d1 = vof(lambda v: v.tensor_scalar(q_sc[:, 0:nb], q_bm[:, 0:nb],
                                              1.0 / 6.0, 1e-12, ALU.mult,
                                              ALU.max), [d0])
            d2 = vof(lambda v: v.reciprocal(q_rs[:, 0:nb], q_sc[:, 0:nb]), [d1])
            rs3 = q_rs[:, 0:nb].rearrange("p (b o) -> p b o", o=1).broadcast_to(
                [128, nb, 16])
            u3 = q_u[:, 0:W].rearrange("p (b k) -> p b k", k=16)
            d3 = vo(lambda v: v.tensor_tensor(u3, s3, rs3, ALU.mult),
                    deps + [d2])
            if track is not None:
                track["d3"] = d3
            d4 = so(lambda s: s.activation(out=q_sq[:, 0:W], in_=q_u[:, 0:W],
                                           func=AF.Square), [d3])
            d5 = so(lambda s: s.activation(out=q_sg[:, 0:W], in_=q_u[:, 0:W],
                                           func=AF.Sign), [d3])
            e = vo(lambda v: v.tensor_scalar(q_z[:, 0:W], q_sq[:, 0:W], T2[0],
                                             None, ALU.is_gt), [d4])
            for th in T2[1:4]:
                e = vo(lambda v, t_=th: v.scalar_tensor_tensor(
                    q_z[:, 0:W], q_sq[:, 0:W], t_, q_z[:, 0:W], ALU.is_gt,
                    ALU.add), [e])
            f = vo(lambda v: v.tensor_scalar(q_zc[:, 0:W], q_sq[:, 0:W], T2[4],
                                             None, ALU.is_gt), [d4])
            for th in (T2[5], T2[6], T2[6]):
                f = vo(lambda v, t_=th: v.scalar_tensor_tensor(
                    q_zc[:, 0:W], q_sq[:, 0:W], t_, q_zc[:, 0:W], ALU.is_gt,
                    ALU.add), [f])
            g = vo(lambda v: v.scalar_tensor_tensor(
                q_sq[:, 0:W], q_z[:, 0:W], 0.5, q_zc[:, 0:W], ALU.mult,
                ALU.add), [e, f])
            g2 = vo(lambda v: v.tensor_tensor(q_u[:, 0:W], q_sq[:, 0:W],
                                              q_sg[:, 0:W], ALU.mult),
                    [g, d5])
            sc3 = q_sc[:, 0:nb].rearrange("p (b o) -> p b o", o=1).broadcast_to(
                [128, nb, 16])
            o3 = out[:, outoff:outoff + W].rearrange("p (b k) -> p b k", k=16)
            return vo(lambda v: v.tensor_tensor(o3, u3, sc3, ALU.mult),
                      [g2] + list(outguard))

        def emit_quant_pool(W, src, out, deps):
            nb = W // 16
            s_ap = src[:, 0:W]
            s3 = s_ap.rearrange("p (b k) -> p b k", k=16)
            d0 = vof(lambda v: v.tensor_reduce(q_bm[:, 0:nb], s3, AX.X, ALU.max,
                                              apply_absolute_value=True), deps)
            d1 = vof(lambda v: v.tensor_scalar(q_sc[:, 0:nb], q_bm[:, 0:nb],
                                              1.0 / 6.0, 1e-12, ALU.mult,
                                              ALU.max), [d0])
            d2 = vof(lambda v: v.reciprocal(q_rs[:, 0:nb], q_sc[:, 0:nb]), [d1])
            rs3 = q_rs[:, 0:nb].rearrange("p (b o) -> p b o", o=1).broadcast_to(
                [128, nb, 16])
            u3 = q_u[:, 0:W].rearrange("p (b k) -> p b k", k=16)
            d3 = go(lambda g: g.tensor_tensor(u3, s3, rs3, ALU.mult),
                    deps + [d2])
            d4 = so(lambda s: s.activation(out=q_sq[:, 0:W], in_=q_u[:, 0:W],
                                           func=AF.Square), [d3])
            d5 = so(lambda s: s.activation(out=q_sg[:, 0:W], in_=q_u[:, 0:W],
                                           func=AF.Sign), [d3])
            tmp = src
            e = go(lambda g: g.tensor_scalar(q_z[:, 0:W], q_sq[:, 0:W], T2[0],
                                             None, ALU.is_gt), [d4, d3])
            for th in T2[1:4]:
                t1 = go(lambda g, t_=th: g.tensor_scalar(
                    tmp[:, 0:W], q_sq[:, 0:W], t_, None, ALU.is_gt),
                    [d4, d3, e])
                e = go(lambda g: g.tensor_tensor(q_z[:, 0:W], q_z[:, 0:W],
                                                 tmp[:, 0:W], ALU.add), [t1, e])
            f = go(lambda g: g.tensor_scalar(q_zc[:, 0:W], q_sq[:, 0:W], T2[4],
                                             None, ALU.is_gt), [d4])
            for th in (T2[5], T2[6], T2[6]):
                t1 = go(lambda g, t_=th: g.tensor_scalar(
                    tmp[:, 0:W], q_sq[:, 0:W], t_, None, ALU.is_gt), [f])
                f = go(lambda g: g.tensor_tensor(q_zc[:, 0:W], q_zc[:, 0:W],
                                                 tmp[:, 0:W], ALU.add),
                       [t1, f])
            g_ = go(lambda g: g.tensor_scalar(q_z[:, 0:W], q_z[:, 0:W], 0.5,
                                              None, ALU.mult), [e, f])
            g1 = go(lambda g: g.tensor_tensor(q_sq[:, 0:W], q_z[:, 0:W],
                                              q_zc[:, 0:W], ALU.add), [g_])
            g2 = go(lambda g: g.tensor_tensor(q_u[:, 0:W], q_sq[:, 0:W],
                                              q_sg[:, 0:W], ALU.mult),
                    [g1, d5])
            sc3 = q_sc[:, 0:nb].rearrange("p (b o) -> p b o", o=1).broadcast_to(
                [128, nb, 16])
            o3 = out[:, 0:W].rearrange("p (b k) -> p b k", k=16)
            return go(lambda g: g.tensor_tensor(o3, u3, sc3, ALU.mult), [g2])

        dc_id = dmas(lambda e: e.dma_start(out=idf[:, :], in_=idenf[:, :]))
        dc_ib = dmas(lambda e: e.dma_start(out=idb[:, :], in_=idenb[:, :]))
        dc_mk = dmas(lambda e: e.dma_start(out=mskt[:, :], in_=maskut[:, :]))
        dc_on = dmas(lambda e: e.dma_start(out=onet[:, :], in_=onesc[:, :]))
        dc_wb = dmas(lambda e: e.dma_start(out=wabt[:, :], in_=Wab[:, :]))
        dc_aj = dmas(lambda e: e.dma_start(out=adjt[:, :], in_=adjc[:, :]))
        dc_y0 = dmas(lambda e: e.dma_start(out=y0t[:, :], in_=y0fc[:, :]))

        esA = ExitStack()

        def sbA(n, s, d=F32):
            return esA.enter_context(nc.sbuf_tensor(n, s, d)).ap()

        WT = sbA("WT", [128, 8 * 2560], BF16)
        WoT = sbA("WoT", [128, 8 * D], BF16)
        sm_b = sbA("sm_b", [8, 2 * SCORE], F32)
        sm_u = sbA("sm_u", [9, 2 * SCORE], F32)
        rhse = sbA("rhse", [9, 1024], F32)
        trnP = [sbA("trn0", [8, 768], F32), sbA("trn1", [8, 768], F32)]
        trn = trnP[0]
        lnZt = sbA("lnZt", [8, 2], F32)
        Pst = sbA("Pst", [8, 2 * D], F32)
        Pbf = sbA("Pbf", [8, 2 * D], BF16)
        dAt = sbA("dAt", [8, 2], F32)
        kdbfP = [sbA("kdbf0", [8, 128], BF16), sbA("kdbf1", [8, 128], BF16)]
        kdbf = kdbfP[0]
        ccs = sbA("ccs", [128, 1032], F32)
        ccj = sbA("ccj", [8, 2 * 1032], F32)
        vvtP = [sbA("vvt0", [128, D], BF16), sbA("vvt1", [128, D], BF16)]
        vvt = vvtP[0]
        accRP = [sbA("accR0", [128, 8], F32), sbA("accR1", [128, 8], F32)]
        accR = accRP[0]
        DtokP = [sbA("Dtok0", [128, 8], F32), sbA("Dtok1", [128, 8], F32)]
        Dtok = DtokP[0]
        rDtkP = [sbA("rDtk0", [128, 8], F32), sbA("rDtk1", [128, 8], F32)]
        rDtk = rDtkP[0]
        eKTP = [sbA("eKT0", [128, 8], F32), sbA("eKT1", [128, 8], F32)]
        eKT = eKTP[0]
        wclTP = [sbA("wclT0", [128, 8], BF16), sbA("wclT1", [128, 8], BF16)]
        wclT = wclTP[0]
        pvTP = [sbA("pvT0", [128, 64], F32), sbA("pvT1", [128, 64], F32)]
        pvT = pvTP[0]
        RhP = [sbA("Rh0", [128, 1024], F32), sbA("Rh1", [128, 1024], F32)]
        Rh = RhP[0]
        RTsP = [sbA("RTs0", [128, 128], BF16), sbA("RTs1", [128, 128], BF16)]
        RTs = RTsP[0]
        yTs = sbA("yTs", [128, D], BF16)
        sxP = [sbA("sx0", [128, 1], F32), sbA("sx1", [128, 1], F32)]
        sx = sxP[0]
        sxxP = [sbA("sxx0", [128, 1], F32), sbA("sxx1", [128, 1], F32)]
        sxx = sxxP[0]
        mntP = [sbA("mnt0", [128, 1], F32), sbA("mnt1", [128, 1], F32)]
        mnt = mntP[0]
        vrtP = [sbA("vrt0", [128, 1], F32), sbA("vrt1", [128, 1], F32)]
        vrt = vrtP[0]
        rstdP = [sbA("rstd0", [128, 1], F32), sbA("rstd1", [128, 1], F32)]
        rstd = rstdP[0]
        mbnP = [sbA("mbn0", [128, 1], F32), sbA("mbn1", [128, 1], F32)]
        mbn = mbnP[0]

        wsl = sbA("wsl", [128, 1024], F32)
        wq_oP = [sbA("wq_o0", [128, 1024], BF16),
                 sbA("wq_o1", [128, 1024], BF16)]
        wq_o = wq_oP[0]
        wq_p = sbA("wq_p", [128, 1024], BF16)
        xnt = sbA("xnt", [128, D], BF16)
        xnT = sbA("xnT", [128, D], BF16)
        kev = sbA("kev", [128, 512], F32)
        qkt = sbA("qkt", [128, 512], F32)
        spe = sbA("spe", [128, 512], F32)
        batP = [sbA("bat0", [128, 16], F32), sbA("bat1", [128, 16], F32)]
        xqT = sbA("xqT", [128, D], BF16)

        mzc = [go(lambda g: g.memset(cst[:, 0:1], 1e-5)),
               go(lambda g: g.memset(cst[:, 1:2], 1.0))]
        mz = [go(lambda g: g.memset(sm_u[:, :], 1.0)),
              go(lambda g: g.memset(lnZt[:, :], NEG)),
              go(lambda g: g.memset(Pst[:, :], 0.0)),
              go(lambda g: g.memset(dAt[:, :], 0.0))]
        dc_re = dmas(lambda e: e.dma_start(out=rhse[:, :],
                                           in_=rhsec[:, :]))

        prevw = None
        prev_wcb = None
        prevwP = [None, None]
        wslc = 0
        for (Wsrc, nr, kind) in (() if _os.environ.get("MK_SKIPW") else
                                 ((Wq, 512, 0), (Wk, 512, 1), (Wa, 512, 2),
                                  (Wv, D, 3), (Wo, D, 4))):
            for r in range(nr // 128):
                wo = wq_oP[wslc % 2]
                dld = dmas(lambda e, Ws=Wsrc, rr=r: e.dma_start(
                    out=wsl[:, 0:1024], in_=Ws[rr * 128:(rr + 1) * 128, :]),
                    [prev_wcb])
                wcb = vo(lambda v, w_=wo: v.tensor_copy(w_[:, 0:1024],
                                                        wsl[:, 0:1024]),
                         [dld, prevwP[wslc % 2]])
                prev_wcb = wcb
                pieces = []
                for dc in range(8):
                    if kind < 4:
                        col = dc * 2560 + kind * 512 + r * 128
                        tgt = WT
                    else:
                        col = dc * D + r * 128
                        tgt = WoT
                    pieces.append((wo[:, dc * 128:(dc + 1) * 128],
                                   (lambda s, ps, cl=col, tg=tgt:
                                    s.activation(out=tg[:, cl:cl + 128],
                                                 in_=ps, func=AF.Copy))))
                evs = twave(pieces, [wcb])
                prevw = evs[-1]
                prevwP[wslc % 2] = evs[-1]
                wslc += 1
        wA_last = prevw if prevw is not None else dc_id

        wqs = {"q": wA_last, "pd": None, "d3": None, "ev": None, "dq": {}}
        wq_emits = []

        def _wq_front(sl, Wsrc, rr, off, w_):
            wo = wq_oP[sl % 2]
            dld = dmas(lambda e, Ws=Wsrc: e.dma_start(
                out=wsl[:, 0:w_],
                in_=Ws[rr * 128:(rr + 1) * 128, off:off + w_]),
                [wqs["d3"] if wqs["d3"] is not None else wqs["q"]])
            dq = emit_quant(w_, wsl, wo, [dld],
                            outguard=([wqs["ev"]] if wqs["ev"] else []),
                            track=wqs)
            wqs["q"] = dq
            wqs["dq"][sl] = dq

        def _wq_back(sl, Wsrc, rr, off, w_, dst, colbase):
            wo = wq_oP[sl % 2]
            dq = wqs["dq"][sl]
            npc = w_ // 128
            evs = []
            for w0 in range(0, npc, 8):
                pieces = []
                for dcl in range(w0, min(w0 + 8, npc)):
                    pieces.append((wo[:, dcl * 128:(dcl + 1) * 128],
                                   (lambda s, ps, d2=dcl:
                                    s.activation(
                        out=wq_p[:, d2 * 128:(d2 + 1) * 128],
                        in_=ps, func=AF.Copy))))
                evs += twave(pieces,
                             [dq] + ([wqs["pd"]] if wqs["pd"] else []))
            wqs["ev"] = evs[-1]
            for dcl in range(npc):
                dc = off // 128 + dcl
                wqs["pd"] = dmag(lambda e, ds=dst, dcc=dc, d2=dcl,
                                 cb=colbase: e.dma_start(
                    out=ds[dcc, :, cb + rr * 128:cb + rr * 128 + 128],
                    in_=wq_p[:, d2 * 128:(d2 + 1) * 128]),
                    [evs[dcl]] + ([wqs["pd"]] if wqs["pd"] else []))

        for (Wsrc, nr, wc, dst, colbase) in () if (
                PHASE < 2 or _os.environ.get("MK_SKIPW")) else (
                (Wg, I_FF, D, s_wgu, 0), (Wu, I_FF, D, s_wgu, I_FF),
                (Wd, D, I_FF, s_wd, 0)):
            slices = ([(0, 1024), (1024, 1024), (2048, 768)]
                      if wc == 2816 else [(0, wc)])
            for r in range(nr // 128):
                for (off, w_) in slices:
                    wq_emits.append((Wsrc, r, off, w_, dst, colbase))

        wq_i = {"f": 0, "b": 0}

        def wq_pump(frac):
            tgt = min(len(wq_emits), int(frac * len(wq_emits) + 0.999))
            while wq_i["f"] < tgt:
                sl = wq_i["f"]
                (Ws, r, off, w_, ds, cb) = wq_emits[sl]
                _wq_front(sl, Ws, r, off, w_)
                wq_i["f"] += 1
                if wq_i["f"] >= 2 and wq_i["b"] < wq_i["f"] - 1:
                    sb_ = wq_i["b"]
                    (Ws, r, off, w_, ds, cb) = wq_emits[sb_]
                    _wq_back(sb_, Ws, r, off, w_, ds, cb)
                    wq_i["b"] += 1
            if frac >= 1.0:
                while wq_i["b"] < wq_i["f"]:
                    sb_ = wq_i["b"]
                    (Ws, r, off, w_, ds, cb) = wq_emits[sb_]
                    _wq_back(sb_, Ws, r, off, w_, ds, cb)
                    wq_i["b"] += 1

        psA_free = [wA_last]
        psB_free = []
        vv_st = {}
        chain = list(mz)
        chainA = {0: list(mz), 1: list(mz)}
        prev_dvv = None
        prev_pmm = []
        prev_ar = []
        prev_xnd = []
        pend0 = [[], []]
        first_l2 = mzc[0]
        A0S = {}
        freeA = list(psA_free)
        freeB01 = []
        freeB2A = []

        def a0_tiles(cc):
            pc = cc % 2
            return (trnP[pc], vvtP[pc], sxP[pc], sxxP[pc], mntP[pc],
                    vrtP[pc], rstdP[pc], mbnP[pc], batP[pc],
                    xtP[pc], wclTP[pc], pvTP[pc])

        def a0_alpha(cc, pc):
            nonlocal prev_dvv, prev_pmm, prev_ar, first_l2, prev_xnd
            bi, c = cc // NCH, cc % NCH
            chain = chainA[bi]
            (trn, vvt, sx, sxx, mnt, vrt, rstd, mbn, bat,
             xt, wclT, pvT) = a0_tiles(cc)
            pguard = pend0[pc]
            st = {}
            A0S[cc] = st
            row0 = bi * SCORE + c * C
            dx = dmas(lambda e, r0=row0: e.dma_start(
                out=xt[:, :], in_=x_sl[r0:r0 + 128, :]), pguard)
            l1 = vof(lambda v: v.tensor_reduce(sx[:, :], xt[:, :], AX.X,
                                              ALU.add), [dx])
            l2 = so(lambda s: s.activation(out=q_sq[:, 1024:2048],
                                           in_=xt[:, :],
                                           func=AF.Square,
                                           accum_out=sxx[:, :]),
                    [dx, first_l2] + chain[-1:])
            first_l2 = l2
            l3 = vof(lambda v: v.tensor_scalar(mnt[:, :], sx[:, :], 1.0 / D,
                                              None, ALU.mult), [l1])
            l4 = vof(lambda v: v.tensor_scalar(vrt[:, :], sxx[:, :], 1.0 / D,
                                              None, ALU.mult), [l2])
            l5 = vof(lambda v: v.tensor_tensor(sx[:, :], mnt[:, :], mnt[:, :],
                                              ALU.mult), [l3])
            l6 = vof(lambda v: v.tensor_tensor(vrt[:, :], vrt[:, :], sx[:, :],
                                              ALU.subtract), [l4, l5])
            l7 = so(lambda s: s.activation(out=vrt[:, :], in_=vrt[:, :],
                                           func=AF.Ln, bias=cst[:, 0:1]),
                    [l6] + mzc)
            l8 = so(lambda s: s.activation(out=rstd[:, :], in_=vrt[:, :],
                                           func=AF.Exp, scale=-0.5), [l7])
            l9 = vof(lambda v: v.tensor_tensor(mbn[:, :], mnt[:, :],
                                              rstd[:, :], ALU.mult), [l8])
            l10 = vof(lambda v: v.tensor_scalar(mbn[:, :], mbn[:, :], -1.0,
                                               None, ALU.mult), [l9])
            l11 = vo(lambda v: v.tensor_scalar(xnt[:, :], xt[:, :],
                                               rstd[:, :], mbn[:, :],
                                               ALU.mult, ALU.add),
                      [l10, l8] + prev_xnd)
            xnd = twave([(xnt[:, dc * 128:(dc + 1) * 128],
                          (lambda s, ps, d2=dc: s.activation(
                              out=xnT[:, d2 * 128:(d2 + 1) * 128],
                              in_=ps, func=AF.Copy)))
                         for dc in range(8)], [l11] + prev_pmm)
            pb = mm(psA[:, 1024:1536], onet[0:1, :], wabt[0:1, :], True, False,
                    [dc_on, dc_wb] + xnd + freeA)
            pmm = []
            for dc in range(8):
                lT = xnT[:, dc * 128:(dc + 1) * 128]
                bs = dc * 2560
                for si, (oap, wcl) in enumerate((
                        (psA[:, 0:512], bs), (psA[:, 512:1024], bs + 512),
                        (psA[:, 1024:1536], bs + 1024),
                        (psB[:, 0:512], bs + 1536),
                        (psB[:, 512:1024], bs + 2048))):
                    dd = [pb] + xnd + (freeB01 if si >= 3 else [])
                    pmm.append(mm(oap, lT, WT[:, wcl:wcl + 512],
                                  (dc == 0) and si != 2, dc == 7, dd))
            kd_ = so(lambda s: s.activation(out=kev[:, :],
                                            in_=psA[:, 512:1024],
                                            func=AF.Copy), pmm)
            qk = vo(lambda v: v.tensor_tensor(qkt[:, :], psA[:, 0:512],
                                              kev[:, :], ALU.mult),
                    [kd_] + pmm)
            qk3 = qkt[:, :].rearrange("p (h k) -> p h k", k=64)
            br = vof(lambda v: v.tensor_reduce(bat[:, 0:8], qk3, AX.X, ALU.add),
                    [qk])
            sp1 = so(lambda s: s.activation(out=spe[:, :],
                                            in_=psA[:, 1024:1536],
                                            func=AF.Exp), pmm + prev_ar)
            sp2 = so(lambda s: s.activation(out=spe[:, :], in_=spe[:, :],
                                            func=AF.Ln, bias=cst[:, 1:2]),
                     [sp1] + mzc)
            sp3 = spe[:, :].rearrange("p (h k) -> p h k", k=64)
            ar = vof(lambda v: v.tensor_reduce(bat[:, 8:16], sp3, AX.X, ALU.add,
                                              negate=True), [sp2])
            vvd = []
            for (pp, c0, o0) in ((psB, 0, 0), (psB, 512, 512)):
                sg1 = so(lambda s, p_=pp, a_=c0: s.activation(
                    out=q_sg[:, 1024:1536], in_=p_[:, a_:a_ + 512],
                    func=AF.Sign),
                    pmm + [qk, sp1, kd_] + vvd +
                    ([prev_dvv] if prev_dvv else []))
                sg2 = so(lambda s: s.mul(q_sg[:, 1024:1536],
                                         q_sg[:, 1024:1536], C1V), [sg1])
                vvd.append(vo(lambda v, p_=pp, a_=c0, b_=o0:
                              v.scalar_tensor_tensor(
                    vvt[:, b_:b_ + 512], p_[:, a_:a_ + 512], C2V,
                    q_sg[:, 1024:1536], ALU.mult, ALU.add), [sg2]))
            dvv = dmag(lambda e, ccx=cc: e.dma_start(
                out=s_vv[ccx, :, :], in_=vvt[:, :]), vvd)
            if cc == NCC - 1 and _os.environ.get("MK_DBG"):
                S_.add("sync", lambda e: e.dma_start(
                    out=d_vvl[:, :], in_=vvt[:, :]), vvd + [dvv], is_dma=True)
            vv_st[cc] = dvv
            prev_dvv = dvv
            prev_pmm = pmm[-1:]
            prev_ar = [ar]
            prev_xnd = xnd[-1:]
            st.update(l11=l11, pmm=pmm, kd_=kd_, qk=qk, br=br, sp1=sp1,
                      sp2=sp2, ar=ar, vvd=vvd, dvv=dvv)

        def a0_beta(cc, pc):
            nonlocal psA_free, psB_free, freeA, freeB01, freeB2A
            bi, c = cc // NCH, cc % NCH
            chain = chainA[bi]
            (trn, vvt, sx, sxx, mnt, vrt, rstd, mbn, bat,
             xt, wclT, pvT) = a0_tiles(cc)
            st = A0S[cc]
            l11, pmm, kd_, qk, br, sp1, sp2, ar, vvd, dvv = (
                st["l11"], st["pmm"], st["kd_"], st["qk"], st["br"],
                st["sp1"], st["sp2"], st["ar"], st["vvd"], st["dvv"])
            gc = bi * SCORE + c * C
            uc = slice(gc, gc + C)
            bt = mm(psB[0:8, 1024:1152], bat[:, 0:8], idf[:, :],
                    True, True, [br, dc_id] + freeB2A, tr=True)
            at = mm(psB[0:8, 1152:1280], bat[:, 8:16], idf[:, :],
                    True, True, [ar, dc_id] + freeB2A, tr=True)
            acp = vof(lambda v: v.tensor_copy(
                trn[0:8, 384:512], psB[0:8, 1152:1280]), [at])
            Acs = vof(lambda v: v.tensor_tensor_scan(
                trn[0:8, 0:128], trn[0:8, 384:512],
                trn[0:8, 384:512], 0.0, ALU.add, ALU.bypass), [acp])
            ucv = vof(lambda v: v.scalar_tensor_tensor(
                trn[0:8, 128:256], psB[0:8, 1024:1152], SCL_B,
                trn[0:8, 0:128], ALU.mult, ALU.subtract), [Acs, bt])
            bsc = vof(lambda v: v.tensor_scalar(
                trn[0:8, 512:640], psB[0:8, 1024:1152], SCL_B,
                None, ALU.mult), [bt])
            du = dmag(lambda e, u_=uc: e.dma_start(
                out=sm_u[0:8, u_], in_=trn[0:8, 128:256]),
                [ucv] + mz[0:2])
            db = dmag(lambda e, u_=uc: e.dma_start(
                out=sm_b[0:8, u_], in_=trn[0:8, 512:640]), [bsc])
            nl = vof(lambda v: v.tensor_scalar(
                trn[0:8, 640:641], trn[0:8, 255:256], -1.0, None,
                ALU.mult), [ucv])
            wx = so(lambda s: s.activation(
                out=trn[0:8, 256:384], in_=trn[0:8, 128:256],
                func=AF.Exp, bias=trn[0:8, 640:641]), [nl])
            rz = vof(lambda v: v.tensor_reduce(
                trn[0:8, 641:642], trn[0:8, 256:384], AX.X,
                ALU.add), [wx])
            ek = so(lambda s, b2=bi: s.activation(
                out=trn[0:8, 642:643], in_=lnZt[0:8, b2:b2 + 1],
                func=AF.Exp, bias=trn[0:8, 640:641]), [nl] + chain)
            dl = vof(lambda v: v.tensor_tensor(
                trn[0:8, 643:644], trn[0:8, 641:642],
                trn[0:8, 642:643], ALU.add), [rz, ek])
            rdl = vof(lambda v: v.reciprocal(
                trn[0:8, 644:645], trn[0:8, 643:644]), [dl])
            kd0 = vof(lambda v: v.tensor_tensor(
                trn[0:8, 645:646], trn[0:8, 642:643],
                trn[0:8, 644:645], ALU.mult), [rdl])
            wn = vof(lambda v: v.tensor_scalar(
                trn[0:8, 256:384], trn[0:8, 256:384],
                trn[0:8, 644:645], None, ALU.mult), [rdl, rz])
            wnt = mm(psB[:, 1280:1288], trn[0:8, 256:384], idf[0:8, 0:8],
                     True, True, [wn, dc_id, bsc, db] + freeB2A, tr=True)
            wne = vo(lambda v: v.tensor_copy(wclT[:, :], psB[:, 1280:1288]),
                     [wnt])
            pvm = []
            for dc in range(8):
                pvm.append(mm(psB[:, 1288 + dc * 8:1296 + dc * 8],
                              vvt[:, dc * 128:(dc + 1) * 128], wclT[:, :],
                              True, True, [wne] + vvd + freeB2A))
            pve = vo(lambda v: v.tensor_copy(pvT[:, :], psB[:, 1288:1352]),
                     pvm)
            ptm = []
            for dc in range(8):
                ptm.append(mm(psA[0:8, dc * 128:(dc + 1) * 128],
                              pvT[:, dc * 8:(dc + 1) * 8], idf[:, :],
                              True, True,
                              [pve, dc_id, kd_, qk, sp1] + vvd, tr=True))
            pup = vo(lambda v, b2=bi: v.scalar_tensor_tensor(
                Pst[0:8, b2 * D:(b2 + 1) * D], Pst[0:8, b2 * D:(b2 + 1) * D],
                trn[0:8, 645:646], psA[0:8, 0:1024], ALU.mult,
                ALU.add), ptm + [kd0] + chain)
            lnd = so(lambda s: s.activation(
                out=trn[0:8, 646:647], in_=trn[0:8, 643:644],
                func=AF.Ln), [dl])
            lnu = vo(lambda v, b2=bi: v.tensor_tensor(
                lnZt[0:8, b2:b2 + 1], trn[0:8, 639:640],
                trn[0:8, 646:647], ALU.add), [lnd, bsc, ek] + chain)
            dau = vo(lambda v, b2=bi: v.tensor_tensor(
                dAt[0:8, b2:b2 + 1], dAt[0:8, b2:b2 + 1],
                trn[0:8, 127:128], ALU.add), [Acs, ucv] + chain)
            chainA[bi] = [pup, lnu, dau]
            freeA = [pup, kd_, qk, sp1]
            freeB01 = list(vvd)
            freeB2A = [Acs, ucv, bsc, wne, pve, acp, pup]
            psA_free = [pup, kd_, qk, sp1]
            psB_free = [Acs, ucv, bsc, wne, pve, acp] + vvd
            pend0[pc] = [pup, lnu, dau, dvv, du, db, pve, wne, l11, qk,
                         kd_, sp2, ar]

        NA0 = 0 if _os.environ.get('MK_NOA0') else (NCC if PHASE >= 3 else 0)
        seq0 = [bi * NCH + c for c in range(NCH) for bi in (0, 1)][:NA0]
        for k, cc in enumerate(seq0):
            if k > 0:
                a0_beta(seq0[k - 1], (k - 1) % 2)
            wq_pump((k + 1.0) / max(NA0, 1))
            a0_alpha(cc, k % 2)
        if NA0:
            a0_beta(seq0[-1], (len(seq0) - 1) % 2)
        wq_pump(1.0)
        chain = chainA[0][-3:] + chainA[1][-3:]
        prevq = wqs["q"]
        wq_last = wqs["pd"] if wqs["pd"] is not None else wA_last

        mzp = go(lambda g: g.memset(trn[0:8, 700:706], 0.0))
        dis = [] if PHASE < 4 else [
            dmag(lambda e: e.dma_start(out=cc_in[0:8, 1026:1032],
                                       in_=trn[0:8, 700:706]), [mzp] + chain),
            dmag(lambda e: e.dma_start(out=cc_in[8:16, 1026:1032],
                                       in_=trn[0:8, 700:706]), [mzp] + chain),
            dmag(lambda e: e.dma_start(out=cc_in[0:8, 0:1],
                                       in_=lnZt[0:8, 0:1],
                                       allow_slow_non_contiguous=True), chain),
            dmag(lambda e: e.dma_start(out=cc_in[8:16, 0:1],
                                       in_=lnZt[0:8, 1:2],
                                       allow_slow_non_contiguous=True), chain),
            dmag(lambda e: e.dma_start(out=cc_in[0:8, 1:2],
                                       in_=dAt[0:8, 0:1],
                                       allow_slow_non_contiguous=True), chain),
            dmag(lambda e: e.dma_start(out=cc_in[8:16, 1:2],
                                       in_=dAt[0:8, 1:2],
                                       allow_slow_non_contiguous=True), chain),
            dmag(lambda e: e.dma_start(out=cc_in[0:8, 2:1026],
                                       in_=Pst[0:8, 0:D]), chain),
            dmag(lambda e: e.dma_start(out=cc_in[8:16, 2:1026],
                                       in_=Pst[0:8, D:2 * D]), chain),
        ]
        if _os.environ.get("MK_NOCOL"):
            col = None if PHASE < 4 else go(
                lambda g: g.memset(ccs[0:1, 0:1], 0.0), dis)
        else:
            col = None if PHASE < 4 else S_.add(
                "gpsimd", lambda g: g.collective_compute(
                    "AllGather", ALU.bypass,
                    replica_groups=[list(range(NCORE))],
                    ins=[cc_in[:, :]], outs=[cc_out[:, :]]), dis, force=True)
        dco = None if PHASE < 4 else dmag(
            lambda e: e.dma_start(out=ccs[:, :], in_=cc_out[:, :]), [col])
        aj1 = None if PHASE < 4 else vof(lambda v: v.tensor_tensor(ccs[:, 0:1], ccs[:, 0:1],
                                           adjt[:, 0:1], ALU.add),
                 [dco, dc_aj])
        aj2 = None if PHASE < 4 else vof(lambda v: v.tensor_tensor(ccs[:, 1:2], ccs[:, 1:2],
                                           adjt[:, 1:2], ALU.mult),
                 [dco, dc_aj])
        rs1 = None if PHASE < 4 else go(lambda g: g.memset(lnZt[:, :], NEG),
                                        chain + dis)
        rs2 = None if PHASE < 4 else go(lambda g: g.memset(Pst[:, :], 0.0),
                                        chain + dis)
        fold = [rs1, rs2, aj1, aj2] if PHASE >= 4 else list(chain)
        for j in range(0 if _os.environ.get('MK_NOF') else (NCORE - 1 if PHASE >= 4 else 0)):
            dj1 = dmag(lambda e, jj=j: e.dma_start(
                out=ccj[0:8, 0:1032], in_=ccs[16 * jj:16 * jj + 8, :]), fold)
            dj2 = dmag(lambda e, jj=j: e.dma_start(
                out=ccj[0:8, 1032:2064],
                in_=ccs[16 * jj + 8:16 * jj + 16, :]), fold)
            nfold = []
            for bi in range(2):
                co = bi * 1032
                f1 = vof(lambda v, c_=co, b2=bi: v.tensor_tensor(
                    trn[0:8, 0:1], lnZt[0:8, b2:b2 + 1],
                    ccj[0:8, c_ + 1:c_ + 2], ALU.add), [dj1, dj2] + fold)
                f2 = vof(lambda v, c_=co: v.tensor_tensor(
                    trn[0:8, 1:2], trn[0:8, 0:1], ccj[0:8, c_:c_ + 1],
                    ALU.max), [f1])
                f3a = vof(lambda v: v.tensor_tensor(
                    trn[0:8, 2:3], trn[0:8, 0:1], trn[0:8, 1:2],
                    ALU.subtract), [f2])
                f3b = vof(lambda v, c_=co: v.tensor_tensor(
                    trn[0:8, 3:4], ccj[0:8, c_:c_ + 1], trn[0:8, 1:2],
                    ALU.subtract), [f2])
                f4 = so(lambda s: s.activation(out=trn[0:8, 4:5],
                                               in_=trn[0:8, 2:3],
                                               func=AF.Exp), [f3a])
                f5 = so(lambda s: s.activation(out=trn[0:8, 5:6],
                                               in_=trn[0:8, 3:4],
                                               func=AF.Exp), [f3b])
                f6 = vof(lambda v: v.tensor_tensor(trn[0:8, 6:7], trn[0:8, 4:5],
                                                  trn[0:8, 5:6], ALU.add),
                        [f4, f5])
                f7 = so(lambda s: s.activation(out=trn[0:8, 7:8],
                                               in_=trn[0:8, 6:7],
                                               func=AF.Ln), [f6])
                f8 = vof(lambda v, b2=bi: v.tensor_tensor(
                    lnZt[0:8, b2:b2 + 1], trn[0:8, 1:2], trn[0:8, 7:8],
                    ALU.add), [f7])
                f9a = vof(lambda v, b2=bi: v.tensor_tensor(
                    trn[0:8, 8:9], trn[0:8, 0:1], lnZt[0:8, b2:b2 + 1],
                    ALU.subtract), [f8])
                f9b = vof(lambda v, c_=co, b2=bi: v.tensor_tensor(
                    trn[0:8, 9:10], ccj[0:8, c_:c_ + 1],
                    lnZt[0:8, b2:b2 + 1], ALU.subtract), [f8])
                fa = so(lambda s: s.activation(out=trn[0:8, 10:11],
                                               in_=trn[0:8, 8:9],
                                               func=AF.Exp), [f9a])
                fb = so(lambda s: s.activation(out=trn[0:8, 11:12],
                                               in_=trn[0:8, 9:10],
                                               func=AF.Exp), [f9b])
                fc = vof(lambda v, c_=co: v.tensor_scalar(
                    ccj[0:8, c_ + 2:c_ + 1026], ccj[0:8, c_ + 2:c_ + 1026],
                    trn[0:8, 11:12], None, ALU.mult), [fb])
                fd = vo(lambda v, c_=co, b2=bi: v.scalar_tensor_tensor(
                    Pst[0:8, b2 * D:(b2 + 1) * D],
                    Pst[0:8, b2 * D:(b2 + 1) * D], trn[0:8, 10:11],
                    ccj[0:8, c_ + 2:c_ + 1026], ALU.mult, ALU.add), [fa, fc])
                nfold += [fd, f8]
            fold = nfold

        a0_tail = (list(chain) + psA_free + psB_free + pend0[0] + pend0[1]
                   + [prevq, wq_last])
        chain = list(fold)
        chainB = {0: list(fold), 1: list(fold)}
        prev_emm = []
        prev_x2L = list(psB_free)
        prev_x2 = None
        prev_sx2 = None
        prev_xq = None
        pend1 = [list(a0_tail), list(a0_tail)]
        prev_dxq = []
        prev_m2 = []

        freeA01 = list(psA_free)
        freeA2 = list(psA_free)
        freeB2 = list(psB_free)
        prev_pups = list(psB_free)
        prev_yg = []
        A1S = {}

        def a1_tiles(pc):
            return (trnP[pc], vvtP[pc], accRP[pc], DtokP[pc], rDtkP[pc],
                    eKTP[pc], wclTP[pc], pvTP[pc], RhP[pc], RTsP[pc],
                    sxP[pc], sxxP[pc], mntP[pc], vrtP[pc],
                    rstdP[pc], mbnP[pc], batP[pc], xtP[pc],
                    kdbfP[pc])

        def a1_alpha(cc, pc):
            bi, c = cc // NCH, cc % NCH
            chain = chainB[bi]
            (trn, vvt, accR, Dtok, rDtk, eKT, wclT, pvT, Rh, RTs, sx,
             sxx, mnt, vrt, rstd, mbn, bat,
             xt, kdbf) = a1_tiles(pc)
            pguard = pend1[pc]
            st = {}
            A1S[cc] = st
            gc = bi * SCORE + c * C
            uc = slice(gc, gc + C)
            dvv = dmag(lambda e, ccx=cc: e.dma_start(
                out=vvt[:, :], in_=s_vv[ccx, :, :]),
                [vv_st[cc]] + pguard)
            duf = dmag(lambda e, u_=uc: e.dma_start(
                out=rhse[8:9, 0:1024], in_=sm_u[0:8, u_]),
                [dc_re] + prev_emm + pguard)
            pbf = vo(lambda v: v.tensor_copy(Pbf[0:8, :], Pst[0:8, :]),
                     chain + prev_yg)
            nl1 = vof(lambda v, u_=uc: v.tensor_scalar(
                trn[0:8, 0:128], sm_u[0:8, u_], -1.0, None,
                ALU.mult), chain)
            emm = []
            for h in range(8):
                emm.append(mm(psA[:, h * 128:(h + 1) * 128],
                              sm_u[0:9, uc],
                              rhse[0:9, h * 128:(h + 1) * 128],
                              h % 4 == 0, False, [duf] + freeA01))
            for h in range(8):
                emm.append(mm(psA[:, h * 128:(h + 1) * 128], idf[:, :],
                              mskt[:, :], False, h % 4 == 3,
                              [dc_id, dc_mk, emm[h]]))
            exd = []
            for h in range(8):
                stopper = emm[8 + (3 if h < 4 else 7)]
                exd.append(so(lambda s, hh=h: s.activation(
                    out=Rh[:, hh * 128:(hh + 1) * 128],
                    in_=psA[:, hh * 128:(hh + 1) * 128], func=AF.Exp,
                    accum_out=accR[:, hh:hh + 1]), [emm[8 + h], stopper]))
            ekh = so(lambda s, b2=bi: s.activation(
                out=trn[0:8, 128:256], in_=trn[0:8, 0:128],
                func=AF.Exp, bias=lnZt[0:8, b2:b2 + 1]), [nl1] + chain)
            ekt = mm(psB[:, 1024:1032], trn[0:8, 128:256], idf[0:8, 0:8],
                     True, True, [ekh, dc_id] + freeB2, tr=True)
            eke = vof(lambda v: v.tensor_copy(eKT[:, :], psB[:, 1024:1032]),
                     [ekt])
            dtk = vof(lambda v: v.tensor_tensor(Dtok[:, :], accR[:, :],
                                               eKT[:, :], ALU.add),
                     exd + [eke])
            rdt = vof(lambda v: v.reciprocal(rDtk[:, :], Dtok[:, :]), [dtk])
            nrm = []
            for h in range(8):
                nrm.append(vo(lambda v, hh=h: v.tensor_scalar(
                    Rh[:, hh * 128:(hh + 1) * 128],
                    Rh[:, hh * 128:(hh + 1) * 128], rDtk[:, hh:hh + 1], None,
                    ALU.mult), [rdt, exd[h]]))
            dtt = mm(psB[0:8, 1032:1160], Dtok[:, :], idf[:, :],
                     True, True, [dtk, dc_id, rdt] + freeB2, tr=True)
            kdt = vof(lambda v: v.tensor_tensor(eKT[:, :], eKT[:, :],
                                               rDtk[:, :], ALU.mult),
                     [rdt, dtk])
            rdl = vof(lambda v: v.reciprocal(
                trn[0:8, 648:649], psB[0:8, 1159:1160]), [dtt])
            lnd = so(lambda s: s.activation(
                out=trn[0:8, 650:651], in_=psB[0:8, 1159:1160],
                func=AF.Ln), [dtt])
            kdm = mm(psB[0:8, 1160:1288], eKT[:, :], idf[:, :],
                     True, True, [kdt, dc_id, rdl, lnd] + freeB2, tr=True)
            kde = vo(lambda v: v.tensor_copy(
                kdbf[0:8, :], psB[0:8, 1160:1288]), [kdm])
            kdl = vof(lambda v: v.tensor_copy(
                trn[0:8, 647:648], psB[0:8, 1287:1288]), [kdm])
            rtm = []
            for h in range(8):
                rtm.append(mm(psA[:, 1024:1152],
                              Rh[:, h * 128:(h + 1) * 128], idf[:, :],
                              h == 0, h == 7,
                              [nrm[h], dc_id] + freeA2, tr=True))
            rte = so(lambda s: s.activation(out=RTs[:, :],
                                            in_=psA[:, 1024:1152],
                                            func=AF.Copy), rtm)
            lnu = vo(lambda v, u_=uc, b2=bi: v.tensor_tensor(
                lnZt[0:8, b2:b2 + 1],
                sm_b[0:8, u_.start + 127:u_.start + 128],
                trn[0:8, 650:651], ALU.add), [lnd, ekh] + chain)
            prev_emm[:] = emm[7:8]
            st.update(dvv=dvv, exd=exd, nrm=nrm, rte=rte, kde=kde, kdl=kdl,
                      rdl=rdl, lnd=lnd, eke=eke, dtk=dtk, kdt=kdt, lnu=lnu,
                      pbf=pbf, ekh=ekh)

        def a1_beta(cc, pc):
            nonlocal prev_sx2, prev_xq, prev_dxq, freeA01, freeA2, \
                freeB2, prev_pups, prev_yg, prev_m2
            bi, c = cc // NCH, cc % NCH
            chain = chainB[bi]
            (trn, vvt, accR, Dtok, rDtk, eKT, wclT, pvT, Rh, RTs, sx,
             sxx, mnt, vrt, rstd, mbn, bat,
             xt, kdbf) = a1_tiles(pc)
            pguard = pend1[pc]
            st = A1S[cc]
            dvv, exd, rte, kde, kdl, rdl, lnd, lnu, pbf = (
                st["dvv"], st["exd"], st["rte"], st["kde"], st["kdl"],
                st["rdl"], st["lnd"], st["lnu"], st["pbf"])
            row0 = bi * SCORE + c * C
            gc = bi * SCORE + c * C
            uc = slice(gc, gc + C)
            ymm = []
            yg = []
            for dc in range(8):
                m1 = mm(psB[:, dc * 128:(dc + 1) * 128],
                        Pbf[0:8, bi * D + dc * 128:bi * D + (dc + 1) * 128],
                        kdbf[0:8, :], dc % 4 == 0, False,
                        [pbf, kde, dvv] + prev_pups)
                m2_ = mm(psB[:, dc * 128:(dc + 1) * 128],
                         vvt[:, dc * 128:(dc + 1) * 128], RTs[:, :],
                         False, dc % 4 == 3, [m1, rte, dvv])
                ymm += [m1, m2_]
                yg.append(m2_)
            yev = []
            for dc in range(8):
                stopper = yg[3 if dc < 4 else 7]
                yev.append(so(lambda s, dcc=dc: s.activation(
                    out=yTs[:, dcc * 128:(dcc + 1) * 128],
                    in_=psB[:, dcc * 128:(dcc + 1) * 128], func=AF.Copy),
                    [yg[dc], stopper]))
            if c == 0:
                for dc in range(8):
                    yev.append(vo(lambda v, dcc=dc: v.tensor_scalar(
                        yTs[:, dcc * 128:dcc * 128 + 1],
                        yTs[:, dcc * 128:dcc * 128 + 1], y0t[:, 0:1], None,
                        ALU.mult), [yev[dc], dc_y0]))
            nl2 = vof(lambda v, u_=uc: v.tensor_scalar(
                trn[0:8, 649:650],
                sm_u[0:8, u_.start + 127:u_.start + 128],
                -1.0, None, ALU.mult), chain)
            wx1 = so(lambda s, u_=uc: s.activation(
                out=trn[0:8, 256:384], in_=sm_u[0:8, u_],
                func=AF.Exp, bias=trn[0:8, 649:650]), [nl2])
            wn1 = vof(lambda v: v.tensor_scalar(
                trn[0:8, 256:384], trn[0:8, 256:384],
                trn[0:8, 648:649], None, ALU.mult), [wx1, rdl])
            wt1 = mm(psB[:, 1280:1288], trn[0:8, 256:384], idf[0:8, 0:8],
                     True, True, [wn1, dc_id, lnd, kde, kdl] + freeB2,
                     tr=True)
            we1 = vo(lambda v: v.tensor_copy(wclT[:, :], psB[:, 1280:1288]),
                     [wt1])
            pvm = []
            for dc in range(8):
                pvm.append(mm(psB[:, 1288 + dc * 8:1296 + dc * 8],
                              vvt[:, dc * 128:(dc + 1) * 128], wclT[:, :],
                              True, True, [we1, dvv]))
            pve = vo(lambda v: v.tensor_copy(pvT[:, :], psB[:, 1288:1352]),
                     pvm)
            ptm2 = []
            for dc in range(8):
                ptm2.append(mm(psB[0:8, dc * 128:(dc + 1) * 128],
                               pvT[:, dc * 8:(dc + 1) * 8], idf[:, :],
                               True, True,
                               [pve, dc_id] + yev[0:8], tr=True))
            pups = [vo(lambda v, b2=bi: v.scalar_tensor_tensor(
                Pst[0:8, b2 * D:(b2 + 1) * D], Pst[0:8, b2 * D:(b2 + 1) * D],
                trn[0:8, 647:648], psB[0:8, 0:1024], ALU.mult, ALU.add),
                ptm2 + [pbf, kdl] + chain)]
            wmm = []
            for dc in range(8):
                lT = yTs[:, dc * 128:(dc + 1) * 128]
                for si in range(2):
                    wmm.append(mm(
                        psA[:, si * 512:(si + 1) * 512], lT,
                        WoT[:, dc * D + si * 512:dc * D + (si + 1) * 512],
                        dc == 0, dc == 7, yev + exd))
            dx2 = dmas(lambda e, r0=row0: e.dma_start(
                out=xt[:, :], in_=x_sl[r0:r0 + 128, :]), pguard)
            x2v = vo(lambda v: v.tensor_tensor(x2t[:, :], psA[:, 0:1024],
                                               xt[:, :], ALU.add),
                     wmm + [dx2] + prev_m2
                     + ([prev_sx2] if prev_sx2 else []))
            sx2 = dmag(lambda e, r0=row0: e.dma_start(
                out=s_x2[r0:r0 + 128, :], in_=x2t[:, :]), [x2v])
            prev_sx2 = sx2
            m1 = vof(lambda v: v.tensor_reduce(sx[:, :], x2t[:, :], AX.X,
                                              ALU.add), [x2v])
            m2 = so(lambda s: s.activation(out=q_sq[:, 1024:2048],
                                           in_=x2t[:, :],
                                           func=AF.Square,
                                           accum_out=sxx[:, :]),
                    [x2v] + ([prev_xq] if prev_xq else []))
            m3 = vof(lambda v: v.tensor_scalar(mnt[:, :], sx[:, :], 1.0 / D,
                                              None, ALU.mult), [m1])
            m4 = vof(lambda v: v.tensor_scalar(vrt[:, :], sxx[:, :], 1.0 / D,
                                              None, ALU.mult), [m2])
            m5 = vof(lambda v: v.tensor_tensor(sx[:, :], mnt[:, :], mnt[:, :],
                                              ALU.mult), [m3])
            m6 = vof(lambda v: v.tensor_tensor(vrt[:, :], vrt[:, :], sx[:, :],
                                              ALU.subtract), [m4, m5])
            m7 = so(lambda s: s.activation(out=vrt[:, :], in_=vrt[:, :],
                                           func=AF.Ln, bias=cst[:, 0:1]),
                    [m6] + mzc)
            m8 = so(lambda s: s.activation(out=rstd[:, :], in_=vrt[:, :],
                                           func=AF.Exp, scale=-0.5), [m7])
            m9 = vof(lambda v: v.tensor_tensor(mbn[:, :], mnt[:, :], rstd[:, :],
                                              ALU.mult), [m8])
            m10 = vof(lambda v: v.tensor_scalar(mbn[:, :], mbn[:, :], -1.0,
                                               None, ALU.mult), [m9])
            m11 = vo(lambda v: v.tensor_scalar(xt[:, :], x2t[:, :],
                                               rstd[:, :], mbn[:, :],
                                               ALU.mult, ALU.add),
                     [m10, m8, sx2])
            xq = emit_quant(1024, xt, xnt,
                            ([m11, m2] + a0_tail) if cc == 0
                            else ([m11, m2] + prev_dxq))
            prev_xq = xq
            xqd = twave([(xnt[:, dc * 128:(dc + 1) * 128],
                          (lambda s, ps, d2=dc: s.activation(
                              out=xqT[:, d2 * 128:(d2 + 1) * 128],
                              in_=ps, func=AF.Copy)))
                         for dc in range(8)],
                        [xq] + pguard + prev_dxq)
            dxq = dmag(lambda e, ccx=cc: e.dma_start(
                out=s_xqT[ccx, :, :], in_=xqT[:, :]), xqd)
            prev_dxq = [dxq]
            prev_m2 = [m2]
            chainB[bi] = [lnu] + pups[-1:]
            freeA01 = [x2v]
            freeA2 = [rte]
            freeB2 = [st["eke"], kde, kdl, rdl, lnd, we1, pve]
            prev_pups = list(pups)
            prev_yg = yg[-1:]
            pend1[pc] = ([x2v, sx2, dxq, xq, pve, lnu, kde, kdl, rdl, lnd,
                          st["eke"], we1, rte] + pups + yev[0:8] + xqd
                         + pvm[-1:] + [st["dtk"], st["kdt"], st["nrm"][-1]])

        NA1 = NCC if PHASE >= 5 else 0
        seq1 = [bi * NCH + c for c in range(NCH) for bi in (0, 1)][:NA1]
        for k, cc in enumerate(seq1):
            if k > 0:
                a1_beta(seq1[k - 1], (k - 1) % 2)
            a1_alpha(cc, k % 2)
        if NA1:
            a1_beta(seq1[-1], (len(seq1) - 1) % 2)
        chain = chainB[0][-2:] + chainB[1][-2:]

        if _os.environ.get("MK_DBG"):
            S_.add("sync", lambda e: e.dma_start(
                out=d_smub[:, :], in_=sm_u[:, :]), chain, is_dma=True)
            S_.add("sync", lambda e: e.dma_start(
                out=d_smbb[:, :], in_=sm_b[:, :]), chain, is_dma=True)
        prev_odma = None
        aend = [x for x in (pend1[0] + pend1[1]
                            + [prev_sx2, prev_dvv] + chain
                            + psA_free + psB_free + [prev_xq])
                if x is not None]
        esA.close()

        esB = ExitStack()

        def sbB(n, s, d=F32):
            return esB.enter_context(nc.sbuf_tensor(n, s, d)).ap()

        with esB:
            WguT = sbB("WguT", [128, 8 * 2 * I_FF], BF16)
            WdT = sbB("WdT", [128, 22 * D], BF16)
            hq = sbB("hq", [128, I_FF], BF16)
            hqT = sbB("hqT", [128, I_FF], BF16)
            silg = sbB("silg", [128, 704], F32)
            hhf = sbB("hhf", [128, 704], F32)
            outt = sbB("outt", [128, D], F32)
            xqL = sbB("xqL", [128, D], BF16)

            wld = []
            for dc in range(8 if PHASE >= 6 else 0):
                wld.append(dmas(lambda e, dcc=dc: e.dma_start(
                    out=WguT[:, dcc * 2 * I_FF:(dcc + 1) * 2 * I_FF],
                    in_=s_wgu[dcc, :, :]), [wq_last] + aend))
            for ic in range(22 if PHASE >= 6 else 0):
                wld.append(dmas(lambda e, icc=ic: e.dma_start(
                    out=WdT[:, icc * D:(icc + 1) * D], in_=s_wd[icc, :, :]),
                    [wq_last] + aend))

            prev_out = None
            prev_odma2 = None
            prev_hte = []
            psAf = aend
            psBf = aend
            for cc in range(NCC if PHASE >= 6 else 0):
                bi, c = cc // NCH, cc % NCH
                row0 = bi * SCORE + c * C
                dxq = dmas(lambda e, ccx=cc: e.dma_start(
                    out=xqL[:, :], in_=s_xqT[ccx, :, :]),
                    ([prev_out] if prev_out else aend))
                dx2 = dmas(lambda e, r0=row0: e.dma_start(
                    out=x2t[:, :], in_=s_x2[r0:r0 + 128, :]),
                    ([prev_out] if prev_out else aend))
                hqd = []
                hte = []
                hte_prev = list(prev_hte)
                isl = [(0, 512), (512, 512), (1024, 512), (1536, 512),
                       (2048, 512), (2560, 256)]
                for (io, wi) in isl:
                    gmm = []
                    for dc in range(8):
                        lT = xqL[:, dc * 128:(dc + 1) * 128]
                        wb = dc * 2 * I_FF
                        gmm.append(mm(psA[:, 0:wi], lT,
                                      WguT[:, wb + io:wb + io + wi],
                                      dc == 0, dc == 7,
                                      [dxq, wld[-1]] + psAf))
                        gmm.append(mm(psA[:, 512:512 + wi], lT,
                                      WguT[:, wb + I_FF + io:wb + I_FF + io + wi],
                                      dc == 0, dc == 7,
                                      [dxq, wld[-1]] + psAf))
                    sil = so(lambda s, wi_=wi: s.activation(
                        out=silg[:, 0:wi_], in_=psA[:, 0:wi_],
                        func=AF.Silu), gmm)
                    hmu = vo(lambda v, wi_=wi: v.tensor_tensor(
                        hhf[:, 0:wi_], psA[:, 512:512 + wi_],
                        silg[:, 0:wi_], ALU.mult), [sil] + gmm)
                    hqd.append(emit_quant(wi, hhf, hq, [hmu] + hte_prev,
                                          outoff=io))
                    hte_prev = []
                    psAf = [hmu, sil]
                for w0 in range(0, 22, 8):
                    pieces = []
                    for ic in range(w0, min(w0 + 8, 22)):
                        pieces.append((hq[:, ic * 128:(ic + 1) * 128],
                                       (lambda s, ps, i2=ic: s.activation(
                                           out=hqT[:, i2 * 128:(i2 + 1) * 128],
                                           in_=ps, func=AF.Copy))))
                    hte += twave(pieces, hqd)
                dmm = []
                for ic in range(22):
                    lT = hqT[:, ic * 128:(ic + 1) * 128]
                    for si in range(2):
                        dmm.append(mm(
                            psB[:, si * 512:(si + 1) * 512], lT,
                            WdT[:, ic * D + si * 512:ic * D + (si + 1) * 512],
                            ic == 0, ic == 21, hte + psBf))
                ov = vo(lambda v: v.tensor_tensor(outt[:, :], psB[:, 0:1024],
                                                  x2t[:, :], ALU.add),
                        dmm + [dx2]
                        + ([prev_odma2] if prev_odma2 else []))
                odma = dmas(lambda e, r0=row0: e.dma_start(
                    out=out_sl[r0:r0 + 128, :], in_=outt[:, :]), [ov])
                prev_out = ov
                prev_odma2 = odma
                prev_hte = [hte[-1]]
                psBf = [ov]

            if cutoff is not None:
                S_.items = S_.items[:cutoff]
            if _os.environ.get("MK_NITEMS"):
                print("SCHED items:", len(S_.items))
            S_.resolve()
            with (nc.semaphore("sd") as sem_sd, nc.semaphore("gd") as sem_gd,
                  nc.semaphore("v") as sem_v, nc.semaphore("s") as sem_s,
                  nc.semaphore("t") as sem_t, nc.semaphore("g") as sem_g,
                  nc.Block() as block):
                sems = {"sd": sem_sd, "gd": sem_gd, "vector": sem_v,
                        "scalar": sem_s, "tensor": sem_t, "gpsimd": sem_g}

                @block.sync
                def _(sync):
                    S_.emit("sync", sync, sems)
                    sync.wait_ge(sem_sd, S_.final.get("sd", 0))
                    if S_.final.get("gd"):
                        sync.wait_ge(sem_gd, S_.final["gd"])

                @block.gpsimd
                def _(g):
                    S_.emit("gpsimd", g, sems)

                @block.vector
                def _(v):
                    S_.emit("vector", v, sems)

                @block.scalar
                def _(s):
                    S_.emit("scalar", s, sems)

                @block.tensor
                def _(t):
                    S_.emit("tensor", t, sems)
    return nc


_NC_CACHE = [None]


def kernel(L0_attn_x, L0_attn_W_q, L0_attn_W_k, L0_attn_W_v, L0_attn_W_alpha_w,
           L0_attn_W_alpha_b, L0_attn_W_o, L0_ffn_w_gate, L0_ffn_w_up,
           L0_ffn_w_down):
    x = np.ascontiguousarray(np.asarray(L0_attn_x, np.float32))
    if _NC_CACHE[0] is None:
        _NC_CACHE[0] = build()
    nc = _NC_CACHE[0]
    iden = np.eye(128, dtype=np.float32)
    idenb = np.eye(128, dtype=ml_dtypes.bfloat16)
    maskut = np.where(np.arange(128)[:, None] < np.arange(128)[None, :],
                      np.float32(NEG), np.float32(0.0)).astype(np.float32)
    rhsec = np.zeros((9, 1024), np.float32)
    for h in range(8):
        rhsec[h, h * 128:(h + 1) * 128] = -1.0
    onesc = np.ones((1, 128), np.float32)
    wab2 = np.ascontiguousarray(
        np.asarray(L0_attn_W_alpha_b, np.float32).reshape(1, 512))
    in_maps = []
    for r in range(NCORE):
        adj = np.zeros((128, 2), np.float32)
        for j in range(NCORE):
            if j >= r:
                adj[16 * j:16 * (j + 1), 0] = NEG
            else:
                adj[16 * j:16 * (j + 1), 1] = 1.0
        y0 = np.full((128, 1), 1.0, np.float32)
        if r == 0:
            y0[:] = np.float32(1.0) / np.float32(1.0 + EPS)
        xs = np.ascontiguousarray(
            x[:, r * SCORE:(r + 1) * SCORE, :].reshape(2 * SCORE, D))
        in_maps.append({
            "x_sl": xs,
            "Wq": np.ascontiguousarray(np.asarray(L0_attn_W_q, np.float32)),
            "Wk": np.ascontiguousarray(np.asarray(L0_attn_W_k, np.float32)),
            "Wa": np.ascontiguousarray(
                np.asarray(L0_attn_W_alpha_w, np.float32)),
            "Wab": wab2,
            "Wv": np.ascontiguousarray(np.asarray(L0_attn_W_v, np.float32)),
            "Wo": np.ascontiguousarray(np.asarray(L0_attn_W_o, np.float32)),
            "Wg": np.ascontiguousarray(np.asarray(L0_ffn_w_gate, np.float32)),
            "Wu": np.ascontiguousarray(np.asarray(L0_ffn_w_up, np.float32)),
            "Wd": np.ascontiguousarray(np.asarray(L0_ffn_w_down, np.float32)),
            "idenf": iden, "idenb": idenb, "maskut": maskut,
            "rhsec": rhsec, "onesc": onesc, "adjc": adj, "y0fc": y0,
        })
    res = run_bass_kernel_spmd(nc, in_maps, core_ids=list(range(NCORE)))
    out = np.empty((B, S, D), np.float32)
    for r in range(NCORE):
        o = np.asarray(res.results[r]["out_sl"]).reshape(2, SCORE, D)
        out[:, r * SCORE:(r + 1) * SCORE, :] = o
    return out
